# version 01
# baseline (speedup 1.0000x reference)
import contextlib
import numpy as np
import ml_dtypes
import concourse.bass as bass
import concourse.mybir as mybir

F32 = mybir.dt.float32
BF16 = mybir.dt.bfloat16
I32 = mybir.dt.int32
ALU = mybir.AluOpType
AF = mybir.ActivationFunctionType

H = 2048
MLP = 16384
TCORE = 2048
NKT = 16
NNT = 128
NHT = 16
GELU_C1 = 1.5957691216057308
GELU_C2 = 0.07135481283247557
SQRT_C2 = GELU_C2 ** 0.5
LN_EPS = 1e-5
NCORES = 8

ENGS = ["sync", "tensor", "vector", "scalar", "gpsimd"]


def down_scratch_row(nt):
    if nt < 64:
        return (nt // 8) * 2048 + (nt % 8) * 128
    return ((nt - 64) // 8) * 2048 + (8 + (nt - 64) % 8) * 128


class Op:
    __slots__ = ["emit", "waits", "inc_sem", "inc_k"]

    def __init__(self, emit, waits, inc_sem, inc_k):
        self.emit = emit
        self.waits = waits
        self.inc_sem = inc_sem
        self.inc_k = inc_k


class Prog:
    def __init__(self, nc, es):
        self.nc = nc
        self.es = es
        self.ops = {e: [] for e in ENGS}
        self.sems = {}
        for name in ["cPE", "cVE", "cAC", "cGP", "cSY", "dS", "dA", "dG", "cCC"]:
            self.sems[name] = [es.enter_context(nc.semaphore(name)), 0]
        self.eng_comp_sem = {"tensor": "cPE", "vector": "cVE", "scalar": "cAC",
                            "gpsimd": "cGP", "sync": "cSY"}
        self.eng_dma_sem = {"sync": "dS", "scalar": "dA", "gpsimd": "dG"}
        self.sb = {}
        self.dr = {}
        self.ps = []
        self.pending_barrier = {}

    def sbuf(self, name, shape, dt, stack=None):
        self._uniq = getattr(self, "_uniq", 0) + 1
        t = (stack or self.es).enter_context(
            self.nc.sbuf_tensor(f"{name}_{self._uniq}", list(shape), dt))
        self.sb[name] = t
        return t

    def dram(self, name, shape, dt, kind="Internal", addr_space="Local"):
        t = self.nc.dram_tensor(name, list(shape), dt, kind=kind,
                                addr_space=addr_space).ap()
        self.dr[name] = t
        return t

    def getsem(self, name):
        if name not in self.sems:
            self.sems[name] = [self.es.enter_context(self.nc.semaphore(name)), 0]
        return self.sems[name]

    def op(self, eng, emit, waits=(), dma=False, collective=False, sem=None):
        if collective:
            semname, k = "cCC", 1
        elif dma:
            semname, k = sem or self.eng_dma_sem[eng], 16
            self.getsem(semname)
        else:
            semname, k = self.eng_comp_sem[eng], 1
        s = self.sems[semname]
        s[1] += k
        w = [x for x in waits if x is not None]
        if self.pending_barrier.get(eng):
            w = self.pending_barrier.pop(eng) + w
        self.ops[eng].append(Op(emit, tuple(w), semname, k))
        return (semname, s[1])

    def fence(self, eng, semnames):
        w = [(n, self.sems[n][1]) for n in semnames if n in self.sems and self.sems[n][1] > 0]
        return self.op(eng, lambda e: e.nop(), waits=w)

    def barrier(self):
        snap = [(n, s[1]) for n, s in self.sems.items() if s[1] > 0]
        for e in ENGS:
            self.pending_barrier[e] = list(snap)

    def flush(self):
        nc = self.nc
        if not any(self.ops[e] for e in ENGS):
            return
        if not hasattr(self, "_waited"):
            self._waited = {e: {} for e in ENGS}
        with nc.Block() as blk:
            for eng_name in ENGS:
                oplist = self.ops[eng_name]
                if not oplist:
                    continue

                def body(eng, oplist=oplist, eng_name=eng_name):
                    waited = self._waited[eng_name]
                    for o in oplist:
                        for (sn, cnt) in o.waits:
                            if waited.get(sn, 0) >= cnt:
                                continue
                            eng.wait_ge(self.sems[sn][0], cnt)
                            waited[sn] = cnt
                        instr = o.emit(eng)
                        instr.then_inc(self.sems[o.inc_sem][0], o.inc_k)

                getattr(blk, eng_name)(body)
        self.ops = {e: [] for e in ENGS}

    def emit_all(self):
        self.flush()

    def n_ops(self):
        return {e: len(self.ops[e]) for e in ENGS}



def alloc_dram(p, cfg):
    p.dram("xin", [H, TCORE], F32, kind="ExternalInput")
    for n in ["gpk_my", "upk_my", "dpk_my"]:
        p.dram(n, [1024, 2048], I32, kind="ExternalInput")
    for n in ["gsc_my", "usc_my", "dsc_my"]:
        p.dram(n, [64, 2048], F32, kind="ExternalInput")
    p.dram("wx1", [H, 320], F32, kind="ExternalInput")
    p.dram("wx2", [64, 5 * H], F32, kind="ExternalInput")
    p.dram("xb8in", [128, 80], F32, kind="ExternalInput")
    p.dram("mux8in", [128, 16], F32, kind="ExternalInput")
    p.dram("wr", [H, 1024], F32, kind="ExternalInput")
    p.dram("wk", [H, 1024], F32, kind="ExternalInput")
    p.dram("wv", [H, 1024], F32, kind="ExternalInput")
    p.dram("wwa", [H, 64], F32, kind="ExternalInput")
    p.dram("wga", [H, 64], F32, kind="ExternalInput")
    p.dram("wwb", [64, 1024], F32, kind="ExternalInput")
    p.dram("wgb", [64, 1024], F32, kind="ExternalInput")
    for n in ["bw8in", "bg8in", "bonus8in", "lnw8in", "lnb8in"]:
        p.dram(n, [128, 8], F32, kind="ExternalInput")
    for n in ["bonus8Bin", "lnw8Bin", "lnb8Bin"]:
        p.dram(n, [64, 8], F32, kind="ExternalInput")
    p.dram("wo", [1024, H], F32, kind="ExternalInput")
    p.dram("ident", [128, 128], BF16, kind="ExternalInput")
    p.dram("mask2", [64, 128], BF16, kind="ExternalInput")
    p.dram("eye2", [64, 128], BF16, kind="ExternalInput")
    p.dram("cmaskin", [128, 32], F32, kind="ExternalInput")
    p.dram("bonesin", [128, 2], F32, kind="ExternalInput")
    p.dram("bonesTin", [2, 128], F32, kind="ExternalInput")
    p.dram("out", [H, TCORE], F32, kind="ExternalOutput")
    for d in cfg.get("debug", []):
        p.dram("dbg_" + d, [H, TCORE], F32, kind="ExternalOutput")

    p.dram("x1s", [H, TCORE], F32)
    p.dram("x2s", [H, TCORE], F32)
    p.dram("gsp", [64, 2048], F32)
    p.dram("usp", [64, 2048], F32)
    p.dram("dsp", [64, 2048], F32)
    p.dram("shard_g", [2048, 2048], BF16)
    p.dram("shard_u", [2048, 2048], BF16)
    p.dram("shard_d", [2048, 2048], BF16)
    p.dram("wscr_g", [16384, 2048], BF16, addr_space="Shared")
    p.dram("wscr_u", [16384, 2048], BF16, addr_space="Shared")
    p.dram("wscr_d", [16384, 2048], BF16, addr_space="Shared")
    p.dram("wrb", [128, 16 * 1024], BF16)
    p.dram("wkb", [128, 16 * 1024], BF16)
    p.dram("wvb", [128, 16 * 1024], BF16)
    p.dram("wob", [128, 8 * 2048], BF16)
    p.dram("wwab", [128, 16 * 64], BF16)
    p.dram("wgab", [128, 16 * 64], BF16)
    p.dram("wx1b", [128, 16 * 320], BF16)
    p.dram("wx2bs", [64, 5 * 2048], BF16)
    p.dram("wwbs", [64, 1024], BF16)
    p.dram("wgbs", [64, 1024], BF16)
    p.dram("gstash", [1024, TCORE], BF16)
    p.dram("ostash", [1024, TCORE], BF16)
    p.dram("rrelstash", [1024, TCORE], BF16)
    p.dram("dbgf", [128, 1024], F32)
    p.dram("dbgb", [128, 2048], BF16)
    p.dram("ag1_in", [128, 16], F32)
    p.dram("ag1_out", [1024, 16], F32, addr_space="Shared")
    p.dram("ag2_in", [1040, 64], F32)
    p.dram("ag2_out", [8320, 64], F32, addr_space="Shared")




def alloc_persist(p):
    p.sbuf("ones1", [128, 1], F32)
    p.sbuf("onesr", [1, 128], F32)
    p.sbuf("identb", [128, 128], BF16)
    p.sbuf("mask2b", [64, 128], BF16)
    p.sbuf("eye2b", [64, 128], BF16)
    p.sbuf("cmask", [128, 32], F32)
    p.sbuf("bonus8", [128, 8], F32)
    p.sbuf("lnw8", [128, 8], F32)
    p.sbuf("lnb8", [128, 8], F32)
    p.sbuf("bw8", [128, 8], F32)
    p.sbuf("bg8", [128, 8], F32)
    p.sbuf("mux8", [128, 16], F32)
    p.sbuf("bonus8B", [64, 8], F32)
    p.sbuf("lnw8B", [64, 8], F32)
    p.sbuf("lnb8B", [64, 8], F32)
    p.sbuf("xb8", [128, 80], F32)
    p.sbuf("mrow", [1, 1024], F32)
    p.sbuf("vrow", [1, 1024], F32)
    p.sbuf("rrow", [1, 1024], F32)
    p.sbuf("epscol", [1, 1], F32)
    p.sbuf("eps2", [2, 1], F32)
    p.sbuf("agcol", [128, 16], F32)



def build_p0(p, cfg):
    SCF = float(2.0 ** 126)
    p.op("vector", lambda e: e.memset(p.sb["ones1"][:, :], 1.0))
    p.op("vector", lambda e: e.memset(p.sb["epscol"][0:1, :], LN_EPS))
    p.op("vector", lambda e: e.memset(p.sb["eps2"][0:2, :], LN_EPS))
    p.op("vector", lambda e: e.memset(p.sb["onesr"][:, :], 1.0))
    for (dst, src) in [("identb", "ident"), ("mask2b", "mask2"), ("eye2b", "eye2"),
                       ("cmask", "cmaskin"), ("bonus8", "bonus8in"),
                       ("lnw8", "lnw8in"), ("lnb8", "lnb8in"), ("bw8", "bw8in"),
                       ("bg8", "bg8in"), ("mux8", "mux8in"), ("xb8", "xb8in"),
                       ("bonus8B", "bonus8Bin"), ("lnw8B", "lnw8Bin"),
                       ("lnb8B", "lnb8Bin")]:
        p.op("sync", lambda e, dst=dst, src=src: e.dma_start(
            out=p.sb[dst][:, :], in_=p.dr[src][:, :]), dma=True)
    p.consts_ready = p.fence("sync", ["dS"])

    with contextlib.ExitStack() as st0:
        cvt32 = [p.sbuf(f"cvt32{s}", [128, 2048], F32, stack=st0) for s in range(2)]
        cvt16 = [p.sbuf(f"cvt16{s}", [128, 2048], BF16, stack=st0) for s in range(2)]
        cv_marks = {}
        ci = 0
        for (srcn, dstn, width, nchunk) in [
                ("wr", "wrb", 1024, 16), ("wk", "wkb", 1024, 16),
                ("wv", "wvb", 1024, 16), ("wwa", "wwab", 64, 16),
                ("wga", "wgab", 64, 16), ("wx1", "wx1b", 320, 16),
                ("wo", "wob", 2048, 8)]:
            for j in range(nchunk):
                slot = ci % 2
                mld = p.op("sync", lambda e, srcn=srcn, j=j, width=width, slot=slot: e.dma_start(
                    out=cvt32[slot][:, 0:width], in_=p.dr[srcn][128 * j:128 * j + 128, :]),
                    dma=True, sem=f"ldc{slot}", waits=[cv_marks.get(("cv", ci - 2))])
                mcv = p.op("vector", lambda e, width=width, slot=slot: e.tensor_copy(
                    cvt16[slot][:, 0:width], cvt32[slot][:, 0:width]),
                    waits=[mld, cv_marks.get(("st", ci - 2))])
                cv_marks[("cv", ci)] = mcv
                mst = p.op("scalar", lambda e, dstn=dstn, j=j, width=width, slot=slot: e.dma_start(
                    out=p.dr[dstn][:, width * j:width * j + width],
                    in_=cvt16[slot][:, 0:width]), dma=True, sem=f"stc{slot}",
                    waits=[mcv])
                cv_marks[("st", ci)] = mst
                ci += 1
        for (srcn, dstn, width) in [("wx2", "wx2bs", 5 * 2048),
                                    ("wwb", "wwbs", 1024), ("wgb", "wgbs", 1024)]:
            for c0 in range(0, width, 2048):
                cw = min(2048, width - c0)
                slot = ci % 2
                mld = p.op("sync", lambda e, srcn=srcn, c0=c0, cw=cw, slot=slot: e.dma_start(
                    out=cvt32[slot][0:64, 0:cw], in_=p.dr[srcn][:, c0:c0 + cw]),
                    dma=True, sem=f"ldc{slot}", waits=[cv_marks.get(("cv", ci - 2))])
                mcv = p.op("vector", lambda e, cw=cw, slot=slot: e.tensor_copy(
                    cvt16[slot][0:64, 0:cw], cvt32[slot][0:64, 0:cw]),
                    waits=[mld, cv_marks.get(("st", ci - 2))])
                cv_marks[("cv", ci)] = mcv
                mst = p.op("scalar", lambda e, dstn=dstn, c0=c0, cw=cw, slot=slot: e.dma_start(
                    out=p.dr[dstn][:, c0:c0 + cw], in_=cvt16[slot][0:64, 0:cw]),
                    dma=True, sem=f"stc{slot}", waits=[mcv])
                cv_marks[("st", ci)] = mst
                ci += 1
        p.conv_fence = p.fence("scalar", ["stc0", "stc1"])
        scb = p.sbuf("scbuf", [64, 2048], F32, stack=st0)
        scpre = {}
        for (mat, mult) in [("g", SCF), ("u", SCF), ("d", SCF * 0.5)]:
            mld = p.op("sync", lambda e, mat=mat: e.dma_start(
                out=scb[:, :], in_=p.dr[f"{mat}sc_my"][:, :]), dma=True,
                sem="ldsc", waits=[scpre.get("last")])
            mmul = p.op("vector", lambda e, mult=mult: e.tensor_scalar_mul(
                scb[:, :], scb[:, :], mult), waits=[mld])
            mst = p.op("scalar", lambda e, mat=mat: e.dma_start(
                out=p.dr[f"{mat}sp"][:, :], in_=scb[:, :]), dma=True,
                sem="stsc", waits=[mmul])
            scpre[mat] = p.fence("scalar", ["stsc"])
            scpre["last"] = scpre[mat]

        pkbuf = [p.sbuf(f"pkbuf{s}", [128, 2048], I32, stack=st0) for s in range(2)]
        scexp = [p.sbuf(f"scexp{s}", [128, 2048], F32, stack=st0) for s in range(2)]
        dqt1 = p.sbuf("dqt1", [128, 2048], I32, stack=st0)
        dqt2 = p.sbuf("dqt2", [128, 2048], I32, stack=st0)
        dqlo = [p.sbuf(f"dqlo{s}", [128, 2048], BF16, stack=st0) for s in range(2)]
        dqhi = [p.sbuf(f"dqhi{s}", [128, 2048], BF16, stack=st0) for s in range(2)]
        p.ag_done = {}
        st_marks = {}
        vec_marks = {}
        for mi, mat in enumerate(["g", "u", "d"]):
            pk, shard, spre = p.dr[f"{mat}pk_my"], p.dr[f"shard_{mat}"], p.dr[f"{mat}sp"]
            for phg in range(8):
                ph = phg
                gi = mi * 8 + phg
                slot = gi % 2
                w_ld = [scpre[mat], vec_marks.get(gi - 2)]
                mpk = p.op("sync", lambda e, pk=pk, ph=ph, slot=slot: e.dma_start(
                    out=pkbuf[slot][:, :], in_=pk[128 * ph:128 * ph + 128, :]),
                    dma=True, sem=f"ldpk{slot}", waits=w_ld)
                msc = None
                for j in range(8):
                    msc = p.op("sync", lambda e, spre=spre, ph=ph, j=j, slot=slot: e.dma_start(
                        out=scexp[slot][16 * j:16 * j + 16, :],
                        in_=spre[8 * ph + j:8 * ph + j + 1, :].broadcast_to([16, 2048])),
                        dma=True, sem=f"ldse{slot}", waits=w_ld if j == 0 else ())
                wv = [mpk, msc, st_marks.get(gi - 2)]
                p.op("vector", lambda e, slot=slot: e.tensor_scalar(
                    dqt1[:, :], pkbuf[slot][:, :], 8, 28,
                    ALU.bitwise_and, ALU.logical_shift_left), waits=wv)
                p.op("vector", lambda e, slot=slot: e.tensor_scalar(
                    dqt2[:, :], pkbuf[slot][:, :], 7, 22,
                    ALU.bitwise_and, ALU.logical_shift_left))
                p.op("vector", lambda e: e.tensor_tensor(
                    dqt1[:, :], dqt1[:, :], dqt2[:, :], ALU.bitwise_or))
                mlo = p.op("vector", lambda e, slot=slot: e.tensor_tensor(
                    dqlo[slot][:, :], dqt1[:, :].bitcast(F32), scexp[slot][:, :],
                    ALU.mult))
                p.op("vector", lambda e, slot=slot: e.tensor_scalar(
                    dqt1[:, :], pkbuf[slot][:, :], 128, 24,
                    ALU.bitwise_and, ALU.logical_shift_left))
                p.op("vector", lambda e, slot=slot: e.tensor_scalar(
                    dqt2[:, :], pkbuf[slot][:, :], 112, 18,
                    ALU.bitwise_and, ALU.logical_shift_left))
                p.op("vector", lambda e: e.tensor_tensor(
                    dqt1[:, :], dqt1[:, :], dqt2[:, :], ALU.bitwise_or))
                mhi = p.op("vector", lambda e, slot=slot: e.tensor_tensor(
                    dqhi[slot][:, :], dqt1[:, :].bitcast(F32), scexp[slot][:, :],
                    ALU.mult))
                vec_marks[gi] = mhi
                if mat in ("g", "u"):
                    last = None
                    for ntl in range(16):
                        p.op("scalar", lambda e, shard=shard, ntl=ntl, ph=ph, slot=slot: e.dma_start(
                            out=shard[128 * ntl:128 * ntl + 128, 128 * ph:128 * ph + 128],
                            in_=dqlo[slot][:, 128 * ntl:128 * ntl + 128]),
                            dma=True, sem=f"stq{slot}", waits=[mlo, mhi])
                        last = p.op("scalar", lambda e, shard=shard, ntl=ntl, ph=ph, slot=slot: e.dma_start(
                            out=shard[128 * ntl:128 * ntl + 128,
                                      128 * (8 + ph):128 * (8 + ph) + 128],
                            in_=dqhi[slot][:, 128 * ntl:128 * ntl + 128]),
                            dma=True, sem=f"stq{slot}")
                    st_marks[gi] = last
                else:
                    p.op("scalar", lambda e, shard=shard, ph=ph, slot=slot: e.dma_start(
                        out=shard[128 * ph:128 * ph + 128, :], in_=dqlo[slot][:, :]),
                        dma=True, sem=f"stq{slot}", waits=[mlo, mhi])
                    st_marks[gi] = p.op("scalar",
                        lambda e, shard=shard, ph=ph, slot=slot: e.dma_start(
                            out=shard[128 * (8 + ph):128 * (8 + ph) + 128, :],
                            in_=dqhi[slot][:, :]), dma=True, sem=f"stq{slot}")
            stfence = p.fence("scalar", ["stq0", "stq1"])
            p.ag_done[mat] = p.op("gpsimd",
                lambda e, mat=mat, shard=shard: e.collective_compute(
                    "AllGather", ALU.bypass, replica_groups=[list(range(NCORES))],
                    ins=[shard[:, :].opt()], outs=[p.dr[f"wscr_{mat}"][:, :].opt()]),
                collective=True, waits=[stfence])



def build_ln(p, src, t0, xn_dst, psum_free=None, xtile_free=None):
    xt = [p.sb["xtile0"], p.sb["xtile1"]]
    xq = [p.sb["xsq0"], p.sb["xsq1"]]
    sq_marks, ld_marks = {}, {}
    summ = []
    for kt in range(NKT):
        slot = kt % 2
        w = [sq_marks.get(kt - 2), xtile_free if kt < 2 else None,
             summ[-1] if kt >= 2 else None]
        ld_marks[kt] = p.op("scalar", lambda e, kt=kt, slot=slot: e.dma_start(
            out=xt[slot][:, :], in_=src[128 * kt:128 * kt + 128, t0:t0 + 1024]),
            dma=True, sem=f"ldx{slot}", waits=w)
        sq_marks[kt] = p.op("scalar", lambda e, slot=slot: e.activation(
            xq[slot][:, :], xt[slot][:, :], AF.Square), waits=[ld_marks[kt]])
        for hh in range(2):
            p.op("tensor", lambda e, slot=slot, hh=hh, kt=kt: e.matmul(
                p.ps[4 + hh][0:1, :], p.sb["ones1"][:, 0:1],
                xt[slot][:, 512 * hh:512 * hh + 512],
                start=(kt == 0), stop=(kt == NKT - 1)),
                waits=[ld_marks[kt]] + ([psum_free] if kt == 0 and hh == 0 else []))
            m = p.op("tensor", lambda e, slot=slot, hh=hh, kt=kt: e.matmul(
                p.ps[6 + hh][0:1, :], p.sb["ones1"][:, 0:1],
                xq[slot][:, 512 * hh:512 * hh + 512],
                start=(kt == 0), stop=(kt == NKT - 1)), waits=[sq_marks[kt]])
            summ.append(m)
    mrow, vrow, rrow = p.sb["mrow"], p.sb["vrow"], p.sb["rrow"]
    tmp = rrow
    for hh in range(2):
        p.op("vector", lambda e, hh=hh: e.tensor_scalar_mul(
            mrow[0:1, 512 * hh:512 * hh + 512], p.ps[4 + hh][0:1, :], 1.0 / H),
            waits=[summ[-1]])
        p.op("vector", lambda e, hh=hh: e.tensor_scalar_mul(
            vrow[0:1, 512 * hh:512 * hh + 512], p.ps[6 + hh][0:1, :], 1.0 / H))
    p.op("vector", lambda e: e.tensor_tensor(
        tmp[0:1, :], mrow[0:1, :], mrow[0:1, :], ALU.mult))
    mvar = p.op("vector", lambda e: e.tensor_tensor(
        vrow[0:1, :], vrow[0:1, :], tmp[0:1, :], ALU.subtract))
    mln = p.op("scalar", lambda e: e.activation(
        tmp[0:1, :], vrow[0:1, :], AF.Ln, bias=p.sb["epscol"][0:1, 0:1]),
        waits=[mvar])
    mrs = p.op("scalar", lambda e: e.activation(
        vrow[0:1, :], tmp[0:1, :], AF.Exp, scale=-0.5), waits=[mln])
    bcl = None
    for hh in range(2):
        p.op("tensor", lambda e, hh=hh: e.matmul(
            p.ps[4 + hh][:, :], p.sb["onesr"][0:1, :],
            mrow[0:1, 512 * hh:512 * hh + 512], start=True, stop=True),
            waits=[mrs])
        bcl = p.op("tensor", lambda e, hh=hh: e.matmul(
            p.ps[6 + hh][:, :], p.sb["onesr"][0:1, :],
            vrow[0:1, 512 * hh:512 * hh + 512], start=True, stop=True))
    xc = [p.sb["xsq0"], p.sb["xsq1"]]
    xn_marks = {}
    for kt in range(NKT):
        slot = kt % 2
        mld = p.op("scalar", lambda e, kt=kt, slot=slot: e.dma_start(
            out=xt[slot][:, :], in_=src[128 * kt:128 * kt + 128, t0:t0 + 1024]),
            dma=True, sem=f"ldx{slot}", waits=[xn_marks.get(kt - 2),
                             summ[-1] if kt < 2 else None])
        m = None
        for hh in range(2):
            p.op("vector", lambda e, slot=slot, hh=hh: e.tensor_tensor(
                xc[slot][:, 512 * hh:512 * hh + 512],
                xt[slot][:, 512 * hh:512 * hh + 512], p.ps[4 + hh][:, :],
                ALU.subtract), waits=[mld, bcl])
            m = p.op("vector", lambda e, slot=slot, hh=hh, kt=kt: e.tensor_tensor(
                xn_dst[:, 1024 * kt + 512 * hh:1024 * kt + 512 * hh + 512],
                xc[slot][:, 512 * hh:512 * hh + 512], p.ps[6 + hh][:, :], ALU.mult))
        xn_marks[kt] = m
    xn_marks["last"] = m
    return xn_marks



def build_ffn(p, cfg, src, dst, tag):
    NG = cfg.get("ffn_group", 16)
    ngroups = cfg.get("ngroups_cap", NNT // NG)
    store_marks = []
    psum_free = getattr(p, "ffn_psum_free", None)
    xtile_free = getattr(p, "ffn_xtile_free", None)
    for strip in range(2):
        t0 = 1024 * strip
        xn_marks = build_ln(p, src, t0, p.sb["xn"], psum_free, xtile_free)
        marks = {}
        wg_ld, wu_ld, wd_ld = {}, {}, {}

        def sched_gu_load(nt):
            slot = nt % 2
            for mat, store in (("g", wg_ld), ("u", wu_ld)):
                w = [marks.get(("gufree", nt - 2))]
                if tag == "ffn1" and strip == 0 and nt < 2:
                    w.append(p.ag_done[mat])
                if nt < 2 and getattr(p, "ffn_wg_free", None):
                    w.append(p.ffn_wg_free)
                store[nt] = p.op("sync", lambda e, mat=mat, nt=nt, slot=slot: e.dma_start(
                    out=p.sb[f"wbuf{mat}{slot}"][:, :],
                    in_=p.dr[f"wscr_{mat}"][128 * nt:128 * nt + 128, :]),
                    dma=True, sem=f"ld{mat}{slot}", waits=w)

        def sched_d_load(idx):
            g2, ht = idx // NHT, idx % NHT
            slot = idx % 2
            w = [marks.get(("dmm", idx - 2, 1))]
            if tag == "ffn1" and strip == 0 and idx < 2:
                w.append(p.ag_done["d"])
            if idx < 2 and getattr(p, "ffn_wd_free", None):
                w.append(p.ffn_wd_free)
            r0 = down_scratch_row(g2 * NG)
            r8 = down_scratch_row(g2 * NG + 8)
            if cfg.get("dummy_down_loads"):
                p.op("sync", lambda e, r0=r0, slot=slot: e.dma_start(
                    out=p.sb[f"wbufd{slot}"][:, 0:1024],
                    in_=p.dr["wscr_d"][r0:r0 + 128, 0:1024]),
                    dma=True, sem=f"ldd{slot}", waits=w)
                wd_ld[idx] = p.op("sync", lambda e, r8=r8, slot=slot: e.dma_start(
                    out=p.sb[f"wbufd{slot}"][:, 1024:2048],
                    in_=p.dr["wscr_d"][r8:r8 + 128, 0:1024]),
                    dma=True, sem=f"ldd{slot}")
            else:
                p.op("sync", lambda e, r0=r0, slot=slot, ht=ht: e.dma_start(
                    out=p.sb[f"wbufd{slot}"][:, 0:1024].rearrange("q (a w) -> q a w", a=8),
                    in_=p.dr["wscr_d"][r0:r0 + 1024, 128 * ht:128 * ht + 128]
                        .rearrange("(a q) w -> q a w", q=128)),
                    dma=True, sem=f"ldd{slot}", waits=w)
                wd_ld[idx] = p.op("sync", lambda e, r8=r8, slot=slot, ht=ht: e.dma_start(
                    out=p.sb[f"wbufd{slot}"][:, 1024:2048].rearrange("q (a w) -> q a w", a=8),
                    in_=p.dr["wscr_d"][r8:r8 + 1024, 128 * ht:128 * ht + 128]
                        .rearrange("(a q) w -> q a w", q=128)), dma=True, sem=f"ldd{slot}")

        sched_gu_load(0)
        sched_gu_load(1)
        if not cfg.get("skip_down"):
            sched_d_load(0)
            sched_d_load(1)
        for g in range(ngroups):
            for j in range(NG):
                nt = g * NG + j
                slotw = nt % 2
                for half in range(2):
                    ih = 2 * nt + half
                    par = ih % 2
                    w = [wg_ld[nt], xn_marks["last"], marks.get(("gelufree", ih - 2))]
                    mm = None
                    for kt in range(NKT):
                        mm = p.op("tensor", lambda e, slotw=slotw, kt=kt, half=half, par=par: e.matmul(
                            p.ps[0 + par][:, :],
                            p.sb[f"wbufg{slotw}"][:, 128 * kt:128 * kt + 128],
                            p.sb["xn"][:, 1024 * kt + 512 * half:1024 * kt + 512 * half + 512],
                            start=(kt == 0), stop=(kt == NKT - 1)),
                            waits=w if kt == 0 else ())
                    marks[("g", ih)] = mm
                    w2 = [wu_ld[nt], marks.get(("interdone", ih - 2))]
                    for kt in range(NKT):
                        mm = p.op("tensor", lambda e, slotw=slotw, kt=kt, half=half, par=par: e.matmul(
                            p.ps[2 + par][:, :],
                            p.sb[f"wbufu{slotw}"][:, 128 * kt:128 * kt + 128],
                            p.sb["xn"][:, 1024 * kt + 512 * half:1024 * kt + 512 * half + 512],
                            start=(kt == 0), stop=(kt == NKT - 1)),
                            waits=w2 if kt == 0 else ())
                    marks[("u", ih)] = mm
                    if half == 1:
                        marks[("gufree", nt)] = mm
                        if nt + 2 < NNT:
                            sched_gu_load(nt + 2)
                    msq = p.op("scalar", lambda e, par=par: e.activation(
                        p.sb[f"gsq{par}"][:, :], p.ps[0 + par][:, :], AF.Square,
                        scale=SQRT_C2), waits=[marks[("g", ih)]])
                    mm2 = p.op("vector", lambda e, par=par: e.scalar_tensor_tensor(
                        p.sb[f"gsq{par}"][:, :], p.sb[f"gsq{par}"][:, :], GELU_C1,
                        p.ps[0 + par][:, :], ALU.add, ALU.mult), waits=[msq])
                    msig = p.op("scalar", lambda e, par=par: e.activation(
                        p.sb[f"gsig{par}"][:, :], p.sb[f"gsq{par}"][:, :],
                        AF.Sigmoid), waits=[mm2])
                    mgs = p.op("vector", lambda e, par=par: e.tensor_tensor(
                        p.sb[f"gsig{par}"][:, :], p.ps[0 + par][:, :],
                        p.sb[f"gsig{par}"][:, :], ALU.mult), waits=[msig])
                    marks[("gelufree", ih)] = mgs
                    mint = p.op("vector", lambda e, par=par, j=j, half=half: e.tensor_tensor(
                        p.sb["inter"][:, 1024 * j + 512 * half:1024 * j + 512 * half + 512],
                        p.sb[f"gsig{par}"][:, :], p.ps[2 + par][:, :], ALU.mult),
                        waits=[marks[("u", ih)]])
                    marks[("interdone", ih)] = mint
            if cfg.get("skip_down"):
                for ht in range(NHT):
                    marks[("daccfin", ht)] = marks[("interdone", 2 * (g * NG + NG - 1) + 1)]
                continue
            for ht in range(NHT):
                idx = g * NHT + ht
                slotd = idx % 2
                for half in range(2):
                    par = half
                    w = [wd_ld[idx], marks[("interdone", 2 * (g * NG + NG - 1) + 1)],
                         marks.get(("dfree", 2 * idx + half - 2))]
                    mm = None
                    for j in range(NG):
                        mm = p.op("tensor", lambda e, slotd=slotd, j=j, half=half, par=par: e.matmul(
                            p.ps[4 + par][:, :],
                            p.sb[f"wbufd{slotd}"][:, 128 * j:128 * j + 128],
                            p.sb["inter"][:, 1024 * j + 512 * half:1024 * j + 512 * half + 512],
                            start=(j == 0), stop=(j == NG - 1)),
                            waits=w if j == 0 else ())
                    marks[("dmm", idx, half)] = mm
                    if half == 1 and idx + 2 < ngroups * NHT and not cfg.get("skip_down"):
                        sched_d_load(idx + 2)
                    dst_acc = p.sb["dacc"][:, 1024 * ht + 512 * half:1024 * ht + 512 * half + 512]
                    if g == 0:
                        mev = p.op("vector", lambda e, dst_acc=dst_acc, par=par: e.tensor_copy(
                            dst_acc, p.ps[4 + par][:, :]), waits=[mm])
                    else:
                        mev = p.op("vector", lambda e, dst_acc=dst_acc, par=par: e.tensor_tensor(
                            dst_acc, dst_acc, p.ps[4 + par][:, :], ALU.add),
                            waits=[mm])
                    marks[("dfree", 2 * idx + half)] = mev
                    if g == ngroups - 1:
                        marks[("daccfin", ht)] = mev
        finm = {}
        stm = {}
        for ht in range(NHT):
            slot = ht % 2
            mld = p.op("scalar", lambda e, ht=ht, slot=slot, t0=t0: e.dma_start(
                out=p.sb[f"xtile{slot}"][:, :],
                in_=src[128 * ht:128 * ht + 128, t0:t0 + 1024]), dma=True,
                sem=f"ldx{slot}",
                waits=[finm.get(ht - 2), xn_marks["last"] if ht < 2 else None])
            mfin = p.op("vector", lambda e, ht=ht, slot=slot: e.tensor_tensor(
                p.sb[f"xsq{slot}"][:, :], p.sb[f"xtile{slot}"][:, :],
                p.sb["dacc"][:, 1024 * ht:1024 * ht + 1024], ALU.add),
                waits=[mld, marks[("daccfin", ht)], stm.get(ht - 2)])
            finm[ht] = mfin
            stm[ht] = p.op("gpsimd", lambda e, ht=ht, slot=slot, t0=t0: e.dma_start(
                out=dst[128 * ht:128 * ht + 128, t0:t0 + 1024],
                in_=p.sb[f"xsq{slot}"][:, :]), dma=True, sem=f"stx{slot}",
                waits=[mfin])
            store_marks.append(stm[ht])
            if tag == "ffn1" and strip == 1:
                p.op("vector", lambda e, ht=ht, slot=slot: e.tensor_copy(
                    p.sb["agcol"][:, ht:ht + 1],
                    p.sb[f"xsq{slot}"][:, 1023:1024]), waits=[stm[ht]])
        psum_free = marks.get(("dfree", 2 * (ngroups * NHT - 1) + 1),
                              marks[("daccfin", NHT - 1)])
        xtile_free = finm[NHT - 1]
        p.ffn_wg_free = marks[("u", 2 * (ngroups * NG - 1) + 1)]
        p.ffn_wu_free = p.ffn_wg_free
        p.ffn_wd_free = marks.get(("dmm", ngroups * NHT - 1, 1))
    p.ffn_psum_free = psum_free
    p.ffn_xtile_free = xtile_free
    return store_marks


def alloc_ffn_bufs(p, stack):
    p._uniq = getattr(p, "_uniq", 0) + 1
    p.ps = [stack.enter_context(p.nc.psum_tensor(f"pb{i}_{p._uniq}", [128, 512], F32))
            for i in range(8)]
    p.sbuf("xn", [128, NKT * 1024], BF16, stack=stack)
    p.sbuf("inter", [128, 16 * 1024], BF16, stack=stack)
    p.sbuf("dacc", [128, NHT * 1024], F32, stack=stack)
    for s in range(2):
        p.sbuf(f"xtile{s}", [128, 1024], F32, stack=stack)
        p.sbuf(f"xsq{s}", [128, 1024], F32, stack=stack)
        p.sbuf(f"wbufg{s}", [128, 2048], BF16, stack=stack)
        p.sbuf(f"wbufu{s}", [128, 2048], BF16, stack=stack)
        p.sbuf(f"wbufd{s}", [128, 2048], BF16, stack=stack)
        p.sbuf(f"gsq{s}", [128, 512], F32, stack=stack)
        p.sbuf(f"gsig{s}", [128, 512], F32, stack=stack)



def build_program(cfg):
    nc = bass.Bass("TRN2", target_bir_lowering=False)
    es = contextlib.ExitStack()
    with es:
        p = Prog(nc, es)
        alloc_dram(p, cfg)
        alloc_persist(p)
        build_p0(p, cfg)
        p.barrier()
        upto = cfg.get("upto", "full")
        if upto == "p0":
            with contextlib.ExitStack() as dbgst:
                db = p.sbuf("dbgb", [128, 2048], BF16, stack=dbgst)
                df = p.sbuf("dbgf", [128, 2048], F32, stack=dbgst)
                last = None
                for ht in range(16):
                    mld = p.op("sync", lambda e, ht=ht: e.dma_start(
                        out=db[:, :], in_=p.dr["wscr_g"][128 * ht:128 * ht + 128, :]),
                        dma=True, sem="ldc0", waits=[last])
                    mcv = p.op("vector", lambda e: e.tensor_copy(df[:, :], db[:, :]),
                               waits=[mld])
                    last = p.op("gpsimd", lambda e, ht=ht: e.dma_start(
                        out=p.dr["out"][128 * ht:128 * ht + 128, :], in_=df[:, :]),
                        dma=True, sem="stx0", waits=[mcv])
                p.op("gpsimd", lambda e: e.nop(), waits=[last])
            p.emit_all()
            return nc, p
        if upto == "gate1":
            with contextlib.ExitStack() as ffn_stack:
                alloc_ffn_bufs(p, ffn_stack)
                xn_marks = build_ln(p, p.dr["xin"], 0, p.sb["xn"], None, None)
                mwg = p.op("sync", lambda e: e.dma_start(
                    out=p.sb["wbufg0"][:, :], in_=p.dr["wscr_g"][0:128, :]),
                    dma=True, sem="ldg0", waits=[p.ag_done["g"]])
                mwu = p.op("sync", lambda e: e.dma_start(
                    out=p.sb["wbufu0"][:, :], in_=p.dr["wscr_u"][0:128, :]),
                    dma=True, sem="ldu0", waits=[p.ag_done["u"]])
                outm = []
                for half in range(2):
                    for kt in range(NKT):
                        mg = p.op("tensor", lambda e, kt=kt, half=half: e.matmul(
                            p.ps[0][:, :], p.sb["wbufg0"][:, 128 * kt:128 * kt + 128],
                            p.sb["xn"][:, 1024 * kt + 512 * half:1024 * kt + 512 * half + 512],
                            start=(kt == 0), stop=(kt == NKT - 1)),
                            waits=[mwg, xn_marks["last"], outm[-1] if outm else None] if kt == 0 else ())
                    for kt in range(NKT):
                        mu = p.op("tensor", lambda e, kt=kt, half=half: e.matmul(
                            p.ps[2][:, :], p.sb["wbufu0"][:, 128 * kt:128 * kt + 128],
                            p.sb["xn"][:, 1024 * kt + 512 * half:1024 * kt + 512 * half + 512],
                            start=(kt == 0), stop=(kt == NKT - 1)),
                            waits=[mwu] if kt == 0 else ())
                    msq = p.op("scalar", lambda e: e.activation(
                        p.sb["gsq0"][:, :], p.ps[0][:, :], AF.Square, scale=SQRT_C2),
                        waits=[mg])
                    mm2 = p.op("vector", lambda e: e.scalar_tensor_tensor(
                        p.sb["gsq0"][:, :], p.sb["gsq0"][:, :], GELU_C1,
                        p.ps[0][:, :], ALU.add, ALU.mult), waits=[msq])
                    msig = p.op("scalar", lambda e: e.activation(
                        p.sb["gsig0"][:, :], p.sb["gsq0"][:, :], AF.Sigmoid), waits=[mm2])
                    mgs = p.op("vector", lambda e: e.tensor_tensor(
                        p.sb["gsig0"][:, :], p.ps[0][:, :], p.sb["gsig0"][:, :],
                        ALU.mult), waits=[msig, mu])
                    mint = p.op("vector", lambda e, half=half: e.tensor_tensor(
                        p.sb["inter"][:, 512 * half:512 * half + 512],
                        p.sb["gsig0"][:, :], p.ps[2][:, :], ALU.mult))
                    mcv = p.op("vector", lambda e, half=half: e.tensor_copy(
                        p.sb["dacc"][:, 512 * half:512 * half + 512],
                        p.sb["inter"][:, 512 * half:512 * half + 512]))
                    outm.append(p.op("gpsimd", lambda e, half=half: e.dma_start(
                        out=p.dr["out"][0:128, 512 * half:512 * half + 512],
                        in_=p.sb["dacc"][:, 512 * half:512 * half + 512]),
                        dma=True, sem="stx0", waits=[mcv]))
                p.op("gpsimd", lambda e: e.nop(), waits=[outm[-1]])
            p.emit_all()
            return nc, p
        if upto == "ln":
            with contextlib.ExitStack() as ffn_stack:
                alloc_ffn_bufs(p, ffn_stack)
                xn_marks = build_ln(p, p.dr["xin"], 0, p.sb["xn"], None, None)
                last = None
                for kt in range(16):
                    mcv = p.op("vector", lambda e, kt=kt: e.tensor_copy(
                        p.sb["dacc"][:, 1024 * kt:1024 * kt + 1024],
                        p.sb["xn"][:, 1024 * kt:1024 * kt + 1024]),
                        waits=[xn_marks[kt], last])
                    last = p.op("gpsimd", lambda e, kt=kt: e.dma_start(
                        out=p.dr["out"][128 * kt:128 * kt + 128, 0:1024],
                        in_=p.sb["dacc"][:, 1024 * kt:1024 * kt + 1024]),
                        dma=True, sem="stx0", waits=[mcv])
                p.op("gpsimd", lambda e: e.nop(), waits=[last])
            p.emit_all()
            return nc, p
        with contextlib.ExitStack() as ffn_stack:
            alloc_ffn_bufs(p, ffn_stack)
            dst1 = p.dr["out"] if upto == "x1" else p.dr["x1s"]
            st1 = build_ffn(p, cfg, p.dr["xin"], dst1, "ffn1")
            if upto == "x1":
                p.op("gpsimd", lambda e: e.nop(), waits=[st1[-1]])
                p.emit_all()
                return nc, p
        if upto == "cpy":
            m = p.op("gpsimd", lambda e: e.dma_start(
                out=p.dr["out"][:, :], in_=p.dr["x1s"][:, :]), dma=True, sem="stx0")
            p.op("gpsimd", lambda e: e.nop(), waits=[m])
            p.emit_all()
            return nc, p
        p.barrier()
        p.flush()
        build_attn(p, cfg)
        p.barrier()
        if cfg.get("attn_stop"):
            m = p.op("gpsimd", lambda e: e.dma_start(
                out=p.dr["out"][:, :], in_=p.dr["x1s"][:, :]), dma=True, sem="stx0")
            p.op("gpsimd", lambda e: e.nop(), waits=[m])
            p.emit_all()
            return nc, p
        if cfg.get("dump_dram"):
            nm = cfg["dump_dram"]
            if nm == "ag2_out":
                srews = p.dr[nm].rearrange("(a b) c -> a (b c)", b=16)
                emitf = lambda e: e.dma_start(out=p.dr["out"][0:520, 0:1024], in_=srews)
            else:
                s32 = p.dr[nm].bitcast(F32)
                r, cdim = s32.shape
                emitf = lambda e, s32=s32, r=r, cdim=cdim: e.dma_start(
                    out=p.dr["out"][0:r, 0:cdim], in_=s32)
            m = p.op("gpsimd", emitf, dma=True, sem="stx0", waits=[p.x2_done])
            p.op("gpsimd", lambda e: e.nop(), waits=[m])
            p.emit_all()
            return nc, p
        if upto == "x2":
            m = p.op("gpsimd", lambda e: e.dma_start(
                out=p.dr["out"][:, :], in_=p.dr["x2s"][:, :]), dma=True, sem="stx0")
            p.op("gpsimd", lambda e: e.nop(), waits=[m])
            p.emit_all()
            return nc, p
        p.barrier()
        p.flush()
        with contextlib.ExitStack() as ffn_stack:
            alloc_ffn_bufs(p, ffn_stack)
            st3 = build_ffn(p, cfg, p.dr["x2s"], p.dr["out"], "ffn2")
            p.op("gpsimd", lambda e: e.nop(), waits=[st3[-1], st3[-2]])
        p.emit_all()
    return nc, p



def alloc_attn_psum(p, stack):
    p.aps = {}
    p._uniq = getattr(p, "_uniq", 0) + 1
    for n in ["pA", "pB", "pC", "pD", "pE", "pF", "pG"]:
        p.aps[n] = stack.enter_context(
            p.nc.psum_tensor(f"{n}_{p._uniq}", [128, 512], F32))
    p.aps["pT"] = stack.enter_context(
        p.nc.psum_tensor(f"pT_{p._uniq}", [128, 1024], BF16))


def alloc_attn_outer(p, stack):
    for q in range(8):
        p.sbuf(f"slocA{q}", [64, 64], F32, stack=stack)
        p.sbuf(f"slocB{q}", [64, 64], F32, stack=stack)
        p.sbuf(f"sloc{q}", [128, 64], F32, stack=stack)
        p.sbuf(f"sinA{q}", [64, 64], BF16, stack=stack)
        p.sbuf(f"sinB{q}", [64, 64], BF16, stack=stack)
    p.sbuf("ltot", [128, 8], F32, stack=stack)
    p.sbuf("colrelstash", [128, 256], F32, stack=stack)
    p.sbuf("colrelstashB", [64, 256], F32, stack=stack)
    p.sbuf("xnprev", [128, 16], F32, stack=stack)


def alloc_attn_s1(p, stack):
    p.sbuf("xn2", [128, 16 * 513], BF16, stack=stack)
    p.sbuf("delta", [128, 16 * 512], BF16, stack=stack)
    p.sbuf("xi", [128, 16 * 512], BF16, stack=stack)
    for s in range(2):
        p.sbuf(f"whb{s}", [128, 4096], BF16, stack=stack)
        p.sbuf(f"xtile{s}", [128, 512], F32, stack=stack)
        p.sbuf(f"xsq{s}", [128, 512], F32, stack=stack)
        p.sbuf(f"etmp{s}", [128, 512], F32, stack=stack)
        p.sbuf(f"swp{s}", [64, 2048], BF16, stack=stack)
    for i in range(5):
        p.sbuf(f"lr{i}", [64, 512], BF16, stack=stack)
    p.sbuf("twa", [64, 512], BF16, stack=stack)
    p.sbuf("v64", [64, 8 * 1024], BF16, stack=stack)
    p.sbuf("Pb", [128, 8 * 512], BF16, stack=stack)
    p.sbuf("CG", [128, 8 * 512], F32, stack=stack)
    p.sbuf("rb", [128, 8 * 512], BF16, stack=stack)
    p.sbuf("kb", [128, 8 * 512], BF16, stack=stack)
    p.sbuf("rt", [128, 8 * 512], BF16, stack=stack)
    p.sbuf("ktl", [128, 8 * 512], BF16, stack=stack)
    p.sbuf("ktmp", [128, 8 * 64], BF16, stack=stack)
    p.sbuf("bktmpA", [64, 8 * 64], BF16, stack=stack)
    p.sbuf("bktmpB", [64, 8 * 64], BF16, stack=stack)
    p.sbuf("slocbA", [64, 8 * 64], BF16, stack=stack)
    p.sbuf("slocbB", [64, 8 * 64], BF16, stack=stack)
    p.sbuf("khT", [64, 8 * 128], BF16, stack=stack)
    p.sbuf("msb", [64, 8 * 128], BF16, stack=stack)
    p.sbuf("ov0", [64, 128], BF16, stack=stack)
    p.sbuf("ov1", [64, 128], BF16, stack=stack)
    p.sbuf("mcols", [128, 64], F32, stack=stack)
    p.sbuf("ecols", [128, 64], F32, stack=stack)
    p.sbuf("scols", [128, 64], F32, stack=stack)
    p.sbuf("acols", [128, 64], F32, stack=stack)
    p.sbuf("fcols", [128, 64], F32, stack=stack)
    p.sbuf("kcols", [128, 64], F32, stack=stack)
    p.sbuf("ccols", [128, 64], F32, stack=stack)
    p.sbuf("colsB", [64, 128], F32, stack=stack)
    p.sbuf("zcol", [128, 1], F32, stack=stack)
    p.sbuf("gq0", [128, 512], BF16, stack=stack)
    p.sbuf("mtile0", [128, 512], BF16, stack=stack)
    p.sbuf("wx2t0", [64, 2048], BF16, stack=stack)
    p.sbuf("twb", [64, 1024], BF16, stack=stack)


def build_ln_strip(p, src, t0, width, out_fn, psA, psB, psum_free=None,
                   xtile_free=None, prefix=0):
    xt = [p.sb["xtile0"], p.sb["xtile1"]]
    xq = [p.sb["xsq0"], p.sb["xsq1"]]
    sq_marks, ld_marks = {}, {}
    summ = []
    for kt in range(NKT):
        slot = kt % 2
        w = [sq_marks.get(kt - 2), xtile_free if kt < 2 else None,
             summ[-1] if kt >= 2 else None]
        ld_marks[kt] = p.op("scalar", lambda e, kt=kt, slot=slot: e.dma_start(
            out=xt[slot][:, 0:width], in_=src[128 * kt:128 * kt + 128, t0:t0 + width]),
            dma=True, sem=f"ldx{slot}", waits=w)
        sq_marks[kt] = p.op("scalar", lambda e, slot=slot: e.activation(
            xq[slot][:, 0:width], xt[slot][:, 0:width], AF.Square), waits=[ld_marks[kt]])
        p.op("tensor", lambda e, slot=slot, kt=kt: e.matmul(
            psA[0:1, 0:width], p.sb["ones1"][:, 0:1], xt[slot][:, 0:width],
            start=(kt == 0), stop=(kt == NKT - 1)),
            waits=[ld_marks[kt]] + ([psum_free] if kt == 0 else []))
        m = p.op("tensor", lambda e, slot=slot, kt=kt: e.matmul(
            psB[0:1, 0:width], p.sb["ones1"][:, 0:1], xq[slot][:, 0:width],
            start=(kt == 0), stop=(kt == NKT - 1)), waits=[sq_marks[kt]])
        summ.append(m)
    mrow, vrow, rrow = p.sb["mrow"], p.sb["vrow"], p.sb["rrow"]
    tmp = rrow
    p.op("vector", lambda e: e.tensor_scalar_mul(
        mrow[0:1, 0:width], psA[0:1, 0:width], 1.0 / H), waits=[summ[-1]])
    p.op("vector", lambda e: e.tensor_scalar_mul(
        vrow[0:1, 0:width], psB[0:1, 0:width], 1.0 / H))
    p.op("vector", lambda e: e.tensor_tensor(
        tmp[0:1, 0:width], mrow[0:1, 0:width], mrow[0:1, 0:width], ALU.mult))
    mvar = p.op("vector", lambda e: e.tensor_tensor(
        vrow[0:1, 0:width], vrow[0:1, 0:width], tmp[0:1, 0:width], ALU.subtract))
    mln = p.op("scalar", lambda e: e.activation(
        tmp[0:1, 0:width], vrow[0:1, 0:width], AF.Ln,
        bias=p.sb["epscol"][0:1, 0:1]), waits=[mvar])
    mrs = p.op("scalar", lambda e: e.activation(
        vrow[0:1, 0:width], tmp[0:1, 0:width], AF.Exp, scale=-0.5), waits=[mln])
    p.op("tensor", lambda e: e.matmul(
        psA[:, 0:width], p.sb["onesr"][0:1, :], mrow[0:1, 0:width],
        start=True, stop=True), waits=[mrs])
    bcl = p.op("tensor", lambda e: e.matmul(
        psB[:, 0:width], p.sb["onesr"][0:1, :], vrow[0:1, 0:width],
        start=True, stop=True))
    xn_marks = {}
    m = None
    for kt in range(NKT):
        slot = kt % 2
        mld = p.op("scalar", lambda e, kt=kt, slot=slot: e.dma_start(
            out=xt[slot][:, 0:width], in_=src[128 * kt:128 * kt + 128, t0:t0 + width]),
            dma=True, sem=f"ldx{slot}", waits=[xn_marks.get(kt - 2),
                                               summ[-1] if kt < 2 else None])
        p.op("vector", lambda e, slot=slot: e.tensor_tensor(
            xq[slot][:, 0:width], xt[slot][:, 0:width], psA[:, 0:width],
            ALU.subtract), waits=[mld, bcl, sq_marks[NKT - 1] if kt < 2 else None])
        m = out_fn(kt, slot)
        xn_marks[kt] = m
    xn_marks["last"] = m
    return xn_marks


def build_attn(p, cfg):
    nc = p.nc
    NST = 4
    SW = 512

    with contextlib.ExitStack() as outer:
        alloc_attn_outer(p, outer)
        alloc_attn_psum(p, outer)
        A = p.aps
        for q in range(8):
            p.op("vector", lambda e, q=q: e.memset(p.sb[f"slocA{q}"][:, :], 0.0))
            p.op("vector", lambda e, q=q: e.memset(p.sb[f"slocB{q}"][:, :], 0.0))
        p.op("vector", lambda e: e.memset(p.sb["ltot"][:, :], 0.0))

        magi = p.op("gpsimd", lambda e: e.dma_start(
            out=p.dr["ag1_in"][:, :], in_=p.sb["agcol"][:, :]), dma=True, sem="stag")
        mag1 = p.op("gpsimd", lambda e: e.collective_compute(
            "AllGather", ALU.bypass, replica_groups=[list(range(NCORES))],
            ins=[p.dr["ag1_in"][:, :].opt()], outs=[p.dr["ag1_out"][:, :].opt()]),
            collective=True, waits=[magi])

        with contextlib.ExitStack() as s1:
            alloc_attn_s1(p, s1)
            p.op("vector", lambda e: e.memset(p.sb["zcol"][:, :], 0.0))
            p.op("vector", lambda e: e.memset(p.sb["slocbA"][:, :], 0.0))
            p.op("vector", lambda e: e.memset(p.sb["slocbB"][:, :], 0.0))

            xpt = [p.sb["xtile0"], p.sb["xtile1"]]
            mxp = p.op("vector", lambda e: e.memset(p.sb["xnprev"][:, :], 0.0),
                       waits=[mag1])
            for j in range(7):
                mld = p.op("scalar", lambda e, j=j: e.dma_start(
                    out=xpt[j % 2][:, 0:16],
                    in_=p.dr["ag1_out"][128 * j:128 * j + 128, :]),
                    dma=True, sem=f"ldx{j % 2}", waits=[mag1, mxp])
                mxp = p.op("vector", lambda e, j=j: e.scalar_tensor_tensor(
                    p.sb["xnprev"][:, :], xpt[j % 2][:, 0:16],
                    p.sb["cmask"][:, 8 + j:9 + j], p.sb["xnprev"][:, :],
                    ALU.mult, ALU.add), waits=[mld])
            mm1 = p.op("tensor", lambda e: e.matmul(
                A["pA"][0:1, 0:16], p.sb["ones1"][:, 0:1], p.sb["xnprev"][:, :],
                start=True, stop=True), waits=[mxp])
            msq = p.op("scalar", lambda e: e.activation(
                xpt[0][:, 0:16], p.sb["xnprev"][:, :], AF.Square), waits=[mxp])
            mm2 = p.op("tensor", lambda e: e.matmul(
                A["pB"][0:1, 0:16], p.sb["ones1"][:, 0:1], xpt[0][:, 0:16],
                start=True, stop=True), waits=[msq])
            mrd1 = p.op("vector", lambda e: e.reduce_sum(
                p.sb["mrow"][0:1, 0:1], A["pA"][0:1, 0:16], mybir.AxisListType.X),
                waits=[mm1, mm2])
            mrd2 = p.op("vector", lambda e: e.reduce_sum(
                p.sb["mrow"][0:1, 1:2], A["pB"][0:1, 0:16], mybir.AxisListType.X))
            mscl = p.op("vector", lambda e: e.tensor_scalar_mul(
                p.sb["mrow"][0:1, 0:2], p.sb["mrow"][0:1, 0:2], 1.0 / H),
                waits=[mrd1, mrd2])
            msqm = p.op("vector", lambda e: e.tensor_tensor(
                p.sb["mrow"][0:1, 2:3], p.sb["mrow"][0:1, 0:1],
                p.sb["mrow"][0:1, 0:1], ALU.mult), waits=[mscl])
            mv = p.op("vector", lambda e: e.tensor_tensor(
                p.sb["mrow"][0:1, 1:2], p.sb["mrow"][0:1, 1:2],
                p.sb["mrow"][0:1, 2:3], ALU.subtract), waits=[msqm])
            ml = p.op("scalar", lambda e: e.activation(
                p.sb["mrow"][0:1, 3:4], p.sb["mrow"][0:1, 1:2], AF.Ln,
                bias=p.sb["epscol"][0:1, 0:1]), waits=[mv])
            mr = p.op("scalar", lambda e: e.activation(
                p.sb["mrow"][0:1, 4:5], p.sb["mrow"][0:1, 3:4], AF.Exp,
                scale=-0.5), waits=[ml])
            mbc = p.op("tensor", lambda e: e.matmul(
                A["pA"][:, 0:1], p.sb["onesr"][0:1, :], p.sb["mrow"][0:1, 0:1],
                start=True, stop=True), waits=[mr])
            mbc2 = p.op("tensor", lambda e: e.matmul(
                A["pB"][:, 0:1], p.sb["onesr"][0:1, :], p.sb["mrow"][0:1, 4:5],
                start=True, stop=True))
            mxs1 = p.op("vector", lambda e: e.tensor_scalar(
                p.sb["xnprev"][:, :], p.sb["xnprev"][:, :], A["pA"][:, 0:1],
                None, ALU.subtract), waits=[mbc, mbc2])
            mxs2 = p.op("vector", lambda e: e.tensor_scalar(
                p.sb["xnprev"][:, :], p.sb["xnprev"][:, :], A["pB"][:, 0:1],
                None, ALU.mult), waits=[mxs1])
            mxnp = p.op("vector", lambda e: e.tensor_scalar(
                p.sb["xnprev"][:, :], p.sb["xnprev"][:, :],
                p.sb["cmask"][:, 16:17], None, ALU.mult), waits=[mxs2])

            pm = {"lnfree": mbc2}
            for st in range(NST):
                t0 = SW * st
                sm = {}
                bnd = []
                for kt in range(NKT):
                    if st == 0:
                        bnd.append(p.op("vector", lambda e, kt=kt: e.tensor_copy(
                            p.sb["xn2"][:, 513 * kt:513 * kt + 1],
                            p.sb["xnprev"][:, kt:kt + 1]), waits=[mxnp]))
                    else:
                        bnd.append(p.op("vector", lambda e, kt=kt: e.tensor_copy(
                            p.sb["xn2"][:, 513 * kt:513 * kt + 1],
                            p.sb["xn2"][:, 513 * kt + 512:513 * kt + 513]),
                            waits=[pm.get(("xi_done", st - 1))]))

                def out_fn(kt, slot, bnd=bnd):
                    return p.op("vector", lambda e, kt=kt, slot=slot: e.tensor_tensor(
                        p.sb["xn2"][:, 513 * kt + 1:513 * kt + 513],
                        p.sb[f"xsq{slot}"][:, 0:512],
                        A["pB"][:, 0:512], ALU.mult), waits=[bnd[kt]])

                xnm = build_ln_strip(p, p.dr["x1s"], t0, SW, out_fn,
                                     A["pA"], A["pB"],
                                     psum_free=pm.get("lnfree"),
                                     xtile_free=pm.get("xtfree"))
                pm["xtfree"] = xnm["last"]
                for kt in range(NKT):
                    sm[("delta", kt)] = p.op("vector", lambda e, kt=kt: e.tensor_tensor(
                        p.sb["delta"][:, 512 * kt:512 * kt + 512],
                        p.sb["xn2"][:, 513 * kt:513 * kt + 512],
                        p.sb["xn2"][:, 513 * kt + 1:513 * kt + 513], ALU.subtract),
                        waits=[xnm[kt]])
                x0m = []
                for kt in range(NKT):
                    x0m.append(p.op("vector", lambda e, kt=kt: e.scalar_tensor_tensor(
                        p.sb["xi"][:, 512 * kt:512 * kt + 512],
                        p.sb["delta"][:, 512 * kt:512 * kt + 512],
                        p.sb["mux8"][:, kt:kt + 1],
                        p.sb["xn2"][:, 513 * kt + 1:513 * kt + 513],
                        ALU.mult, ALU.add),
                        waits=[sm[("delta", kt)],
                               pm.get(("xiPE", st - 1)) if kt == 0 else None]))

                if cfg.get("attn_stop") == "ln2":
                    continue
                wev, wcons = {}, {}

                def wload(ev, emit):
                    wev[ev] = p.op("sync", emit, dma=True, sem=f"ldw{ev % 2}",
                                   waits=[wcons.get(ev - 2,
                                                    pm.get(f"watfin{ev % 2}")),
                                          p.conv_fence])

                def emit_q(e, name, qh, slot):
                    return e.dma_start(
                        out=p.sb[f"whb{slot}"][:, :].rearrange("p (a b) -> p a b", b=256),
                        in_=p.dr[name].rearrange("p (a h b) -> p a h b", h=4, b=256)
                            [:, :, qh, :])

                wload(0, lambda e: e.dma_start(
                    out=p.sb["whb0"][:, 0:16 * 192].rearrange("p (a b) -> p a b", b=192),
                    in_=p.dr["wx1b"].rearrange("p (a h b) -> p a h b", h=1, b=320)
                        [:, :, 0, 0:192]))
                wload(1, lambda e: e.dma_start(
                    out=p.sb["whb1"][:, 0:16 * 128].rearrange("p (a b) -> p a b", b=128),
                    in_=p.dr["wx1b"].rearrange("p (a h b) -> p a h b", h=1, b=320)
                        [:, :, 0, 192:320]))
                def wload_next(ev):
                    if ev > 15:
                        return
                    if 2 <= ev <= 5:
                        wload(ev, lambda e, qh=ev - 2: emit_q(e, "wrb", qh, ev % 2))
                    elif ev == 6:
                        wload(6, lambda e: e.dma_start(
                            out=p.sb["whb0"][:, 0:16 * 64], in_=p.dr["wwab"][:, :]))
                    elif 7 <= ev <= 10:
                        wload(ev, lambda e, qh=ev - 7: emit_q(e, "wkb", qh, ev % 2))
                    elif 11 <= ev <= 14:
                        wload(ev, lambda e, qh=ev - 11: emit_q(e, "wvb", qh, ev % 2))
                    else:
                        wload(15, lambda e: e.dma_start(
                            out=p.sb["whb1"][:, 0:16 * 64], in_=p.dr["wgab"][:, :]))

                prev_ev = pm.get("pCfree")
                lr_marks = {}
                for i in range(5):
                    ev = 0 if i < 3 else 1
                    w0 = 192 if i < 3 else 128
                    od0 = 64 * i if i < 3 else 64 * (i - 3)
                    slotw = ev % 2
                    mm = None
                    for kt in range(NKT):
                        mm = p.op("tensor", lambda e, od0=od0, kt=kt, slotw=slotw, w0=w0: e.matmul(
                            A["pC"][0:64, 0:512],
                            p.sb[f"whb{slotw}"][:, w0 * kt + od0:w0 * kt + od0 + 64],
                            p.sb["xi"][:, 512 * kt:512 * kt + 512],
                            start=(kt == 0), stop=(kt == NKT - 1)),
                            waits=[wev[ev], x0m[-1], prev_ev] if kt == 0 else ())
                    prev_ev = p.op("scalar", lambda e, i=i: e.activation(
                        p.sb[f"lr{i}"][:, 0:512], A["pC"][0:64, 0:512], AF.Tanh),
                        waits=[mm])
                    lr_marks[i] = prev_ev
                lr_last = prev_ev
                wcons[0] = lr_marks[2]
                wcons[1] = lr_marks[4]
                wload_next(2)
                wload_next(3)

                if cfg.get("attn_stop") == "lr":
                    continue
                for i in range(cfg.get("attn_imax", 5)):
                    wq = [sm.get(("musdone", i - 1), pm.get("musfin")),
                          p.conv_fence]
                    wx2ld = p.op("sync", lambda e, i=i: e.dma_start(
                        out=p.sb["wx2t0"][:, :],
                        in_=p.dr["wx2bs"][:, 2048 * i:2048 * i + 2048]), dma=True,
                        sem="ldq0", waits=wq)
                    xim = []
                    for kt in range(NKT):
                        ps = A["pD"] if kt % 2 == 0 else A["pE"]
                        wmus = [wx2ld, lr_marks[i], sm.get(("musfree", i, kt - 2))]
                        if i == 0 and kt < 2:
                            for x in pm.get(("pDEfree", st), []):
                                wmus.append(x)
                        if i == 4 and kt < 2:
                            wmus.append(sm[("vproj", 3)])
                        if i == 2 and kt < 2:
                            wmus.append(sm[("proj", 1)])
                        mm = p.op("tensor", lambda e, kt=kt, i=i, ps=ps: e.matmul(
                            ps[:, 0:512],
                            p.sb["wx2t0"][:, 128 * kt:128 * kt + 128],
                            p.sb[f"lr{i}"][:, 0:512],
                            start=True, stop=True), waits=wmus)
                        m1 = p.op("vector", lambda e, kt=kt, i=i, ps=ps: e.tensor_scalar(
                            p.sb["mtile0"][:, 0:512], ps[:, 0:512],
                            p.sb["xb8"][:, 16 * i + kt:16 * i + kt + 1], None,
                            ALU.add), waits=[mm])
                        sm[("musfree", i, kt)] = m1
                        p.op("vector", lambda e, kt=kt: e.tensor_tensor(
                            p.sb["mtile0"][:, 0:512], p.sb["mtile0"][:, 0:512],
                            p.sb["delta"][:, 512 * kt:512 * kt + 512], ALU.mult))
                        xim.append(p.op("vector", lambda e, kt=kt: e.tensor_tensor(
                            p.sb["xi"][:, 512 * kt:512 * kt + 512],
                            p.sb["mtile0"][:, 0:512],
                            p.sb["xn2"][:, 513 * kt + 1:513 * kt + 513], ALU.add)))
                    sm[("musdone", i)] = xim[-1]
                    if i == 4:
                        pm[("xi_done", st)] = xim[-1]

                    if i in (0, 2):
                        dst = "rb" if i == 0 else "kb"
                        ev0 = 2 if i == 0 else 7
                        evs = []
                        for j in range(8):
                            qh = j // 2
                            ev = ev0 + qh
                            slotw = ev % 2
                            ps = A["pA"] if j % 2 == 0 else A["pB"]
                            wj = [wev[ev], xim[-1]]
                            if len(evs) >= 2:
                                wj.append(evs[-2])
                            elif i == 0:
                                wj.append(pm.get(("pABfree", st)))
                            mm = None
                            for kt in range(NKT):
                                mm = p.op("tensor", lambda e, j=j, kt=kt, slotw=slotw, ps=ps: e.matmul(
                                    ps[:, 0:512],
                                    p.sb[f"whb{slotw}"][:, 256 * kt + 128 * (j % 2):256 * kt + 128 * (j % 2) + 128],
                                    p.sb["xi"][:, 512 * kt:512 * kt + 512],
                                    start=(kt == 0), stop=(kt == NKT - 1)),
                                    waits=wj if kt == 0 else ())
                            evs.append(p.op("scalar", lambda e, j=j, dst=dst, ps=ps: e.activation(
                                p.sb[dst][:, 512 * j:512 * j + 512], ps[:, 0:512],
                                AF.Identity), waits=[mm]))
                            if j % 2 == 1:
                                wcons[ev] = evs[-1]
                                wload_next(ev + 2)
                        sm[("proj", i)] = evs[-1]
                    elif i in (1, 4):
                        bcol = "bw8" if i == 1 else "bg8"
                        ev = 6 if i == 1 else 15
                        slotw = ev % 2
                        mtb = p.op("sync", lambda e, i=i: e.dma_start(
                            out=p.sb["twb"][:, :],
                            in_=p.dr["wwbs" if i == 1 else "wgbs"][:, :]),
                            dma=True, sem="ldtb",
                            waits=[sm.get(("lowfin", i - 3), pm.get("twbfree")),
                                   p.conv_fence])
                        mm = None
                        for kt in range(NKT):
                            mm = p.op("tensor", lambda e, kt=kt, slotw=slotw: e.matmul(
                                A["pC"][0:64, 0:512],
                                p.sb[f"whb{slotw}"][:, 64 * kt:64 * kt + 64],
                                p.sb["xi"][:, 512 * kt:512 * kt + 512],
                                start=(kt == 0), stop=(kt == NKT - 1)),
                                waits=[wev[ev], xim[-1],
                                       lr_last if i == 1 else sm[("lowev", 1)]]
                                if kt == 0 else ())
                        mtw = p.op("scalar", lambda e: e.activation(
                            p.sb["twa"][0:64, 0:512], A["pC"][0:64, 0:512], AF.Tanh),
                            waits=[mm])
                        sm[("lowev", i)] = mtw
                        wcons[ev] = mtw
                        wload_next(ev + 2)
                        evs = []
                        for j in range(8):
                            ps = A["pD"] if j % 2 == 0 else A["pE"]
                            wj = [mtw, mtb]
                            if len(evs) >= 2:
                                wj.append(evs[-2])
                            else:
                                wj.append(sm[("musfree", i, 14 + j)])
                            mm = p.op("tensor", lambda e, j=j, ps=ps: e.matmul(
                                ps[:, 0:512], p.sb["twb"][:, 128 * j:128 * j + 128],
                                p.sb["twa"][0:64, 0:512], start=True, stop=True),
                                waits=wj)
                            if j == 7:
                                sm[("lowfin", i)] = mm
                            if i == 1:
                                evs.append(p.op("scalar", lambda e, j=j, ps=ps, bcol=bcol: e.activation(
                                    p.sb["Pb"][:, 512 * j:512 * j + 512], ps[:, 0:512],
                                    AF.Exp, bias=p.sb[bcol][:, j:j + 1]), waits=[mm]))
                            else:
                                mge = p.op("scalar", lambda e, j=j, ps=ps, bcol=bcol: e.activation(
                                    p.sb["gq0"][:, 0:512], ps[:, 0:512],
                                    AF.Identity, bias=p.sb[bcol][:, j:j + 1]),
                                    waits=[mm, sm.get(("gstash", j - 1),
                                                      pm.get(("gstash", st - 1)))])
                                evs.append(p.op("scalar", lambda e, j=j, t0=t0: e.dma_start(
                                    out=p.dr["gstash"][128 * j:128 * j + 128, t0:t0 + 512],
                                    in_=p.sb["gq0"][:, 0:512]), dma=True,
                                    sem="stg0", waits=[mge]))
                                sm[("gstash", j)] = evs[-1]
                        sm[("proj", i)] = evs[-1]
                        if i == 4:
                            pm[("gstash", st)] = evs[-1]
                    else:
                        for qh in range(4):
                            ev = 11 + qh
                            slotw = ev % 2
                            evs = []
                            for c in range(8):
                                psv = A["pC"] if c % 2 == 0 else A["pD"]
                                wv0 = [wev[ev], xim[-1]]
                                if len(evs) >= 2:
                                    wv0.append(evs[-2])
                                elif qh == 0:
                                    wv0 += [sm[("lowev", 1)], sm[("proj", 1)]]
                                mm0 = None
                                for kt in range(NKT):
                                    mm0 = p.op("tensor", lambda e, c=c, kt=kt, slotw=slotw, psv=psv: e.matmul(
                                        psv[0:64, 0:256],
                                        p.sb["xi"][:, 512 * kt + 64 * c:512 * kt + 64 * c + 64],
                                        p.sb[f"whb{slotw}"][:, 256 * kt:256 * kt + 256],
                                        start=(kt == 0), stop=(kt == NKT - 1)),
                                        waits=wv0 if kt == 0 else ())
                                evs.append(p.op("scalar", lambda e, c=c, qh=qh, psv=psv: e.activation(
                                    p.sb["v64"][:, 1024 * c + 256 * qh:1024 * c + 256 * qh + 256],
                                    psv[0:64, 0:256], AF.Identity), waits=[mm0]))
                            wcons[ev] = evs[-1]
                            wload_next(ev + 2)
                        sm[("vproj", 3)] = wcons[14]
                        sm[("proj", 3)] = wcons[14]

                    if i == 2 and cfg.get("attn_dk", 9) >= 1:
                        dm = []
                        for j in range(8):
                            dm.append(p.op("vector", lambda e, j=j: e.tensor_tensor_scan(
                                p.sb["CG"][:, 512 * j:512 * j + 512],
                                p.sb["Pb"][:, 512 * j:512 * j + 512],
                                p.sb["zcol"][:, 0:1].broadcast_to([128, 512]), 0.0,
                                ALU.add, ALU.add), waits=[sm[("proj", 1)]]))
                        if cfg.get("attn_dk", 9) < 2:
                            sm["decay"] = dm[-1]
                            continue
                        for j in range(8):
                            p.op("vector", lambda e, j=j: e.tensor_copy(
                                p.sb["mcols"][:, 8 * j:8 * j + 8],
                                p.sb["CG"][:, 512 * j + 31:512 * j + 512:64]),
                                waits=[dm[j]])
                        for j in range(8):
                            p.op("vector", lambda e, j=j: e.tensor_copy(
                                p.sb["ecols"][:, 8 * j:8 * j + 8],
                                p.sb["CG"][:, 512 * j + 63:512 * j + 512:64]))
                        for j in range(8):
                            p.op("vector", lambda e, j=j: e.memset(
                                p.sb["scols"][:, 8 * j:8 * j + 1], 0.0))
                        mcx = None
                        for j in range(8):
                            mcx = p.op("vector", lambda e, j=j: e.tensor_copy(
                                p.sb["scols"][:, 8 * j + 1:8 * j + 8],
                                p.sb["ecols"][:, 8 * j:8 * j + 7]))
                        mav = p.op("vector", lambda e: e.tensor_tensor(
                            p.sb["acols"][:, :], p.sb["scols"][:, :],
                            p.sb["ecols"][:, :], ALU.subtract), waits=[mcx])
                        macol = p.op("scalar", lambda e: e.activation(
                            p.sb["acols"][:, :], p.sb["acols"][:, :], AF.Exp),
                            waits=[mav])
                        mfv = p.op("vector", lambda e: e.tensor_tensor(
                            p.sb["fcols"][:, :], p.sb["scols"][:, :],
                            p.sb["mcols"][:, :], ALU.subtract), waits=[macol])
                        mfcol = p.op("scalar", lambda e: e.activation(
                            p.sb["fcols"][:, :], p.sb["fcols"][:, :], AF.Exp),
                            waits=[mfv])
                        mkv = p.op("vector", lambda e: e.tensor_tensor(
                            p.sb["kcols"][:, :], p.sb["mcols"][:, :],
                            p.sb["ecols"][:, :], ALU.subtract), waits=[mfcol])
                        mkcol = p.op("scalar", lambda e: e.activation(
                            p.sb["kcols"][:, :], p.sb["kcols"][:, :], AF.Exp),
                            waits=[mkv])
                        mcc = None
                        for j in range(8):
                            mcc = p.op("vector", lambda e, j=j: e.tensor_scalar(
                                p.sb["ccols"][:, 8 * j:8 * j + 8],
                                p.sb["mcols"][:, 8 * j:8 * j + 8],
                                p.sb["ltot"][:, j:j + 1], None, ALU.add),
                                waits=[mkcol])
                        mccol = p.op("scalar", lambda e: e.activation(
                            p.sb["ccols"][:, :], p.sb["ccols"][:, :], AF.Exp,
                            scale=-1.0), waits=[mcc])
                        mcst = p.op("vector", lambda e, st=st: e.tensor_copy(
                            p.sb["colrelstash"][:, 64 * st:64 * st + 64],
                            p.sb["ccols"][:, :]), waits=[mccol])
                        if st == 0 and cfg.get("dbg_point") == "cols":
                            for (cn, src, wm) in [
                                    ("acols", "acols", macol), ("fcols", "fcols", mfcol),
                                    ("kcols", "kcols", mkcol), ("mcols", "mcols", mkcol),
                                    ("ecols", "ecols", mkcol), ("scols", "scols", mkcol),
                                    ("ccols", "ccols", mccol)]:
                                off = {"acols": 0, "fcols": 64, "kcols": 128,
                                       "mcols": 192, "ecols": 256, "scols": 320,
                                       "ccols": 384}[cn]
                                p.op("gpsimd", lambda e, src=src, off=off: e.dma_start(
                                    out=p.dr["dbgf"][:, off:off + 64],
                                    in_=p.sb[src][:, :]), dma=True, sem="stdb",
                                    waits=[wm])
                        mlt = None
                        for j in range(8):
                            mlt = p.op("vector", lambda e, j=j: e.tensor_tensor(
                                p.sb["ltot"][:, j:j + 1],
                                p.sb["ltot"][:, j:j + 1],
                                p.sb["CG"][:, 512 * j + 511:512 * j + 512], ALU.add),
                                waits=[mcst])
                        sm["ltotupd"] = mlt
                        if cfg.get("attn_dk", 9) < 3:
                            sm["decay"] = mlt
                            continue
                        mby = p.op("gpsimd", lambda e: e.dma_start(
                            out=p.sb["colsB"][:, 0:64],
                            in_=p.sb["acols"][64:128, :]), dma=True, sem="stag",
                            waits=[macol])
                        mby = p.op("gpsimd", lambda e: e.dma_start(
                            out=p.sb["colsB"][:, 64:128],
                            in_=p.sb["fcols"][64:128, :]), dma=True, sem="stag",
                            waits=[mfcol])
                        mby2 = p.op("gpsimd", lambda e, st=st: e.dma_start(
                            out=p.sb["colrelstashB"][:, 64 * st:64 * st + 64],
                            in_=p.sb["ccols"][64:128, :]), dma=True, sem="stag",
                            waits=[mcst])
                        sm["colsB"] = mby
                        sm["colsBrel"] = mby2
                        if cfg.get("attn_dk", 9) < 4:
                            sm["decay"] = mby2
                            continue
                        etm = {}
                        for j in range(8):
                            e0 = p.op("vector", lambda e, j=j: e.tensor_tensor(
                                p.sb["etmp0"][:, 0:512].rearrange("p (a b) -> p a b", b=64),
                                p.sb["mcols"][:, 8 * j:8 * j + 8]
                                    .rearrange("p (a o) -> p a o", o=1)
                                    .broadcast_to([128, 8, 64]),
                                p.sb["CG"][:, 512 * j:512 * j + 512]
                                    .rearrange("p (a b) -> p a b", b=64),
                                ALU.subtract), waits=[mkcol, etm.get(j - 1)])
                            e1 = p.op("vector", lambda e, j=j: e.tensor_tensor(
                                p.sb["etmp1"][:, 0:512], p.sb["etmp0"][:, 0:512],
                                p.sb["Pb"][:, 512 * j:512 * j + 512], ALU.add))
                            e1x = p.op("scalar", lambda e: e.activation(
                                p.sb["etmp1"][:, 0:512], p.sb["etmp1"][:, 0:512],
                                AF.Exp), waits=[e1, mccol])
                            mrt = p.op("vector", lambda e, j=j: e.tensor_tensor(
                                p.sb["rt"][:, 512 * j:512 * j + 512],
                                p.sb["rb"][:, 512 * j:512 * j + 512],
                                p.sb["etmp1"][:, 0:512], ALU.mult),
                                waits=[e1x, sm[("proj", 0)],
                                       pm.get(("rtstash", st - 1)) if j == 0 else None])
                            e0x = p.op("scalar", lambda e: e.activation(
                                p.sb["etmp0"][:, 0:512], p.sb["etmp0"][:, 0:512],
                                AF.Exp, scale=-1.0), waits=[mrt])
                            mkt = p.op("vector", lambda e, j=j: e.tensor_tensor(
                                p.sb["ktl"][:, 512 * j:512 * j + 512],
                                p.sb["kb"][:, 512 * j:512 * j + 512],
                                p.sb["etmp0"][:, 0:512], ALU.mult),
                                waits=[e0x, sm[("proj", 2)]])
                            etm[j] = mkt
                        sm["decay"] = etm[7]
                        strt = []
                        for j in range(8):
                            strt.append(p.op("scalar", lambda e, j=j, t0=t0: e.dma_start(
                                out=p.dr["rrelstash"][128 * j:128 * j + 128, t0:t0 + 512],
                                in_=p.sb["rt"][:, 512 * j:512 * j + 512]),
                                dma=True, sem="strt", waits=[etm[j]]))
                        pm[("rtstash", st)] = strt[-1]

                if cfg.get("attn_stop") == "proj" or cfg.get("attn_imax", 5) < 5:
                    break
                pm["twbfree"] = sm[("lowfin", 4)]
                pm["watfin0"] = wcons[14]
                pm["watfin1"] = wcons[15]
                pm["musfin"] = sm[("musdone", 4)]
                pm[("xiPE", st)] = sm[("lowev", 4)]

                if cfg.get("attn_stop") == "proj":
                    continue
                oev, upd = {}, {}
                tr_ev = pm.get("trv")
                swm = {}
                for c in range(8):
                    slot = c % 2
                    wsw = [sm["decay"], upd.get((c - 2, 7))]
                    for ti, tn in enumerate(["rt", "ktl", "kb", "rb"]):
                        swm[c] = p.op("gpsimd", lambda e, ti=ti, tn=tn, c=c, slot=slot: e.dma_start(
                            out=p.sb[f"swp{slot}"][:, 512 * ti:512 * ti + 512]
                                .rearrange("p (a b) -> p a b", b=64),
                            in_=p.sb[tn][64:128, 64 * c:4096:512]
                                .rearrange("p (a b) -> p a b", b=64)
                            if False else
                            p.sb[tn].ap()[64:128, :]
                                .rearrange("p (a b) -> p a b", b=512)[:, :, 64 * c:64 * c + 64]),
                            dma=True, sem=f"swp{slot}", waits=wsw if ti == 0 else ())
                    sbA, sbB = {}, {}
                    for q in range(8):
                        sbA[q] = p.op("vector", lambda e, q=q, c=c: e.tensor_scalar(
                            p.sb["slocbA"][:, 64 * q:64 * q + 64],
                            p.sb[f"slocA{q}"][:, :],
                            p.sb["fcols"][0:64, 8 * q + c:8 * q + c + 1], None,
                            ALU.mult), waits=[upd.get((c - 1, q)), sm["decay"]])
                    for q in range(8):
                        sbB[q] = p.op("vector", lambda e, q=q, c=c: e.tensor_scalar(
                            p.sb["slocbB"][:, 64 * q:64 * q + 64],
                            p.sb[f"slocB{q}"][:, :],
                            p.sb["colsB"][:, 64 + 8 * q + c:64 + 8 * q + c + 1], None,
                            ALU.mult), waits=[upd.get((c - 1, q)), sm["colsB"]])
                    ktm = {}
                    for q in range(8):
                        ktm[q] = p.op("vector", lambda e, q=q, c=c: e.tensor_scalar(
                            p.sb["ktmp"][:, 64 * q:64 * q + 64],
                            p.sb["ktl"][:, 512 * q + 64 * c:512 * q + 64 * c + 64],
                            p.sb["kcols"][:, 8 * q + c:8 * q + c + 1], None,
                            ALU.mult), waits=[tr_ev])
                    bkA, bkB = {}, {}
                    for q in range(8):
                        bkA[q] = p.op("vector", lambda e, q=q, c=c: e.tensor_scalar(
                            p.sb["bktmpA"][:, 64 * q:64 * q + 64],
                            p.sb["kb"][0:64, 512 * q + 64 * c:512 * q + 64 * c + 64],
                            p.sb["bonus8"][0:64, q:q + 1], None, ALU.mult),
                            waits=[sm.get(("scanMb", c - 1, q), sm[("proj", 2)])])
                    for q in range(8):
                        bkB[q] = p.op("vector", lambda e, q=q, c=c, slot=slot: e.tensor_scalar(
                            p.sb["bktmpB"][:, 64 * q:64 * q + 64],
                            p.sb[f"swp{slot}"][:, 1024 + 64 * q:1024 + 64 * q + 64],
                            p.sb["bonus8B"][:, q:q + 1], None, ALU.mult),
                            waits=[swm[c], sm.get(("scanMb", c - 1, q), sm[("proj", 2)])])
                    mtr = None
                    for q in range(8):
                        mtr = p.op("tensor", lambda e, q=q: e.transpose(
                            A["pT"][0:64, 128 * q:128 * q + 128],
                            p.sb["ktmp"][:, 64 * q:64 * q + 64],
                            p.sb["identb"][:, :]),
                            waits=[ktm[q]] + ([tr_ev] if q == 0 else []))
                    tr_ev = p.op("vector", lambda e: e.tensor_copy(
                        p.sb["khT"][:, :], A["pT"][0:64, :]), waits=[mtr])
                    for q in range(8):
                        c0 = 512 * q + 64 * c
                        sw0 = 64 * q
                        psM = A["pA"] if q % 2 == 0 else A["pB"]
                        psB2 = A["pE"] if q % 2 == 0 else A["pF"]
                        psO = A["pG"] if q % 2 == 0 else A["pC"]
                        wsc = [sm.get(("scanM", c, q - 2)),
                               sm.get(("scanM", c - 1, q + 6)),
                               sm["decay"], swm[c]]
                        if st == 0 and c == 0 and q < 2:
                            wsc.append(sm[("proj", 2)])
                        p.op("tensor", lambda e, c0=c0, psM=psM: e.matmul(
                            psM[0:64, 0:64], p.sb["ktl"][0:64, c0:c0 + 64],
                            p.sb["rt"][0:64, c0:c0 + 64], start=True, stop=True),
                            waits=wsc)
                        mS = p.op("tensor", lambda e, sw0=sw0, slot=slot, psM=psM: e.matmul(
                            psM[0:64, 64:128],
                            p.sb[f"swp{slot}"][:, 512 + sw0:512 + sw0 + 64],
                            p.sb[f"swp{slot}"][:, 0 + sw0:0 + sw0 + 64],
                            start=True, stop=True))
                        wmb = [bkA[q], bkB[q], sm.get(("scanMb", c, q - 2)),
                               sm.get(("scanMb", c - 1, q + 6))]
                        if c == 0 and q < 2:
                            wmb.append(sm[("proj", 4)])
                        p.op("tensor", lambda e, c0=c0, q=q, psB2=psB2: e.matmul(
                            psB2[0:64, 0:64], p.sb["bktmpA"][:, 64 * q:64 * q + 64],
                            p.sb["rb"][0:64, c0:c0 + 64], start=True, stop=True),
                            waits=wmb)
                        mB = p.op("tensor", lambda e, sw0=sw0, q=q, slot=slot, psB2=psB2: e.matmul(
                            psB2[0:64, 64:128], p.sb["bktmpB"][:, 64 * q:64 * q + 64],
                            p.sb[f"swp{slot}"][:, 1536 + sw0:1536 + sw0 + 64],
                            start=True, stop=True))
                        e1 = p.op("vector", lambda e, psM=psM: e.tensor_tensor(
                            p.sb["etmp0"][0:64, 0:128], psM[0:64, 0:128],
                            p.sb["mask2b"][:, :], ALU.mult), waits=[mS])
                        sm[("scanM", c, q)] = e1
                        e2 = p.op("vector", lambda e, psB2=psB2: e.tensor_tensor(
                            p.sb["etmp1"][0:64, 0:128], psB2[0:64, 0:128],
                            p.sb["eye2b"][:, :], ALU.mult), waits=[mB])
                        sm[("scanMb", c, q)] = e2
                        e3 = p.op("vector", lambda e, q=q: e.tensor_tensor(
                            p.sb["msb"][:, 128 * q:128 * q + 128],
                            p.sb["etmp0"][0:64, 0:128], p.sb["etmp1"][0:64, 0:128],
                            ALU.add))
                        vA = 1024 * c + 128 * q
                        wo1 = [e3, sbA[q], sbB[q], sm[("proj", 3)],
                               sm.get(("scanOe", c, q - 2)),
                               sm.get(("scanOe", c - 1, q + 6))]
                        p.op("tensor", lambda e, vA=vA, q=q, psO=psO: e.matmul(
                            psO[0:64, 0:64], p.sb["v64"][:, vA:vA + 64],
                            p.sb["msb"][0:64, 128 * q:128 * q + 64],
                            start=True, stop=False), waits=wo1)
                        p.op("tensor", lambda e, c0=c0, q=q, psO=psO: e.matmul(
                            psO[0:64, 0:64], p.sb["slocbA"][:, 64 * q:64 * q + 64],
                            p.sb["rt"][0:64, c0:c0 + 64], start=False, stop=True))
                        p.op("tensor", lambda e, vA=vA, q=q, psO=psO: e.matmul(
                            psO[0:64, 64:128], p.sb["v64"][:, vA + 64:vA + 128],
                            p.sb["msb"][0:64, 128 * q + 64:128 * q + 128],
                            start=True, stop=False))
                        mo = p.op("tensor", lambda e, sw0=sw0, q=q, slot=slot, psO=psO: e.matmul(
                            psO[0:64, 64:128], p.sb["slocbB"][:, 64 * q:64 * q + 64],
                            p.sb[f"swp{slot}"][:, 0 + sw0:0 + sw0 + 64],
                            start=False, stop=True))
                        oev[(c, q)] = p.op("vector", lambda e, c=c, q=q, psO=psO: e.tensor_copy(
                            p.sb[f"ov{(8 * c + q) % 2}"][:, :],
                            psO[0:64, 0:128]), waits=[mo, sm.get(("ost", 8 * c + q - 2))])
                        sm[("scanOe", c, q)] = oev[(c, q)]
                        p.op("scalar", lambda e, c=c, q=q, t0=t0: e.dma_start(
                            out=p.dr["ostash"][128 * q:128 * q + 64,
                                               t0 + 64 * c:t0 + 64 * c + 64],
                            in_=p.sb[f"ov{(8 * c + q) % 2}"][:, 0:64]),
                            dma=True, sem=f"sto{(8 * c + q) % 2}", waits=[oev[(c, q)]])
                        sm[("ost", 8 * c + q)] = p.op("scalar",
                            lambda e, c=c, q=q, t0=t0: e.dma_start(
                                out=p.dr["ostash"][128 * q + 64:128 * q + 128,
                                                   t0 + 64 * c:t0 + 64 * c + 64],
                                in_=p.sb[f"ov{(8 * c + q) % 2}"][:, 64:128]),
                            dma=True, sem=f"sto{(8 * c + q) % 2}")
                        wu = [tr_ev, upd.get((c - 1, q)), sm.get(("scanU", c, q - 1))]
                        if c == 0 and q == 0:
                            wu += [sm[("proj", 4)], sm[("musfree", 4, 15)],
                                   sm[("musfree", 4, 14)]]
                        p.op("tensor", lambda e, vA=vA, q=q: e.matmul(
                            A["pD"][0:64, 0:64], p.sb["khT"][0:64, 128 * q:128 * q + 64],
                            p.sb["v64"][:, vA:vA + 64], start=True, stop=True),
                            waits=wu)
                        mU = p.op("tensor", lambda e, vA=vA, q=q: e.matmul(
                            A["pD"][0:64, 64:128],
                            p.sb["khT"][0:64, 128 * q + 64:128 * q + 128],
                            p.sb["v64"][:, vA + 64:vA + 128], start=True, stop=True))
                        mua = p.op("vector", lambda e, c=c, q=q: e.scalar_tensor_tensor(
                            p.sb[f"slocA{q}"][:, :], p.sb[f"slocA{q}"][:, :],
                            p.sb["acols"][0:64, 8 * q + c:8 * q + c + 1],
                            A["pD"][0:64, 0:64], ALU.mult, ALU.add), waits=[mU])
                        upd[(c, q)] = p.op("vector", lambda e, c=c, q=q: e.scalar_tensor_tensor(
                            p.sb[f"slocB{q}"][:, :], p.sb[f"slocB{q}"][:, :],
                            p.sb["colsB"][:, 8 * q + c:8 * q + c + 1],
                            A["pD"][0:64, 64:128], ALU.mult, ALU.add), waits=[mua])
                        sm[("scanU", c, q)] = upd[(c, q)]
                pm["trv"] = tr_ev
                pm["lnfree"] = sm[("scanM", 7, 7)]
                pm["pCfree"] = oev[(7, 7)]
                pm[("pABfree", st + 1)] = sm[("scanM", 7, 7)]
                pm[("pDEfree", st + 1)] = [upd[(7, 7)], sm[("scanMb", 7, 7)]]
                pm[("ostash", st)] = sm[("ost", 63)]

            if cfg.get("attn_stop") in ("proj", "scan", "ln2", "lr"):
                return
            stfence = p.fence("scalar", ["sto0", "sto1", "stg0", "strt"])
            ags = []
            for q in range(8):
                ags.append(p.op("gpsimd", lambda e, q=q: e.dma_start(
                    out=p.dr["ag2_in"][128 * q:128 * q + 64, :],
                    in_=p.sb[f"slocA{q}"][:, :]), dma=True, sem="stag",
                    waits=[upd[(7, q)]]))
                ags.append(p.op("gpsimd", lambda e, q=q: e.dma_start(
                    out=p.dr["ag2_in"][128 * q + 64:128 * q + 128, :],
                    in_=p.sb[f"slocB{q}"][:, :]), dma=True, sem="stag"))
            matot = p.op("scalar", lambda e: e.activation(
                p.sb["acols"][:, 0:8], p.sb["ltot"][:, :], AF.Exp, scale=-1.0),
                waits=[upd[(7, 7)], sm["ltotupd"], stfence])
            for q in range(8):
                ags.append(p.op("gpsimd", lambda e, q=q: e.dma_start(
                    out=p.dr["ag2_in"][1024 + 2 * q:1024 + 2 * q + 2, :],
                    in_=p.sb["acols"][:, q:q + 1]), dma=True, sem="stag",
                    waits=[matot]))
            gfence = p.fence("gpsimd", ["stag"])
            mag2 = p.op("gpsimd", lambda e: e.collective_compute(
                "AllGather", ALU.bypass, replica_groups=[list(range(NCORES))],
                ins=[p.dr["ag2_in"][:, :].opt()], outs=[p.dr["ag2_out"][:, :].opt()]),
                collective=True, waits=[gfence, stfence])
            macc = {}
            for q in range(8):
                macc[q] = p.op("vector", lambda e, q=q: e.memset(
                    p.sb[f"sloc{q}"][:, :], 0.0), waits=[ags[-1], mag2])
            mcand = {}
            for gr in range(2):
                for pos in range(1, 4):
                    r = 4 * gr + pos - 1
                    for q in range(8):
                        p.op("scalar", lambda e, r=r, q=q: e.dma_start(
                            out=p.sb["xi"].ap().bitcast(F32)[:, 64 * q:64 * q + 64],
                            in_=p.dr["ag2_out"][1040 * r + 128 * q:1040 * r + 128 * q + 128, :]),
                            dma=True, sem=f"ldx{q % 2}",
                            waits=[mag2, mcand.get((gr, pos - 1, q))])
                        p.op("scalar", lambda e, r=r, q=q: e.dma_start(
                            out=p.sb["mcols"][:, 8 * q:8 * q + 1],
                            in_=p.dr["ag2_out"][1040 * r + 1024 + 2 * q:1040 * r + 1024 + 2 * q + 2, :]),
                            dma=True, sem=f"ldx{q % 2}")
                    relfence = p.fence("scalar", ["ldx0", "ldx1"])
                    for q in range(8):
                        srcap = p.sb["xi"].ap().bitcast(F32)[:, 64 * q:64 * q + 64]
                        if pos == 1:
                            mcand[(gr, pos, q)] = p.op("vector", lambda e, q=q, srcap=srcap: e.tensor_copy(
                                p.sb["CG"][:, 64 * q:64 * q + 64],
                                srcap), waits=[relfence, macc[q]])
                        else:
                            mcand[(gr, pos, q)] = p.op("vector", lambda e, q=q, srcap=srcap: e.scalar_tensor_tensor(
                                p.sb["CG"][:, 64 * q:64 * q + 64],
                                p.sb["CG"][:, 64 * q:64 * q + 64],
                                p.sb["mcols"][:, 8 * q:8 * q + 1],
                                srcap, ALU.mult, ALU.add), waits=[relfence])
                    for q in range(8):
                        mcand[(gr, pos, q)] = p.op("vector", lambda e, q=q, gr=gr, pos=pos: e.scalar_tensor_tensor(
                            p.sb[f"sloc{q}"][:, :], p.sb["CG"][:, 64 * q:64 * q + 64],
                            p.sb["cmask"][:, 4 * gr + pos:4 * gr + pos + 1],
                            p.sb[f"sloc{q}"][:, :], ALU.mult, ALU.add))
            sincp = None
            sinld = {}
            for q in range(8):
                p.op("vector", lambda e, q=q: e.tensor_copy(
                    p.sb[f"sinA{q}"][:, :], p.sb[f"sloc{q}"][0:64, :]))
                mdb = p.op("gpsimd", lambda e, q=q: e.dma_start(
                    out=p.sb["CG"][:, 512 + 64 * q:512 + 64 * q + 64]
                    if False else p.sb["CG"][0:64, 64 * q:64 * q + 64],
                    in_=p.sb[f"sloc{q}"][64:128, :]), dma=True, sem="stag",
                    waits=[mcand[(1, 3, q)]])
                sincp = p.op("vector", lambda e, q=q: e.tensor_copy(
                    p.sb[f"sinB{q}"][:, :], p.sb["CG"][0:64, 64 * q:64 * q + 64]),
                    waits=[mdb])
            p.sin_ready = sincp

        if cfg.get("attn_stop") == "relay":
            return
        p.barrier()
        p.flush()
        with contextlib.ExitStack() as s2:
            p.sbuf("wosA", [64, 16 * 1024], BF16, stack=s2)
            p.sbuf("wosB", [64, 16 * 1024], BF16, stack=s2)
            p.sbuf("oaccA", [64, 8 * 512], F32, stack=s2)
            p.sbuf("oaccB", [64, 8 * 512], F32, stack=s2)
            p.sbuf("obA", [64, 8 * 512], BF16, stack=s2)
            p.sbuf("obB", [64, 8 * 512], BF16, stack=s2)
            p.sbuf("gbA", [64, 8 * 512], BF16, stack=s2)
            p.sbuf("gbB", [64, 8 * 512], BF16, stack=s2)
            p.sbuf("rtbA", [64, 8 * 512], BF16, stack=s2)
            p.sbuf("rtbB", [64, 8 * 512], BF16, stack=s2)
            p.sbuf("gatedA", [64, 8 * 512], BF16, stack=s2)
            p.sbuf("gatedB", [64, 8 * 512], BF16, stack=s2)
            for s in range(2):
                p.sbuf(f"xt2{s}", [128, 512], F32, stack=s2)
                p.sbuf(f"tmpa{s}", [64, 512], F32, stack=s2)
                p.sbuf(f"tmpb{s}", [64, 512], F32, stack=s2)
                p.sbuf(f"rrl{s}", [64, 512], BF16, stack=s2)
            p.sbuf("gstat", [1, 512], F32, stack=s2)
            p.sbuf("gstat2", [1, 512], F32, stack=s2)
            A = p.aps
            mwo = p.op("sync", lambda e: e.dma_start(
                out=p.sb["wosA"][:, :], in_=p.dr["wob"][0:64, :]), dma=True,
                sem="ldw0")
            mwo = p.op("sync", lambda e: e.dma_start(
                out=p.sb["wosB"][:, :], in_=p.dr["wob"][64:128, :]), dma=True,
                sem="ldw0")
            if cfg.get("dbg_point") == "wos":
                p.op("gpsimd", lambda e: e.dma_start(
                    out=p.dr["dbgb"][0:64, 0:2048],
                    in_=p.sb["wosA"][:, 0:2048]), dma=True, sem="stdb",
                    waits=[mwo])
                p.op("gpsimd", lambda e: e.dma_start(
                    out=p.dr["dbgb"][64:128, 0:2048],
                    in_=p.sb["wosB"][:, 0:2048]), dma=True, sem="stdb")
            s2m = {}
            for st in range(NST):
                t0 = SW * st
                for q in range(8):
                    p.op("scalar", lambda e, q=q, t0=t0: e.dma_start(
                        out=p.sb["obA"][:, 512 * q:512 * q + 512],
                        in_=p.dr["ostash"][128 * q:128 * q + 64, t0:t0 + 512]),
                        dma=True, sem="ldo", waits=[s2m.get(("orel", st - 1, q))])
                    p.op("scalar", lambda e, q=q, t0=t0: e.dma_start(
                        out=p.sb["obB"][:, 512 * q:512 * q + 512],
                        in_=p.dr["ostash"][128 * q + 64:128 * q + 128, t0:t0 + 512]),
                        dma=True, sem="ldo")
                    p.op("scalar", lambda e, q=q, t0=t0: e.dma_start(
                        out=p.sb["rtbA"][:, 512 * q:512 * q + 512],
                        in_=p.dr["rrelstash"][128 * q:128 * q + 64, t0:t0 + 512]),
                        dma=True, sem="ldo", waits=[s2m.get(("rrl", st - 1, q))])
                    p.op("scalar", lambda e, q=q, t0=t0: e.dma_start(
                        out=p.sb["rtbB"][:, 512 * q:512 * q + 512],
                        in_=p.dr["rrelstash"][128 * q + 64:128 * q + 128, t0:t0 + 512]),
                        dma=True, sem="ldo")
                    p.op("scalar", lambda e, q=q, t0=t0: e.dma_start(
                        out=p.sb["gbA"][:, 512 * q:512 * q + 512],
                        in_=p.dr["gstash"][128 * q:128 * q + 64, t0:t0 + 512]),
                        dma=True, sem="ldo", waits=[s2m.get(("gated", st - 1, q))])
                    p.op("scalar", lambda e, q=q, t0=t0: e.dma_start(
                        out=p.sb["gbB"][:, 512 * q:512 * q + 512],
                        in_=p.dr["gstash"][128 * q + 64:128 * q + 128, t0:t0 + 512]),
                        dma=True, sem="ldo")
                s2fence = p.fence("scalar", ["ldo"])
                for q in range(8):
                    for half in range(2):
                        hx = "A" if half == 0 else "B"
                        rtb = p.sb[f"rtb{hx}"]
                        crel = (p.sb["colrelstash"][0:64, 64 * st + 8 * q:64 * st + 8 * q + 8]
                                if half == 0 else
                                p.sb["colrelstashB"][:, 64 * st + 8 * q:64 * st + 8 * q + 8])
                        mrr = p.op("vector", lambda e, q=q, st=st, rtb=rtb, crel=crel, half=half: e.tensor_tensor(
                            p.sb[f"rrl{half}"][:, 0:512].rearrange("p (a b) -> p a b", b=64),
                            rtb[:, 512 * q:512 * q + 512]
                                .rearrange("p (a b) -> p a b", b=64),
                            crel.rearrange("p (a o) -> p a o", o=1)
                                .broadcast_to([64, 8, 64]),
                            ALU.mult), waits=[s2fence,
                                              s2m.get(("orelmm", st, 2 * q + half - 4)),
                                              p.sin_ready])
                        s2m[("rrl", st, q)] = mrr
                        ps = A["pA"] if half == 0 else A["pB"]
                        mo2 = p.op("tensor", lambda e, q=q, half=half, ps=ps: e.matmul(
                            ps[0:64, 0:512], p.sb[f"sin{hx}{q}" if False else ("sinA" if half == 0 else "sinB") + str(q)][:, :],
                            p.sb[f"rrl{half}"][:, 0:512], start=True, stop=True),
                            waits=[mrr, s2m.get(("orel", st, 2 * q + half - 4)),
                                   s2m.get(("orel", st - 1, 2 * q + half + 12)),
                                   s2m.get(("wofin", st - 1))])
                        s2m[("orelmm", st, 2 * q + half)] = mo2
                        oacc = p.sb[f"oacc{hx}"]
                        ob = p.sb[f"ob{hx}"]
                        mo3 = p.op("vector", lambda e, q=q, ps=ps, oacc=oacc, ob=ob: e.tensor_tensor(
                            oacc[:, 512 * q:512 * q + 512],
                            ob[:, 512 * q:512 * q + 512], ps[0:64, 0:512],
                            ALU.add), waits=[mo2, s2fence])
                        s2m[("orel", st, 2 * q + half)] = mo3
                        msq2 = p.op("scalar", lambda e, q=q, half=half, oacc=oacc: e.activation(
                            p.sb[f"tmpa{half}"][:, 0:512],
                            oacc[:, 512 * q:512 * q + 512], AF.Square),
                            waits=[mo3])
                        mms = p.op("tensor", lambda e, q=q, oacc=oacc: e.matmul(
                            A["pC"][0:1, 0:512], p.sb["ones1"][0:64, 0:1],
                            oacc[:, 512 * q:512 * q + 512], start=True, stop=True),
                            waits=[mo3, s2m.get(("gnst", st, 2 * q + half - 1))])
                        mms2 = p.op("tensor", lambda e, half=half: e.matmul(
                            A["pD"][0:1, 0:512], p.sb["ones1"][0:64, 0:1],
                            p.sb[f"tmpa{half}"][:, 0:512], start=True, stop=True),
                            waits=[msq2])
                        mst1 = p.op("vector", lambda e: e.tensor_scalar_mul(
                            p.sb["gstat"][0:1, 0:512], A["pC"][0:1, 0:512], 1.0 / 64),
                            waits=[mms, mms2])
                        p.op("vector", lambda e: e.tensor_scalar_mul(
                            p.sb["gstat2"][0:1, 0:512], A["pD"][0:1, 0:512], 1.0 / 64))
                        p.op("vector", lambda e: e.tensor_tensor(
                            p.sb["mrow"][0:1, 0:512], p.sb["gstat"][0:1, 0:512],
                            p.sb["gstat"][0:1, 0:512], ALU.mult))
                        mvar2 = p.op("vector", lambda e: e.tensor_tensor(
                            p.sb["gstat2"][0:1, 0:512], p.sb["gstat2"][0:1, 0:512],
                            p.sb["mrow"][0:1, 0:512], ALU.subtract))
                        s2m[("gnst", st, 2 * q + half)] = mvar2
                        mln2 = p.op("scalar", lambda e: e.activation(
                            p.sb["mrow"][0:1, 0:512], p.sb["gstat2"][0:1, 0:512],
                            AF.Ln, bias=p.sb["epscol"][0:1, 0:1]), waits=[mvar2])
                        mrs2 = p.op("scalar", lambda e: e.activation(
                            p.sb["gstat2"][0:1, 0:512], p.sb["mrow"][0:1, 0:512],
                            AF.Exp, scale=-0.5), waits=[mln2])
                        mbm = p.op("tensor", lambda e: e.matmul(
                            A["pE"][0:64, 0:512], p.sb["onesr"][0:1, 0:64],
                            p.sb["gstat"][0:1, 0:512], start=True, stop=True),
                            waits=[mrs2, s2m.get(("gnn", st, 2 * q + half - 1))])
                        mbr = p.op("tensor", lambda e: e.matmul(
                            A["pF"][0:64, 0:512], p.sb["onesr"][0:1, 0:64],
                            p.sb["gstat2"][0:1, 0:512], start=True, stop=True))
                        lncol = (p.sb["lnw8"][0:64, q:q + 1] if half == 0
                                 else p.sb["lnw8B"][:, q:q + 1])
                        lbcol = (p.sb["lnb8"][0:64, q:q + 1] if half == 0
                                 else p.sb["lnb8B"][:, q:q + 1])
                        mn1 = p.op("vector", lambda e, q=q, half=half, oacc=oacc: e.tensor_tensor(
                            p.sb[f"tmpa{half}"][:, 0:512],
                            oacc[:, 512 * q:512 * q + 512], A["pE"][0:64, 0:512],
                            ALU.subtract), waits=[mbm, mbr])
                        mn2 = p.op("vector", lambda e, half=half: e.tensor_tensor(
                            p.sb[f"tmpa{half}"][:, 0:512], p.sb[f"tmpa{half}"][:, 0:512],
                            A["pF"][0:64, 0:512], ALU.mult))
                        s2m[("gnn", st, 2 * q + half)] = mn2
                        mn3 = p.op("vector", lambda e, half=half, lncol=lncol, lbcol=lbcol: e.tensor_scalar(
                            p.sb[f"tmpa{half}"][:, 0:512], p.sb[f"tmpa{half}"][:, 0:512],
                            lncol, lbcol, ALU.mult, ALU.add))
                        gb = p.sb[f"gb{hx}"]
                        msg = p.op("scalar", lambda e, q=q, half=half, gb=gb: e.activation(
                            p.sb[f"tmpb{half}"][:, 0:512],
                            gb[:, 512 * q:512 * q + 512], AF.Sigmoid),
                            waits=[s2fence, mn2])
                        mgg = p.op("vector", lambda e, q=q, half=half, gb=gb: e.tensor_tensor(
                            p.sb[f"tmpb{half}"][:, 0:512], p.sb[f"tmpb{half}"][:, 0:512],
                            gb[:, 512 * q:512 * q + 512], ALU.mult), waits=[msg])
                        gated = p.sb[f"gated{hx}"]
                        mgt = p.op("vector", lambda e, q=q, half=half, gated=gated: e.tensor_tensor(
                            gated[:, 512 * q:512 * q + 512],
                            p.sb[f"tmpa{half}"][:, 0:512], p.sb[f"tmpb{half}"][:, 0:512],
                            ALU.mult), waits=[mn3])
                        s2m[("gated", st, 2 * q + half)] = mgt
                if st == 0 and cfg.get("dbg_point") == "gated2":
                    p.op("gpsimd", lambda e: e.dma_start(
                        out=p.dr["dbgb"][0:64, 0:2048],
                        in_=p.sb["gatedA"][:, 2048:4096]), dma=True, sem="stdb",
                        waits=[s2m[("gated", st, 15)]])
                    p.op("gpsimd", lambda e: e.dma_start(
                        out=p.dr["dbgb"][64:128, 0:2048],
                        in_=p.sb["gatedB"][:, 2048:4096]), dma=True, sem="stdb")
                if st == 0 and cfg.get("dbg_point") == "gated":
                    p.op("gpsimd", lambda e: e.dma_start(
                        out=p.dr["dbgb"][0:64, 0:2048],
                        in_=p.sb["gatedA"][:, 0:2048]), dma=True, sem="stdb",
                        waits=[s2m[("gated", st, 15)]])
                    p.op("gpsimd", lambda e: e.dma_start(
                        out=p.dr["dbgb"][64:128, 0:2048],
                        in_=p.sb["gatedB"][:, 0:2048]), dma=True, sem="stdb")
                if st == 0 and cfg.get("dbg_point") == "oacc":
                    p.op("gpsimd", lambda e: e.dma_start(
                        out=p.dr["dbgf"][0:64, 0:1024],
                        in_=p.sb["oaccA"][:, 0:1024]), dma=True, sem="stdb",
                        waits=[s2m[("gated", st, 15)]])
                    p.op("gpsimd", lambda e: e.dma_start(
                        out=p.dr["dbgf"][64:128, 0:1024],
                        in_=p.sb["oaccB"][:, 0:1024]), dma=True, sem="stdb")
                fin, stm = {}, {}
                for ht in range(NHT):
                    ps = A["pG"] if ht % 2 == 0 else A["pB"]
                    mld = p.op("scalar", lambda e, ht=ht, t0=t0: e.dma_start(
                        out=p.sb[f"xt2{ht % 2}"][:, 0:512],
                        in_=p.dr["x1s"][128 * ht:128 * ht + 128, t0:t0 + 512]),
                        dma=True, sem=f"ldx{ht % 2}",
                        waits=[stm.get(ht - 2), s2m.get(("xst", st - 1, ht + 14))])
                    mm = None
                    for q in range(8):
                        mm = p.op("tensor", lambda e, ht=ht, q=q, ps=ps: e.matmul(
                            ps[:, 0:512],
                            p.sb["wosA"][:, 2048 * q + 128 * ht:2048 * q + 128 * ht + 128],
                            p.sb["gatedA"][:, 512 * q:512 * q + 512],
                            start=(q == 0), stop=False),
                            waits=[mwo, s2m[("gated", st, 15)],
                                   fin.get(ht - 2)] if q == 0 else ())
                        mm = p.op("tensor", lambda e, ht=ht, q=q, ps=ps: e.matmul(
                            ps[:, 0:512],
                            p.sb["wosB"][:, 2048 * q + 128 * ht:2048 * q + 128 * ht + 128],
                            p.sb["gatedB"][:, 512 * q:512 * q + 512],
                            start=False, stop=(q == 7)))
                    mf = p.op("vector", lambda e, ht=ht, ps=ps: e.tensor_tensor(
                        p.sb[f"xt2{ht % 2}"][:, 0:512], p.sb[f"xt2{ht % 2}"][:, 0:512],
                        ps[:, 0:512], ALU.add), waits=[mm, mld, stm.get(ht - 2)])
                    fin[ht] = mf
                    stm[ht] = p.op("gpsimd", lambda e, ht=ht, t0=t0: e.dma_start(
                        out=p.dr["x2s"][128 * ht:128 * ht + 128, t0:t0 + 512],
                        in_=p.sb[f"xt2{ht % 2}"][:, 0:512]), dma=True,
                        sem=f"stx{ht % 2}", waits=[mf])
                    s2m[("xst", st, ht)] = stm[ht]
                s2m[("wofin", st)] = fin[NHT - 1]
    p.x2_done = p.fence("gpsimd", ["stx0", "stx1"])


PERM_H = np.concatenate([np.arange(0, H, 2), np.arange(1, H, 2)])
PERM_N = np.concatenate([np.arange(0, MLP, 2), np.arange(1, MLP, 2)])


def prep_inputs(inp):
    A = {k[8:]: np.asarray(v) for k, v in inp.items() if k.startswith("L0_attn_")}
    Fn = {k[7:]: np.asarray(v) for k, v in inp.items() if k.startswith("L0_ffn_")}
    x = A["x"].reshape(2 * 8192, H).astype(np.float32)[:, PERM_H]

    gpkT = np.ascontiguousarray(Fn["gate_W_packed"].T[:, PERM_N]).astype(np.int32)
    upkT = np.ascontiguousarray(Fn["up_W_packed"].T[:, PERM_N]).astype(np.int32)
    dpkT = np.ascontiguousarray(Fn["down_W_packed"].T[:, PERM_H]).astype(np.int32)
    gscT = np.ascontiguousarray(Fn["gate_scales"].T[:, PERM_N]).astype(np.float32)
    uscT = np.ascontiguousarray(Fn["up_scales"].T[:, PERM_N]).astype(np.float32)
    dscT = np.ascontiguousarray(Fn["down_scales"].T[:, PERM_H]).astype(np.float32)

    com = {}
    com["wx1"] = np.ascontiguousarray(A["Wx1"][PERM_H, :]).astype(np.float32)
    wx2p = np.transpose(A["Wx2"], (2, 1, 0))[:, :, PERM_H]
    com["wx2"] = np.ascontiguousarray(wx2p.reshape(64, 5 * H)).astype(np.float32)
    xb = A["x_bias"][:, PERM_H].reshape(5, 16, 128)
    com["xb8in"] = np.ascontiguousarray(np.transpose(xb, (2, 0, 1)).reshape(128, 80)).astype(np.float32)
    com["mux8in"] = np.ascontiguousarray(A["mu_x"][PERM_H].reshape(16, 128).T).astype(np.float32)
    com["wr"] = np.ascontiguousarray(A["Wr"][PERM_H, :]).astype(np.float32)
    com["wk"] = np.ascontiguousarray(A["Wk"][PERM_H, :]).astype(np.float32)
    com["wv"] = np.ascontiguousarray(A["Wv"][PERM_H, :]).astype(np.float32)
    com["wwa"] = np.ascontiguousarray(A["Ww_a"][PERM_H, :]).astype(np.float32)
    com["wga"] = np.ascontiguousarray(A["Wg_a"][PERM_H, :]).astype(np.float32)
    com["wwb"] = np.ascontiguousarray(A["Ww_b"]).astype(np.float32)
    com["wgb"] = np.ascontiguousarray(A["Wg_b"]).astype(np.float32)
    com["bw8in"] = np.ascontiguousarray(A["bw"].reshape(8, 128).T).astype(np.float32)
    com["bg8in"] = np.ascontiguousarray(A["bg"].reshape(8, 128).T).astype(np.float32)
    com["bonus8in"] = np.ascontiguousarray(A["bonus"].reshape(-1).reshape(8, 128).T).astype(np.float32)
    com["lnw8in"] = np.ascontiguousarray(np.tile(A["ln_w"], 16).reshape(8, 128).T).astype(np.float32)
    com["lnb8in"] = np.ascontiguousarray(np.tile(A["ln_b"], 16).reshape(8, 128).T).astype(np.float32)
    com["wo"] = np.ascontiguousarray(A["Wo"][:, PERM_H]).astype(np.float32)
    com["ident"] = np.eye(128).astype(ml_dtypes.bfloat16)
    smask = (np.arange(64)[:, None] < np.arange(64)[None, :]).astype(np.float32)
    com["mask2"] = np.concatenate([smask, smask], 1).astype(ml_dtypes.bfloat16)
    eye = np.eye(64, dtype=np.float32)
    com["eye2"] = np.concatenate([eye, eye], 1).astype(ml_dtypes.bfloat16)
    com["bonus8Bin"] = np.ascontiguousarray(com["bonus8in"][64:128])
    com["lnw8Bin"] = np.ascontiguousarray(com["lnw8in"][64:128])
    com["lnb8Bin"] = np.ascontiguousarray(com["lnb8in"][64:128])
    bones = np.zeros((128, 2), np.float32)
    bones[0:64, 0] = 1.0; bones[64:128, 1] = 1.0
    com["bonesin"] = bones
    com["bonesTin"] = np.ascontiguousarray(bones.T)

    in_maps = []
    for c in range(NCORES):
        d = dict(com)
        r0 = c * 2048
        d["xin"] = np.ascontiguousarray(x[r0:r0 + 2048].T)
        d["gpk_my"] = np.ascontiguousarray(gpkT[:, 2048 * c:2048 * c + 2048])
        d["upk_my"] = np.ascontiguousarray(upkT[:, 2048 * c:2048 * c + 2048])
        d["dpk_my"] = np.ascontiguousarray(dpkT[1024 * c:1024 * c + 1024, :])
        d["gsc_my"] = np.ascontiguousarray(gscT[:, 2048 * c:2048 * c + 2048])
        d["usc_my"] = np.ascontiguousarray(uscT[:, 2048 * c:2048 * c + 2048])
        d["dsc_my"] = np.ascontiguousarray(dscT[64 * c:64 * c + 64, :])
        cm = np.zeros((128, 32), np.float32)
        gpos, gr = c % 4, c // 4
        cm[:, gr * 4 + gpos] = 1.0
        if gpos > 0:
            cm[:, 8 + (c - 1)] = 1.0
            cm[:, 16] = 1.0
        d["cmaskin"] = cm
        in_maps.append(d)
    return in_maps


def assemble_out(results, key="out"):
    full = np.empty((16384, H), np.float32)
    for c in range(NCORES):
        full[2048 * c:2048 * c + 2048] = results[c][key].T
    orig = np.empty_like(full)
    orig[:, PERM_H] = full
    return orig.reshape(2, 8192, H)



def kernel(**inputs):
    from concourse.bass_utils import run_bass_kernel_spmd
    in_maps = prep_inputs(inputs)
    nc, p = build_program({"upto": "full"})
    res = run_bass_kernel_spmd(nc, in_maps, core_ids=list(range(NCORES)))
    return assemble_out(res.results)


# version 02 (final)
# speedup vs baseline: 1.0621x; 1.0621x over previous
import contextlib
import numpy as np
import ml_dtypes
import concourse.bass as bass
import concourse.mybir as mybir

F32 = mybir.dt.float32
BF16 = mybir.dt.bfloat16
I32 = mybir.dt.int32
ALU = mybir.AluOpType
AF = mybir.ActivationFunctionType

H = 2048
MLP = 16384
TCORE = 2048
NKT = 16
NNT = 128
NHT = 16
GELU_C1 = 1.5957691216057308
GELU_C2 = 0.07135481283247557
SQRT_C2 = GELU_C2 ** 0.5
LN_EPS = 1e-5
NCORES = 8

ENGS = ["sync", "tensor", "vector", "scalar", "gpsimd"]


def down_scratch_row(nt):
    if nt < 64:
        return (nt // 8) * 2048 + (nt % 8) * 128
    return ((nt - 64) // 8) * 2048 + (8 + (nt - 64) % 8) * 128


class Op:
    __slots__ = ["emit", "waits", "inc_sem", "inc_k"]

    def __init__(self, emit, waits, inc_sem, inc_k):
        self.emit = emit
        self.waits = waits
        self.inc_sem = inc_sem
        self.inc_k = inc_k


class Prog:
    def __init__(self, nc, es):
        self.nc = nc
        self.es = es
        self.ops = {e: [] for e in ENGS}
        self.sems = {}
        for name in ["cPE", "cVE", "cAC", "cGP", "cSY", "dS", "dA", "dG", "cCC"]:
            self.sems[name] = [es.enter_context(nc.semaphore(name)), 0]
        self.eng_comp_sem = {"tensor": "cPE", "vector": "cVE", "scalar": "cAC",
                            "gpsimd": "cGP", "sync": "cSY"}
        self.eng_dma_sem = {"sync": "dS", "scalar": "dA", "gpsimd": "dG"}
        self.sb = {}
        self.dr = {}
        self.ps = []
        self.pending_barrier = {}

    def sbuf(self, name, shape, dt, stack=None):
        self._uniq = getattr(self, "_uniq", 0) + 1
        t = (stack or self.es).enter_context(
            self.nc.sbuf_tensor(f"{name}_{self._uniq}", list(shape), dt))
        self.sb[name] = t
        return t

    def dram(self, name, shape, dt, kind="Internal", addr_space="Local"):
        t = self.nc.dram_tensor(name, list(shape), dt, kind=kind,
                                addr_space=addr_space).ap()
        self.dr[name] = t
        return t

    def getsem(self, name):
        if name not in self.sems:
            self.sems[name] = [self.es.enter_context(self.nc.semaphore(name)), 0]
        return self.sems[name]

    def op(self, eng, emit, waits=(), dma=False, collective=False, sem=None,
           quiet=False):
        w = [x for x in waits if x is not None]
        assert all(x[0] != "QUIET" for x in w), "waited on a quiet op"
        if self.pending_barrier.get(eng):
            w = self.pending_barrier.pop(eng) + w
        if quiet:
            self.ops[eng].append(Op(emit, tuple(w), None, 0))
            return ("QUIET", None)
        if collective:
            semname, k = "cCC", 1
        elif dma:
            semname, k = sem or self.eng_dma_sem[eng], 16
            self.getsem(semname)
        else:
            semname, k = self.eng_comp_sem[eng], 1
        s = self.sems[semname]
        s[1] += k
        self.ops[eng].append(Op(emit, tuple(w), semname, k))
        return (semname, s[1])

    def fence(self, eng, semnames):
        w = [(n, self.sems[n][1]) for n in semnames if n in self.sems and self.sems[n][1] > 0]
        return self.op(eng, lambda e: e.nop(), waits=w)

    def barrier(self):
        snap = [(n, s[1]) for n, s in self.sems.items() if s[1] > 0]
        for e in ENGS:
            self.pending_barrier[e] = list(snap)

    def flush(self, scope=None):
        nc = self.nc
        if not any(self.ops[e] for e in ENGS):
            return
        if not hasattr(self, "_waited"):
            self._waited = {e: {} for e in ENGS}
        sctx = nc.named_scope(scope) if scope else contextlib.nullcontext()
        with sctx, nc.Block() as blk:
            for eng_name in ENGS:
                oplist = self.ops[eng_name]
                if not oplist:
                    continue

                def body(eng, oplist=oplist, eng_name=eng_name):
                    waited = self._waited[eng_name]
                    for o in oplist:
                        for (sn, cnt) in o.waits:
                            if waited.get(sn, 0) >= cnt:
                                continue
                            eng.wait_ge(self.sems[sn][0], cnt)
                            waited[sn] = cnt
                        instr = o.emit(eng)
                        if o.inc_sem is not None:
                            instr.then_inc(self.sems[o.inc_sem][0], o.inc_k)

                getattr(blk, eng_name)(body)
        self.ops = {e: [] for e in ENGS}

    def emit_all(self):
        self.flush("tail")

    def n_ops(self):
        return {e: len(self.ops[e]) for e in ENGS}



def alloc_dram(p, cfg):
    p.dram("xin", [H, TCORE], F32, kind="ExternalInput")
    for n in ["gpk_my", "upk_my", "dpk_my"]:
        p.dram(n, [1024, 2048], I32, kind="ExternalInput")
    for n in ["gsc_my", "usc_my", "dsc_my"]:
        p.dram(n, [64, 2048], F32, kind="ExternalInput")
    p.dram("wx1", [H, 320], F32, kind="ExternalInput")
    p.dram("wx2", [64, 5 * H], F32, kind="ExternalInput")
    p.dram("xb8in", [128, 80], F32, kind="ExternalInput")
    p.dram("mux8in", [128, 16], F32, kind="ExternalInput")
    p.dram("wr", [H, 1024], F32, kind="ExternalInput")
    p.dram("wk", [H, 1024], F32, kind="ExternalInput")
    p.dram("wv", [H, 1024], F32, kind="ExternalInput")
    p.dram("wwa", [H, 64], F32, kind="ExternalInput")
    p.dram("wga", [H, 64], F32, kind="ExternalInput")
    p.dram("wwb", [64, 1024], F32, kind="ExternalInput")
    p.dram("wgb", [64, 1024], F32, kind="ExternalInput")
    for n in ["bw8in", "bg8in", "bonus8in", "lnw8in", "lnb8in"]:
        p.dram(n, [128, 8], F32, kind="ExternalInput")
    for n in ["bonus8Bin", "lnw8Bin", "lnb8Bin"]:
        p.dram(n, [64, 8], F32, kind="ExternalInput")
    p.dram("wo", [1024, H], F32, kind="ExternalInput")
    p.dram("ident", [128, 128], BF16, kind="ExternalInput")
    p.dram("mask2", [64, 128], BF16, kind="ExternalInput")
    p.dram("eye2", [64, 128], BF16, kind="ExternalInput")
    p.dram("cmaskin", [128, 32], F32, kind="ExternalInput")
    p.dram("bonesin", [128, 2], F32, kind="ExternalInput")
    p.dram("bonesTin", [2, 128], F32, kind="ExternalInput")
    p.dram("out", [H, TCORE], F32, kind="ExternalOutput")
    for d in cfg.get("debug", []):
        p.dram("dbg_" + d, [H, TCORE], F32, kind="ExternalOutput")

    p.dram("x1s", [H, TCORE], F32)
    p.dram("x2s", [H, TCORE], F32)
    p.dram("gsp", [64, 2048], F32)
    p.dram("usp", [64, 2048], F32)
    p.dram("dsp", [64, 2048], F32)
    p.dram("shard_g", [2048, 2048], BF16)
    p.dram("shard_u", [2048, 2048], BF16)
    p.dram("shard_d", [2048, 2048], BF16)
    p.dram("wscr_g", [16384, 2048], BF16, addr_space="Shared")
    p.dram("wscr_u", [16384, 2048], BF16, addr_space="Shared")
    p.dram("wscr_d", [16384, 2048], BF16, addr_space="Shared")
    p.dram("wrb", [128, 16 * 1024], BF16)
    p.dram("wkb", [128, 16 * 1024], BF16)
    p.dram("wvb", [128, 16 * 1024], BF16)
    p.dram("wob", [128, 8 * 2048], BF16)
    p.dram("wwab", [128, 16 * 64], BF16)
    p.dram("wgab", [128, 16 * 64], BF16)
    p.dram("wx1b", [128, 16 * 320], BF16)
    p.dram("wx2bs", [64, 5 * 2048], BF16)
    p.dram("wwbs", [64, 1024], BF16)
    p.dram("wgbs", [64, 1024], BF16)
    p.dram("gstash", [1024, TCORE], BF16)
    p.dram("ostash", [1024, TCORE], BF16)
    p.dram("rrelstash", [1024, TCORE], BF16)
    p.dram("dbgf", [128, 1024], F32)
    p.dram("dbgb", [128, 2048], BF16)
    p.dram("ag1_in", [128, 16], F32)
    p.dram("ag1_out", [1024, 16], F32, addr_space="Shared")
    p.dram("ag2_in", [1040, 64], F32)
    p.dram("ag2_out", [8320, 64], F32, addr_space="Shared")




def alloc_persist(p):
    p.sbuf("ones1", [128, 1], F32)
    p.sbuf("ones128", [128, 128], BF16)
    p.sbuf("onesr", [1, 128], F32)
    p.sbuf("identb", [128, 128], BF16)
    p.sbuf("mask2b", [64, 128], BF16)
    p.sbuf("eye2b", [64, 128], BF16)
    p.sbuf("cmask", [128, 32], F32)
    p.sbuf("bonus8", [128, 8], F32)
    p.sbuf("lnw8", [128, 8], F32)
    p.sbuf("lnb8", [128, 8], F32)
    p.sbuf("bw8", [128, 8], F32)
    p.sbuf("bg8", [128, 8], F32)
    p.sbuf("mux8", [128, 16], F32)
    p.sbuf("bonus8B", [64, 8], F32)
    p.sbuf("lnw8B", [64, 8], F32)
    p.sbuf("lnb8B", [64, 8], F32)
    p.sbuf("xb8", [128, 80], F32)
    p.sbuf("mrow", [1, 1024], F32)
    p.sbuf("vrow", [1, 1024], F32)
    p.sbuf("rrow", [1, 1024], F32)
    p.sbuf("epscol", [1, 1], F32)
    p.sbuf("eps128", [128, 1], F32)
    p.sbuf("eps2", [2, 1], F32)
    p.sbuf("agcol", [128, 16], F32)



def build_p0(p, cfg):
    SCF = float(2.0 ** 126)
    p.op("vector", lambda e: e.memset(p.sb["ones1"][:, :], 1.0))
    p.op("vector", lambda e: e.memset(p.sb["ones128"][:, :], 1.0))
    p.op("vector", lambda e: e.memset(p.sb["epscol"][0:1, :], LN_EPS))
    p.op("vector", lambda e: e.memset(p.sb["eps128"][:, :], LN_EPS))
    p.op("vector", lambda e: e.memset(p.sb["eps2"][0:2, :], LN_EPS))
    p.op("vector", lambda e: e.memset(p.sb["onesr"][:, :], 1.0))
    for (dst, src) in [("identb", "ident"), ("mask2b", "mask2"), ("eye2b", "eye2"),
                       ("cmask", "cmaskin"), ("bonus8", "bonus8in"),
                       ("lnw8", "lnw8in"), ("lnb8", "lnb8in"), ("bw8", "bw8in"),
                       ("bg8", "bg8in"), ("mux8", "mux8in"), ("xb8", "xb8in"),
                       ("bonus8B", "bonus8Bin"), ("lnw8B", "lnw8Bin"),
                       ("lnb8B", "lnb8Bin")]:
        p.op("sync", lambda e, dst=dst, src=src: e.dma_start(
            out=p.sb[dst][:, :], in_=p.dr[src][:, :]), dma=True)
    p.consts_ready = p.fence("sync", ["dS"])

    with contextlib.ExitStack() as st0:
        cvt32 = [p.sbuf(f"cvt32{s}", [128, 2048], F32, stack=st0) for s in range(2)]
        cvt16 = [p.sbuf(f"cvt16{s}", [128, 2048], BF16, stack=st0) for s in range(2)]
        cv_marks = {}
        ci = 0
        for (srcn, dstn, width, nchunk) in [
                ("wr", "wrb", 1024, 16), ("wk", "wkb", 1024, 16),
                ("wv", "wvb", 1024, 16), ("wwa", "wwab", 64, 16),
                ("wga", "wgab", 64, 16), ("wx1", "wx1b", 320, 16),
                ("wo", "wob", 2048, 8)]:
            for j in range(nchunk):
                slot = ci % 2
                mld = p.op("sync", lambda e, srcn=srcn, j=j, width=width, slot=slot: e.dma_start(
                    out=cvt32[slot][:, 0:width], in_=p.dr[srcn][128 * j:128 * j + 128, :]),
                    dma=True, sem=f"ldc{slot}", waits=[cv_marks.get(("cv", ci - 2))])
                mcv = p.op("vector", lambda e, width=width, slot=slot: e.tensor_copy(
                    cvt16[slot][:, 0:width], cvt32[slot][:, 0:width]),
                    waits=[mld, cv_marks.get(("st", ci - 2))])
                cv_marks[("cv", ci)] = mcv
                mst = p.op("scalar", lambda e, dstn=dstn, j=j, width=width, slot=slot: e.dma_start(
                    out=p.dr[dstn][:, width * j:width * j + width],
                    in_=cvt16[slot][:, 0:width]), dma=True, sem=f"stc{slot}",
                    waits=[mcv])
                cv_marks[("st", ci)] = mst
                ci += 1
        for (srcn, dstn, width) in [("wx2", "wx2bs", 5 * 2048),
                                    ("wwb", "wwbs", 1024), ("wgb", "wgbs", 1024)]:
            for c0 in range(0, width, 2048):
                cw = min(2048, width - c0)
                slot = ci % 2
                mld = p.op("sync", lambda e, srcn=srcn, c0=c0, cw=cw, slot=slot: e.dma_start(
                    out=cvt32[slot][0:64, 0:cw], in_=p.dr[srcn][:, c0:c0 + cw]),
                    dma=True, sem=f"ldc{slot}", waits=[cv_marks.get(("cv", ci - 2))])
                mcv = p.op("vector", lambda e, cw=cw, slot=slot: e.tensor_copy(
                    cvt16[slot][0:64, 0:cw], cvt32[slot][0:64, 0:cw]),
                    waits=[mld, cv_marks.get(("st", ci - 2))])
                cv_marks[("cv", ci)] = mcv
                mst = p.op("scalar", lambda e, dstn=dstn, c0=c0, cw=cw, slot=slot: e.dma_start(
                    out=p.dr[dstn][:, c0:c0 + cw], in_=cvt16[slot][0:64, 0:cw]),
                    dma=True, sem=f"stc{slot}", waits=[mcv])
                cv_marks[("st", ci)] = mst
                ci += 1
        p.conv_fence = p.fence("scalar", ["stc0", "stc1"])
        scb = p.sbuf("scbuf", [64, 2048], F32, stack=st0)
        scpre = {}
        for (mat, mult) in [("g", SCF), ("u", SCF), ("d", SCF * 0.5)]:
            mld = p.op("sync", lambda e, mat=mat: e.dma_start(
                out=scb[:, :], in_=p.dr[f"{mat}sc_my"][:, :]), dma=True,
                sem="ldsc", waits=[scpre.get("last")])
            mmul = p.op("vector", lambda e, mult=mult: e.tensor_scalar_mul(
                scb[:, :], scb[:, :], mult), waits=[mld])
            mst = p.op("scalar", lambda e, mat=mat: e.dma_start(
                out=p.dr[f"{mat}sp"][:, :], in_=scb[:, :]), dma=True,
                sem="stsc", waits=[mmul])
            scpre[mat] = p.fence("scalar", ["stsc"])
            scpre["last"] = scpre[mat]

        pkbuf = [p.sbuf(f"pkbuf{s}", [128, 2048], I32, stack=st0) for s in range(2)]
        scexp = [p.sbuf(f"scexp{s}", [128, 2048], F32, stack=st0) for s in range(2)]
        dqt1 = p.sbuf("dqt1", [128, 2048], I32, stack=st0)
        dqt2 = p.sbuf("dqt2", [128, 2048], I32, stack=st0)
        dqlo = [p.sbuf(f"dqlo{s}", [128, 2048], BF16, stack=st0) for s in range(2)]
        dqhi = [p.sbuf(f"dqhi{s}", [128, 2048], BF16, stack=st0) for s in range(2)]
        p.ag_done = {}
        st_marks = {}
        vec_marks = {}
        for mi, mat in enumerate(["g", "u", "d"]):
            pk, shard, spre = p.dr[f"{mat}pk_my"], p.dr[f"shard_{mat}"], p.dr[f"{mat}sp"]
            for phg in range(8):
                ph = phg
                gi = mi * 8 + phg
                slot = gi % 2
                w_ld = [scpre[mat], vec_marks.get(gi - 2)]
                mpk = p.op("sync", lambda e, pk=pk, ph=ph, slot=slot: e.dma_start(
                    out=pkbuf[slot][:, :], in_=pk[128 * ph:128 * ph + 128, :]),
                    dma=True, sem=f"ldpk{slot}", waits=w_ld)
                msc = None
                for j in range(8):
                    msc = p.op("sync", lambda e, spre=spre, ph=ph, j=j, slot=slot: e.dma_start(
                        out=scexp[slot][16 * j:16 * j + 16, :],
                        in_=spre[8 * ph + j:8 * ph + j + 1, :].broadcast_to([16, 2048])),
                        dma=True, sem=f"ldse{slot}", waits=w_ld if j == 0 else ())
                wv = [mpk, msc, st_marks.get(gi - 2)]
                p.op("vector", lambda e, slot=slot: e.tensor_scalar(
                    dqt1[:, :], pkbuf[slot][:, :], 8, 28,
                    ALU.bitwise_and, ALU.logical_shift_left), waits=wv)
                p.op("vector", lambda e, slot=slot: e.tensor_scalar(
                    dqt2[:, :], pkbuf[slot][:, :], 7, 22,
                    ALU.bitwise_and, ALU.logical_shift_left))
                p.op("vector", lambda e: e.tensor_tensor(
                    dqt1[:, :], dqt1[:, :], dqt2[:, :], ALU.bitwise_or))
                mlo = p.op("vector", lambda e, slot=slot: e.tensor_tensor(
                    dqlo[slot][:, :], dqt1[:, :].bitcast(F32), scexp[slot][:, :],
                    ALU.mult))
                p.op("vector", lambda e, slot=slot: e.tensor_scalar(
                    dqt1[:, :], pkbuf[slot][:, :], 128, 24,
                    ALU.bitwise_and, ALU.logical_shift_left))
                p.op("vector", lambda e, slot=slot: e.tensor_scalar(
                    dqt2[:, :], pkbuf[slot][:, :], 112, 18,
                    ALU.bitwise_and, ALU.logical_shift_left))
                p.op("vector", lambda e: e.tensor_tensor(
                    dqt1[:, :], dqt1[:, :], dqt2[:, :], ALU.bitwise_or))
                mhi = p.op("vector", lambda e, slot=slot: e.tensor_tensor(
                    dqhi[slot][:, :], dqt1[:, :].bitcast(F32), scexp[slot][:, :],
                    ALU.mult))
                vec_marks[gi] = mhi
                if mat in ("g", "u"):
                    p.op("scalar", lambda e, shard=shard, ph=ph, slot=slot: e.dma_start(
                        out=shard[:, 128 * ph:128 * ph + 128]
                            .rearrange("(a q) w -> q a w", q=128),
                        in_=dqlo[slot][:, :].rearrange("q (a w) -> q a w", w=128)),
                        dma=True, sem=f"stq{slot}", waits=[mlo, mhi])
                    st_marks[gi] = p.op("scalar",
                        lambda e, shard=shard, ph=ph, slot=slot: e.dma_start(
                            out=shard[:, 128 * (8 + ph):128 * (8 + ph) + 128]
                                .rearrange("(a q) w -> q a w", q=128),
                            in_=dqhi[slot][:, :].rearrange("q (a w) -> q a w", w=128)),
                        dma=True, sem=f"stq{slot}")
                else:
                    p.op("scalar", lambda e, shard=shard, ph=ph, slot=slot: e.dma_start(
                        out=shard[128 * ph:128 * ph + 128, :], in_=dqlo[slot][:, :]),
                        dma=True, sem=f"stq{slot}", waits=[mlo, mhi])
                    st_marks[gi] = p.op("scalar",
                        lambda e, shard=shard, ph=ph, slot=slot: e.dma_start(
                            out=shard[128 * (8 + ph):128 * (8 + ph) + 128, :],
                            in_=dqhi[slot][:, :]), dma=True, sem=f"stq{slot}")
            stfence = p.fence("scalar", ["stq0", "stq1"])
            p.ag_done[mat] = p.op("gpsimd",
                lambda e, mat=mat, shard=shard: e.collective_compute(
                    "AllGather", ALU.bypass, replica_groups=[list(range(NCORES))],
                    ins=[shard[:, :].opt()], outs=[p.dr[f"wscr_{mat}"][:, :].opt()]),
                collective=True, waits=[stfence])
        p.flush("p0")



def build_ln(p, src, t0, xn_dst, psum_free=None, xtile_free=None):
    if xtile_free is None:
        xtile_free = []
    elif isinstance(xtile_free, tuple) and len(xtile_free) == 2 and \
            isinstance(xtile_free[0], str):
        xtile_free = [xtile_free]
    else:
        xtile_free = list(xtile_free)
    xt = [p.sb[f"xtile{i}"] for i in range(4)]
    xq = [p.sb[f"xsq{i}"] for i in range(4)]
    sq_marks, ld_marks = {}, {}
    summ = []
    for kt in range(NKT):
        slot = kt % 4
        w = [sq_marks.get(kt - 4)] + (list(xtile_free) if kt < 4 else [])
        if kt >= 4:
            w.append(summ[2 * (kt - 4) + 1])
        ld_marks[kt] = p.op("sync", lambda e, kt=kt, slot=slot: e.dma_start(
            out=xt[slot][:, :], in_=src[128 * kt:128 * kt + 128, t0:t0 + 1024]),
            dma=True, sem=f"lnl{slot % 2}", waits=w)
        mcb = p.op("scalar", lambda e, slot=slot: e.activation(
            p.sb[f"xb{slot}"][:, :], xt[slot][:, :], AF.Identity),
            waits=[ld_marks[kt]])
        sq_marks[kt] = p.op("scalar", lambda e, slot=slot: e.activation(
            xq[slot][:, :], xt[slot][:, :], AF.Square), waits=[ld_marks[kt]])
        for hh in range(2):
            p.op("tensor", lambda e, slot=slot, hh=hh, kt=kt: e.matmul(
                p.ps[0 + hh][:, :], p.sb["ones128"][:, :],
                p.sb[f"xb{slot}"][:, 512 * hh:512 * hh + 512],
                start=(kt == 0), stop=(kt == NKT - 1)),
                waits=[mcb] + ([psum_free] if kt == 0 and hh == 0 else []),
                quiet=(kt != NKT - 1))
            m = p.op("tensor", lambda e, slot=slot, hh=hh, kt=kt: e.matmul(
                p.ps[2 + hh][:, :], p.sb["ones128"][:, :],
                xq[slot][:, 512 * hh:512 * hh + 512],
                start=(kt == 0), stop=(kt == NKT - 1)), waits=[sq_marks[kt]],
                quiet=False)
            summ.append(m)
    lnm, lnr = p.sb["lnm"], p.sb["lnr"]
    lsc, lsc2 = p.sb["lnsc"], p.sb["lnr"]
    mmean = {}
    for hh in range(2):
        mmean[hh] = p.op("vector", lambda e, hh=hh: e.tensor_scalar_mul(
            lnm[:, 512 * hh:512 * hh + 512], p.ps[0 + hh][:, :], 1.0 / H),
            waits=[summ[-1]])
    msq2 = p.op("vector", lambda e: e.tensor_tensor(
        lsc2[:, :], lnm[:, :], lnm[:, :], ALU.mult), waits=[mmean[1]])
    mv2 = {}
    for hh in range(2):
        mv2[hh] = p.op("vector", lambda e, hh=hh: e.tensor_scalar_mul(
            lsc[:, 512 * hh:512 * hh + 512], p.ps[2 + hh][:, :], 1.0 / H),
            waits=[msq2])
    msub = p.op("vector", lambda e: e.tensor_tensor(
        lsc[:, :], lsc[:, :], lsc2[:, :], ALU.subtract), waits=[mv2[1]])
    mln = p.op("scalar", lambda e: e.activation(
        lsc[:, :], lsc[:, :], AF.Ln, bias=p.sb["eps128"][:, 0:1]),
        waits=[msub])
    mrs = p.op("scalar", lambda e: e.activation(
        lnr[:, :], lsc[:, :], AF.Exp, scale=-0.5), waits=[mln])
    xn_marks = {}
    for kt in range(NKT):
        slot = kt % 4
        mld = p.op("sync", lambda e, kt=kt, slot=slot: e.dma_start(
            out=xt[slot][:, :], in_=src[128 * kt:128 * kt + 128, t0:t0 + 1024]),
            dma=True, sem=f"lnl{slot % 2}",
            waits=[xn_marks.get(kt - 4), summ[2 * kt + 1]])
        m1 = p.op("vector", lambda e, slot=slot: e.tensor_tensor(
            xq[slot][:, :], xt[slot][:, :], lnm[:, :],
            ALU.subtract), waits=[mld, mrs])
        m = p.op("vector", lambda e, slot=slot, kt=kt: e.tensor_tensor(
            xn_dst[:, 1024 * kt:1024 * kt + 1024],
            xq[slot][:, :], lnr[:, :], ALU.mult))
        xn_marks[kt] = m
    xn_marks["last"] = m
    return xn_marks



def build_ffn(p, cfg, src, dst, tag):
    NG = cfg.get("ffn_group", 16)
    ngroups = cfg.get("ngroups_cap", NNT // NG)
    store_marks = []
    psum_free = getattr(p, "ffn_psum_free", None)
    xtile_free = getattr(p, "ffn_xtile_free", None)
    for strip in range(2):
        t0 = 1024 * strip
        xn_marks = build_ln(p, src, t0, p.sb["xn"], psum_free, xtile_free)
        marks = {}
        wg_ld, wu_ld, wd_ld = {}, {}, {}

        def sched_gu_load(nt):
            slot = nt % 2
            for mat, store in (("g", wg_ld), ("u", wu_ld)):
                w = [marks.get(("gufree", nt - 2))]
                if tag == "ffn1" and strip == 0 and nt < 2:
                    w.append(p.ag_done[mat])
                if nt < 2 and getattr(p, "ffn_wg_free", None):
                    w.append(p.ffn_wg_free)
                if cfg.get("dummy_gu_loads"):
                    store[nt] = p.op("sync", lambda e, mat=mat, nt=nt, slot=slot: e.dma_start(
                        out=p.sb[f"wbuf{mat}{slot}"][:, 0:64],
                        in_=p.dr[f"wscr_{mat}"][128 * nt:128 * nt + 128, 0:64]),
                        dma=True, sem=f"ld{mat}{slot}", waits=w)
                    continue
                store[nt] = p.op("sync", lambda e, mat=mat, nt=nt, slot=slot: e.dma_start(
                    out=p.sb[f"wbuf{mat}{slot}"][:, :],
                    in_=p.dr[f"wscr_{mat}"][128 * nt:128 * nt + 128, :]),
                    dma=True, sem=f"ld{mat}{slot}", waits=w)

        def sched_d_load(idx):
            g2, ht = idx // NHT, idx % NHT
            slot = idx % 2
            w = [marks.get(("dmm", idx - 2, 1))]
            if tag == "ffn1" and strip == 0 and idx < 2:
                w.append(p.ag_done["d"])
            if idx < 2 and getattr(p, "ffn_wd_free", None):
                w.append(p.ffn_wd_free)
            r0 = down_scratch_row(g2 * NG)
            r8 = down_scratch_row(g2 * NG + 8)
            if cfg.get("dummy_down_loads"):
                p.op("sync", lambda e, r0=r0, slot=slot: e.dma_start(
                    out=p.sb[f"wbufd{slot}"][:, 0:1024],
                    in_=p.dr["wscr_d"][r0:r0 + 128, 0:1024]),
                    dma=True, sem=f"ldd{slot}", waits=w)
                wd_ld[idx] = p.op("sync", lambda e, r8=r8, slot=slot: e.dma_start(
                    out=p.sb[f"wbufd{slot}"][:, 1024:2048],
                    in_=p.dr["wscr_d"][r8:r8 + 128, 0:1024]),
                    dma=True, sem=f"ldd{slot}")
            else:
                p.op("sync", lambda e, r0=r0, slot=slot, ht=ht: e.dma_start(
                    out=p.sb[f"wbufd{slot}"][:, 0:1024].rearrange("q (a w) -> q a w", a=8),
                    in_=p.dr["wscr_d"][r0:r0 + 1024, 128 * ht:128 * ht + 128]
                        .rearrange("(a q) w -> q a w", q=128)),
                    dma=True, sem=f"ldd{slot}", waits=w)
                wd_ld[idx] = p.op("sync", lambda e, r8=r8, slot=slot, ht=ht: e.dma_start(
                    out=p.sb[f"wbufd{slot}"][:, 1024:2048].rearrange("q (a w) -> q a w", a=8),
                    in_=p.dr["wscr_d"][r8:r8 + 1024, 128 * ht:128 * ht + 128]
                        .rearrange("(a q) w -> q a w", q=128)), dma=True, sem=f"ldd{slot}")

        sched_gu_load(0)
        sched_gu_load(1)
        if not cfg.get("skip_down"):
            sched_d_load(0)
            sched_d_load(1)
        for g in range(ngroups):
            for j in range(NG):
                nt = g * NG + j
                slotw = nt % 2
                for half in range(2):
                    ih = 2 * nt + half
                    par = ih % 2
                    w = [wg_ld[nt], xn_marks["last"], marks.get(("gelufree", ih - 2))]
                    mm = None
                    for kt in range(NKT):
                        mm = p.op("tensor", lambda e, slotw=slotw, kt=kt, half=half, par=par: e.matmul(
                            p.ps[0 + par][:, :],
                            p.sb[f"wbufg{slotw}"][:, 128 * kt:128 * kt + 128],
                            p.sb["xn"][:, 1024 * kt + 512 * half:1024 * kt + 512 * half + 512],
                            start=(kt == 0), stop=(kt == NKT - 1)),
                            waits=w if kt == 0 else (), quiet=(kt != NKT - 1))
                    marks[("g", ih)] = mm
                    w2 = [wu_ld[nt], marks.get(("interdone", ih - 2))]
                    for kt in range(NKT):
                        mm = p.op("tensor", lambda e, slotw=slotw, kt=kt, half=half, par=par: e.matmul(
                            p.ps[2 + par][:, :],
                            p.sb[f"wbufu{slotw}"][:, 128 * kt:128 * kt + 128],
                            p.sb["xn"][:, 1024 * kt + 512 * half:1024 * kt + 512 * half + 512],
                            start=(kt == 0), stop=(kt == NKT - 1)),
                            waits=w2 if kt == 0 else (), quiet=(kt != NKT - 1))
                    marks[("u", ih)] = mm
                    if half == 1:
                        marks[("gufree", nt)] = mm
                        if nt + 2 < NNT:
                            sched_gu_load(nt + 2)
                    mcp = p.op("scalar", lambda e, par=par: e.activation(
                        p.sb[f"gps{par}"][:, :], p.ps[0 + par][:, :],
                        AF.Identity), waits=[marks[("g", ih)]])
                    marks[("gelufree", ih)] = mcp
                    msq = p.op("vector", lambda e, par=par: e.tensor_tensor(
                        p.sb[f"gsq{par}"][:, :], p.sb[f"gps{par}"][:, :],
                        p.sb[f"gps{par}"][:, :], ALU.mult), waits=[mcp])
                    mm2 = p.op("vector", lambda e, par=par: e.tensor_scalar(
                        p.sb[f"gsq{par}"][:, :], p.sb[f"gsq{par}"][:, :],
                        GELU_C2 / GELU_C1, 1.0, ALU.mult, ALU.add),
                        waits=[msq])
                    mm3 = p.op("vector", lambda e, par=par: e.tensor_tensor(
                        p.sb[f"gsq{par}"][:, :], p.sb[f"gsq{par}"][:, :],
                        p.sb[f"gps{par}"][:, :], ALU.mult))
                    msig = p.op("scalar", lambda e, par=par: e.activation(
                        p.sb[f"gsig{par}"][:, :], p.sb[f"gsq{par}"][:, :],
                        AF.Sigmoid, scale=GELU_C1), waits=[mm3])
                    mgs = p.op("vector", lambda e, par=par: e.tensor_tensor(
                        p.sb[f"gsig{par}"][:, :], p.sb[f"gps{par}"][:, :],
                        p.sb[f"gsig{par}"][:, :], ALU.mult), waits=[msig])
                    mint = p.op("vector", lambda e, par=par, j=j, half=half: e.tensor_tensor(
                        p.sb["inter"][:, 1024 * j + 512 * half:1024 * j + 512 * half + 512],
                        p.sb[f"gsig{par}"][:, :], p.ps[2 + par][:, :], ALU.mult),
                        waits=[marks[("u", ih)], mgs])
                    marks[("interdone", ih)] = mint
            if cfg.get("skip_down"):
                for ht in range(NHT):
                    marks[("daccfin", ht)] = marks[("interdone", 2 * (g * NG + NG - 1) + 1)]
                continue
            for ht in range(NHT):
                idx = g * NHT + ht
                slotd = idx % 2
                for half in range(2):
                    par = half
                    w = [wd_ld[idx], marks[("interdone", 2 * (g * NG + NG - 1) + 1)],
                         marks.get(("dfree", 2 * idx + half - 2))]
                    mm = None
                    for j in range(NG):
                        mm = p.op("tensor", lambda e, slotd=slotd, j=j, half=half, par=par: e.matmul(
                            p.ps[4 + par][:, :],
                            p.sb[f"wbufd{slotd}"][:, 128 * j:128 * j + 128],
                            p.sb["inter"][:, 1024 * j + 512 * half:1024 * j + 512 * half + 512],
                            start=(j == 0), stop=(j == NG - 1)),
                            waits=w if j == 0 else (), quiet=(j != NG - 1))
                    marks[("dmm", idx, half)] = mm
                    if half == 1 and idx + 2 < ngroups * NHT and not cfg.get("skip_down"):
                        sched_d_load(idx + 2)
                    dst_acc = p.sb["dacc"][:, 1024 * ht + 512 * half:1024 * ht + 512 * half + 512]
                    if g == 0:
                        mev = p.op("vector", lambda e, dst_acc=dst_acc, par=par: e.tensor_copy(
                            dst_acc, p.ps[4 + par][:, :]), waits=[mm])
                    else:
                        mev = p.op("vector", lambda e, dst_acc=dst_acc, par=par: e.tensor_tensor(
                            dst_acc, dst_acc, p.ps[4 + par][:, :], ALU.add),
                            waits=[mm])
                    marks[("dfree", 2 * idx + half)] = mev
                    if g == ngroups - 1:
                        marks[("daccfin", ht)] = mev
        finm = {}
        stm = {}
        for ht in range(NHT):
            slot = ht % 2
            mld = p.op("scalar", lambda e, ht=ht, slot=slot, t0=t0: e.dma_start(
                out=p.sb[f"fint{slot}"][:, :],
                in_=src[128 * ht:128 * ht + 128, t0:t0 + 1024]), dma=True,
                sem=f"ldx{slot}",
                waits=[finm.get(ht - 2), xn_marks["last"] if ht < 2 else None])
            fb = ["lnm", "lnsc"][slot]
            mfin = p.op("vector", lambda e, ht=ht, slot=slot, fb=fb: e.tensor_tensor(
                p.sb[fb][:, :], p.sb[f"fint{slot}"][:, :],
                p.sb["dacc"][:, 1024 * ht:1024 * ht + 1024], ALU.add),
                waits=[mld, marks[("daccfin", ht)], stm.get(ht - 2)])
            finm[ht] = mfin
            stm[ht] = p.op("gpsimd", lambda e, ht=ht, slot=slot, t0=t0, fb=fb: e.dma_start(
                out=dst[128 * ht:128 * ht + 128, t0:t0 + 1024],
                in_=p.sb[fb][:, :]), dma=True, sem=f"stx{slot}",
                waits=[mfin])
            store_marks.append(stm[ht])
            if tag == "ffn1" and strip == 1:
                p.op("vector", lambda e, ht=ht, slot=slot, fb=fb: e.tensor_copy(
                    p.sb["agcol"][:, ht:ht + 1],
                    p.sb[fb][:, 1023:1024]), waits=[stm[ht]])
        p.flush(f"{tag}s{strip}")
        psum_free = marks[("interdone", 2 * (g * NG + NG - 1) + 1)]
        xtile_free = [xn_marks[kt] for kt in range(12, 16)]
        p.ffn_wg_free = marks[("u", 2 * (ngroups * NG - 1) + 1)]
        p.ffn_wu_free = p.ffn_wg_free
        p.ffn_wd_free = marks.get(("dmm", ngroups * NHT - 1, 1))
    p.ffn_psum_free = psum_free
    p.ffn_xtile_free = xtile_free
    return store_marks


def alloc_ffn_bufs(p, stack):
    p._uniq = getattr(p, "_uniq", 0) + 1
    p.ps = [stack.enter_context(p.nc.psum_tensor(f"pb{i}_{p._uniq}", [128, 512], F32))
            for i in range(8)]
    p.sbuf("xn", [128, NKT * 1024], BF16, stack=stack)
    p.sbuf("inter", [128, 16 * 1024], BF16, stack=stack)
    p.sbuf("dacc", [128, NHT * 1024], BF16, stack=stack)
    p.sbuf("fint0", [128, 1024], F32, stack=stack)
    p.sbuf("fint1", [128, 1024], F32, stack=stack)
    p.sbuf("lnm", [128, 1024], F32, stack=stack)
    p.sbuf("lnr", [128, 1024], F32, stack=stack)
    p.sbuf("lnsc", [128, 1024], F32, stack=stack)
    for s in range(4):
        p.sbuf(f"xtile{s}", [128, 1024], F32, stack=stack)
        p.sbuf(f"xsq{s}", [128, 1024], BF16, stack=stack)
        p.sbuf(f"xb{s}", [128, 1024], BF16, stack=stack)
    for s in range(2):
        p.sbuf(f"wbufg{s}", [128, 2048], BF16, stack=stack)
        p.sbuf(f"wbufu{s}", [128, 2048], BF16, stack=stack)
        p.sbuf(f"wbufd{s}", [128, 2048], BF16, stack=stack)
        p.sbuf(f"gsq{s}", [128, 512], F32, stack=stack)
        p.sbuf(f"gsig{s}", [128, 512], F32, stack=stack)
        p.sbuf(f"gps{s}", [128, 512], F32, stack=stack)



def build_program(cfg):
    nc = bass.Bass("TRN2", target_bir_lowering=False)
    es = contextlib.ExitStack()
    with es:
        p = Prog(nc, es)
        alloc_dram(p, cfg)
        alloc_persist(p)
        build_p0(p, cfg)
        p.barrier()
        upto = cfg.get("upto", "full")
        if upto == "p0":
            with contextlib.ExitStack() as dbgst:
                db = p.sbuf("dbgb", [128, 2048], BF16, stack=dbgst)
                df = p.sbuf("dbgf", [128, 2048], F32, stack=dbgst)
                last = None
                for ht in range(16):
                    mld = p.op("sync", lambda e, ht=ht: e.dma_start(
                        out=db[:, :], in_=p.dr["wscr_g"][128 * ht:128 * ht + 128, :]),
                        dma=True, sem="ldc0", waits=[last])
                    mcv = p.op("vector", lambda e: e.tensor_copy(df[:, :], db[:, :]),
                               waits=[mld])
                    last = p.op("gpsimd", lambda e, ht=ht: e.dma_start(
                        out=p.dr["out"][128 * ht:128 * ht + 128, :], in_=df[:, :]),
                        dma=True, sem="stx0", waits=[mcv])
                p.op("gpsimd", lambda e: e.nop(), waits=[last])
            p.emit_all()
            return nc, p
        if upto == "gate1":
            with contextlib.ExitStack() as ffn_stack:
                alloc_ffn_bufs(p, ffn_stack)
                xn_marks = build_ln(p, p.dr["xin"], 0, p.sb["xn"], None, None)
                mwg = p.op("sync", lambda e: e.dma_start(
                    out=p.sb["wbufg0"][:, :], in_=p.dr["wscr_g"][0:128, :]),
                    dma=True, sem="ldg0", waits=[p.ag_done["g"]])
                mwu = p.op("sync", lambda e: e.dma_start(
                    out=p.sb["wbufu0"][:, :], in_=p.dr["wscr_u"][0:128, :]),
                    dma=True, sem="ldu0", waits=[p.ag_done["u"]])
                outm = []
                for half in range(2):
                    for kt in range(NKT):
                        mg = p.op("tensor", lambda e, kt=kt, half=half: e.matmul(
                            p.ps[0][:, :], p.sb["wbufg0"][:, 128 * kt:128 * kt + 128],
                            p.sb["xn"][:, 1024 * kt + 512 * half:1024 * kt + 512 * half + 512],
                            start=(kt == 0), stop=(kt == NKT - 1)),
                            waits=[mwg, xn_marks["last"], outm[-1] if outm else None] if kt == 0 else ())
                    for kt in range(NKT):
                        mu = p.op("tensor", lambda e, kt=kt, half=half: e.matmul(
                            p.ps[2][:, :], p.sb["wbufu0"][:, 128 * kt:128 * kt + 128],
                            p.sb["xn"][:, 1024 * kt + 512 * half:1024 * kt + 512 * half + 512],
                            start=(kt == 0), stop=(kt == NKT - 1)),
                            waits=[mwu] if kt == 0 else ())
                    msq = p.op("scalar", lambda e: e.activation(
                        p.sb["gsq0"][:, :], p.ps[0][:, :], AF.Square, scale=SQRT_C2),
                        waits=[mg])
                    mm2 = p.op("vector", lambda e: e.scalar_tensor_tensor(
                        p.sb["gsq0"][:, :], p.sb["gsq0"][:, :], GELU_C1,
                        p.ps[0][:, :], ALU.add, ALU.mult), waits=[msq])
                    msig = p.op("scalar", lambda e: e.activation(
                        p.sb["gsig0"][:, :], p.sb["gsq0"][:, :], AF.Sigmoid), waits=[mm2])
                    mgs = p.op("vector", lambda e: e.tensor_tensor(
                        p.sb["gsig0"][:, :], p.ps[0][:, :], p.sb["gsig0"][:, :],
                        ALU.mult), waits=[msig, mu])
                    mint = p.op("vector", lambda e, half=half: e.tensor_tensor(
                        p.sb["inter"][:, 512 * half:512 * half + 512],
                        p.sb["gsig0"][:, :], p.ps[2][:, :], ALU.mult))
                    mcv = p.op("vector", lambda e, half=half: e.tensor_copy(
                        p.sb["dacc"][:, 512 * half:512 * half + 512],
                        p.sb["inter"][:, 512 * half:512 * half + 512]))
                    outm.append(p.op("gpsimd", lambda e, half=half: e.dma_start(
                        out=p.dr["out"][0:128, 512 * half:512 * half + 512],
                        in_=p.sb["dacc"][:, 512 * half:512 * half + 512]),
                        dma=True, sem="stx0", waits=[mcv]))
                p.op("gpsimd", lambda e: e.nop(), waits=[outm[-1]])
            p.emit_all()
            return nc, p
        if upto == "ln":
            with contextlib.ExitStack() as ffn_stack:
                alloc_ffn_bufs(p, ffn_stack)
                xn_marks = build_ln(p, p.dr["xin"], 0, p.sb["xn"], None, None)
                last = None
                for kt in range(16):
                    mcv = p.op("vector", lambda e, kt=kt: e.tensor_copy(
                        p.sb["dacc"][:, 1024 * kt:1024 * kt + 1024],
                        p.sb["xn"][:, 1024 * kt:1024 * kt + 1024]),
                        waits=[xn_marks[kt], last])
                    last = p.op("gpsimd", lambda e, kt=kt: e.dma_start(
                        out=p.dr["out"][128 * kt:128 * kt + 128, 0:1024],
                        in_=p.sb["dacc"][:, 1024 * kt:1024 * kt + 1024]),
                        dma=True, sem="stx0", waits=[mcv])
                p.op("gpsimd", lambda e: e.nop(), waits=[last])
            p.emit_all()
            return nc, p
        with contextlib.ExitStack() as ffn_stack:
            alloc_ffn_bufs(p, ffn_stack)
            dst1 = p.dr["out"] if upto == "x1" else p.dr["x1s"]
            p.ffn_xtile_free = p.ag_done["d"]
            st1 = build_ffn(p, cfg, p.dr["xin"], dst1, "ffn1")
            if upto == "x1":
                p.op("gpsimd", lambda e: e.nop(), waits=[st1[-1]])
                p.emit_all()
                return nc, p
        if upto == "cpy":
            m = p.op("gpsimd", lambda e: e.dma_start(
                out=p.dr["out"][:, :], in_=p.dr["x1s"][:, :]), dma=True, sem="stx0")
            p.op("gpsimd", lambda e: e.nop(), waits=[m])
            p.emit_all()
            return nc, p
        p.barrier()
        p.flush("p0ffn1")
        build_attn(p, cfg)
        p.barrier()
        if cfg.get("attn_stop"):
            m = p.op("gpsimd", lambda e: e.dma_start(
                out=p.dr["out"][:, :], in_=p.dr["x1s"][:, :]), dma=True, sem="stx0")
            p.op("gpsimd", lambda e: e.nop(), waits=[m])
            p.emit_all()
            return nc, p
        if cfg.get("dump_dram"):
            nm = cfg["dump_dram"]
            if nm == "ag2_out":
                srews = p.dr[nm].rearrange("(a b) c -> a (b c)", b=16)
                emitf = lambda e: e.dma_start(out=p.dr["out"][0:520, 0:1024], in_=srews)
            else:
                s32 = p.dr[nm].bitcast(F32)
                r, cdim = s32.shape
                emitf = lambda e, s32=s32, r=r, cdim=cdim: e.dma_start(
                    out=p.dr["out"][0:r, 0:cdim], in_=s32)
            m = p.op("gpsimd", emitf, dma=True, sem="stx0", waits=[p.x2_done])
            p.op("gpsimd", lambda e: e.nop(), waits=[m])
            p.emit_all()
            return nc, p
        if upto == "x2":
            m = p.op("gpsimd", lambda e: e.dma_start(
                out=p.dr["out"][:, :], in_=p.dr["x2s"][:, :]), dma=True, sem="stx0")
            p.op("gpsimd", lambda e: e.nop(), waits=[m])
            p.emit_all()
            return nc, p
        p.barrier()
        p.flush("sweep2")
        with contextlib.ExitStack() as ffn_stack:
            alloc_ffn_bufs(p, ffn_stack)
            st3 = build_ffn(p, cfg, p.dr["x2s"], p.dr["out"], "ffn2")
            p.op("gpsimd", lambda e: e.nop(), waits=[st3[-1], st3[-2]])
        p.emit_all()
    return nc, p



def alloc_attn_psum(p, stack):
    p.aps = {}
    p._uniq = getattr(p, "_uniq", 0) + 1
    for n in ["pA", "pB", "pC", "pD", "pE", "pF", "pG"]:
        p.aps[n] = stack.enter_context(
            p.nc.psum_tensor(f"{n}_{p._uniq}", [128, 512], F32))
    p.aps["pT"] = stack.enter_context(
        p.nc.psum_tensor(f"pT_{p._uniq}", [128, 1024], BF16))


def alloc_attn_outer(p, stack):
    for q in range(8):
        p.sbuf(f"slocA{q}", [64, 64], F32, stack=stack)
        p.sbuf(f"slocB{q}", [64, 64], F32, stack=stack)
        p.sbuf(f"sloc{q}", [128, 64], F32, stack=stack)
        p.sbuf(f"sinA{q}", [64, 64], BF16, stack=stack)
        p.sbuf(f"sinB{q}", [64, 64], BF16, stack=stack)
    p.sbuf("ltot", [128, 8], F32, stack=stack)
    p.sbuf("colrelstash", [128, 256], F32, stack=stack)
    p.sbuf("colrelstashB", [64, 256], F32, stack=stack)
    p.sbuf("xnprev", [128, 16], F32, stack=stack)


def alloc_attn_s1(p, stack):
    p.sbuf("xn2", [128, 16 * 513], BF16, stack=stack)
    p.sbuf("delta", [128, 16 * 512], BF16, stack=stack)
    p.sbuf("xi", [128, 16 * 512], BF16, stack=stack)
    for s in range(2):
        p.sbuf(f"whb{s}", [128, 4096], BF16, stack=stack)
        p.sbuf(f"xtile{s}", [128, 512], F32, stack=stack)
        p.sbuf(f"xsq{s}", [128, 512], F32, stack=stack)
        p.sbuf(f"etmp{s}", [128, 512], F32, stack=stack)
        p.sbuf(f"swp{s}", [64, 2048], BF16, stack=stack)
    for i in range(5):
        p.sbuf(f"lr{i}", [64, 512], BF16, stack=stack)
    p.sbuf("twa", [64, 512], BF16, stack=stack)
    p.sbuf("v64", [64, 8 * 1024], BF16, stack=stack)
    p.sbuf("Pb", [128, 8 * 512], BF16, stack=stack)
    p.sbuf("CG", [128, 8 * 512], F32, stack=stack)
    p.sbuf("rb", [128, 8 * 512], BF16, stack=stack)
    p.sbuf("kb", [128, 8 * 512], BF16, stack=stack)
    p.sbuf("rt", [128, 8 * 512], BF16, stack=stack)
    p.sbuf("ktl", [128, 8 * 512], BF16, stack=stack)
    p.sbuf("ktmp", [128, 8 * 64], BF16, stack=stack)
    p.sbuf("bktmpA", [64, 8 * 64], BF16, stack=stack)
    p.sbuf("bktmpB", [64, 8 * 64], BF16, stack=stack)
    p.sbuf("slocbA", [64, 8 * 64], BF16, stack=stack)
    p.sbuf("slocbB", [64, 8 * 64], BF16, stack=stack)
    p.sbuf("khT", [64, 8 * 128], BF16, stack=stack)
    p.sbuf("msb", [64, 8 * 128], BF16, stack=stack)
    p.sbuf("ov0", [64, 128], BF16, stack=stack)
    p.sbuf("ov1", [64, 128], BF16, stack=stack)
    p.sbuf("mcols", [128, 64], F32, stack=stack)
    p.sbuf("ecols", [128, 64], F32, stack=stack)
    p.sbuf("scols", [128, 64], F32, stack=stack)
    p.sbuf("acols", [128, 64], F32, stack=stack)
    p.sbuf("fcols", [128, 64], F32, stack=stack)
    p.sbuf("kcols", [128, 64], F32, stack=stack)
    p.sbuf("ccols", [128, 64], F32, stack=stack)
    p.sbuf("colsB", [64, 128], F32, stack=stack)
    p.sbuf("zcol", [128, 1], F32, stack=stack)
    p.sbuf("gq0", [128, 512], BF16, stack=stack)
    p.sbuf("mtile0", [128, 512], BF16, stack=stack)
    p.sbuf("wx2t0", [64, 2048], BF16, stack=stack)
    p.sbuf("twb", [64, 1024], BF16, stack=stack)


def build_ln_strip(p, src, t0, width, out_fn, psA, psB, psum_free=None,
                   xtile_free=None, prefix=0):
    xt = [p.sb["xtile0"], p.sb["xtile1"]]
    xq = [p.sb["xsq0"], p.sb["xsq1"]]
    sq_marks, ld_marks = {}, {}
    summ = []
    for kt in range(NKT):
        slot = kt % 2
        w = [sq_marks.get(kt - 2), xtile_free if kt < 2 else None,
             summ[-1] if kt >= 2 else None]
        ld_marks[kt] = p.op("scalar", lambda e, kt=kt, slot=slot: e.dma_start(
            out=xt[slot][:, 0:width], in_=src[128 * kt:128 * kt + 128, t0:t0 + width]),
            dma=True, sem=f"ldx{slot}", waits=w)
        sq_marks[kt] = p.op("scalar", lambda e, slot=slot: e.activation(
            xq[slot][:, 0:width], xt[slot][:, 0:width], AF.Square), waits=[ld_marks[kt]])
        p.op("tensor", lambda e, slot=slot, kt=kt: e.matmul(
            psA[0:1, 0:width], p.sb["ones1"][:, 0:1], xt[slot][:, 0:width],
            start=(kt == 0), stop=(kt == NKT - 1)),
            waits=[ld_marks[kt]] + ([psum_free] if kt == 0 else []))
        m = p.op("tensor", lambda e, slot=slot, kt=kt: e.matmul(
            psB[0:1, 0:width], p.sb["ones1"][:, 0:1], xq[slot][:, 0:width],
            start=(kt == 0), stop=(kt == NKT - 1)), waits=[sq_marks[kt]])
        summ.append(m)
    mrow, vrow, rrow = p.sb["mrow"], p.sb["vrow"], p.sb["rrow"]
    tmp = rrow
    p.op("vector", lambda e: e.tensor_scalar_mul(
        mrow[0:1, 0:width], psA[0:1, 0:width], 1.0 / H), waits=[summ[-1]])
    p.op("vector", lambda e: e.tensor_scalar_mul(
        vrow[0:1, 0:width], psB[0:1, 0:width], 1.0 / H))
    p.op("vector", lambda e: e.tensor_tensor(
        tmp[0:1, 0:width], mrow[0:1, 0:width], mrow[0:1, 0:width], ALU.mult))
    mvar = p.op("vector", lambda e: e.tensor_tensor(
        vrow[0:1, 0:width], vrow[0:1, 0:width], tmp[0:1, 0:width], ALU.subtract))
    mln = p.op("scalar", lambda e: e.activation(
        tmp[0:1, 0:width], vrow[0:1, 0:width], AF.Ln,
        bias=p.sb["epscol"][0:1, 0:1]), waits=[mvar])
    mrs = p.op("scalar", lambda e: e.activation(
        vrow[0:1, 0:width], tmp[0:1, 0:width], AF.Exp, scale=-0.5), waits=[mln])
    p.op("tensor", lambda e: e.matmul(
        psA[:, 0:width], p.sb["onesr"][0:1, :], mrow[0:1, 0:width],
        start=True, stop=True), waits=[mrs])
    bcl = p.op("tensor", lambda e: e.matmul(
        psB[:, 0:width], p.sb["onesr"][0:1, :], vrow[0:1, 0:width],
        start=True, stop=True))
    xn_marks = {}
    m = None
    for kt in range(NKT):
        slot = kt % 2
        mld = p.op("scalar", lambda e, kt=kt, slot=slot: e.dma_start(
            out=xt[slot][:, 0:width], in_=src[128 * kt:128 * kt + 128, t0:t0 + width]),
            dma=True, sem=f"ldx{slot}", waits=[xn_marks.get(kt - 2),
                                               summ[-1] if kt < 2 else None])
        p.op("vector", lambda e, slot=slot: e.tensor_tensor(
            xq[slot][:, 0:width], xt[slot][:, 0:width], psA[:, 0:width],
            ALU.subtract), waits=[mld, bcl, sq_marks[NKT - 1] if kt < 2 else None])
        m = out_fn(kt, slot)
        xn_marks[kt] = m
    xn_marks["last"] = m
    return xn_marks


def build_attn(p, cfg):
    nc = p.nc
    NST = 4
    SW = 512

    with contextlib.ExitStack() as outer:
        alloc_attn_outer(p, outer)
        alloc_attn_psum(p, outer)
        A = p.aps
        for q in range(8):
            p.op("vector", lambda e, q=q: e.memset(p.sb[f"slocA{q}"][:, :], 0.0))
            p.op("vector", lambda e, q=q: e.memset(p.sb[f"slocB{q}"][:, :], 0.0))
        p.op("vector", lambda e: e.memset(p.sb["ltot"][:, :], 0.0))

        magi = p.op("gpsimd", lambda e: e.dma_start(
            out=p.dr["ag1_in"][:, :], in_=p.sb["agcol"][:, :]), dma=True, sem="stag")
        mag1 = p.op("gpsimd", lambda e: e.collective_compute(
            "AllGather", ALU.bypass, replica_groups=[list(range(NCORES))],
            ins=[p.dr["ag1_in"][:, :].opt()], outs=[p.dr["ag1_out"][:, :].opt()]),
            collective=True, waits=[magi])

        with contextlib.ExitStack() as s1:
            alloc_attn_s1(p, s1)
            p.op("vector", lambda e: e.memset(p.sb["zcol"][:, :], 0.0))
            p.op("vector", lambda e: e.memset(p.sb["slocbA"][:, :], 0.0))
            p.op("vector", lambda e: e.memset(p.sb["slocbB"][:, :], 0.0))

            xpt = [p.sb["xtile0"], p.sb["xtile1"]]
            mxp = p.op("vector", lambda e: e.memset(p.sb["xnprev"][:, :], 0.0),
                       waits=[mag1])
            for j in range(7):
                mld = p.op("scalar", lambda e, j=j: e.dma_start(
                    out=xpt[j % 2][:, 0:16],
                    in_=p.dr["ag1_out"][128 * j:128 * j + 128, :]),
                    dma=True, sem=f"ldx{j % 2}", waits=[mag1, mxp])
                mxp = p.op("vector", lambda e, j=j: e.scalar_tensor_tensor(
                    p.sb["xnprev"][:, :], xpt[j % 2][:, 0:16],
                    p.sb["cmask"][:, 8 + j:9 + j], p.sb["xnprev"][:, :],
                    ALU.mult, ALU.add), waits=[mld])
            mm1 = p.op("tensor", lambda e: e.matmul(
                A["pA"][0:1, 0:16], p.sb["ones1"][:, 0:1], p.sb["xnprev"][:, :],
                start=True, stop=True), waits=[mxp])
            msq = p.op("scalar", lambda e: e.activation(
                xpt[0][:, 0:16], p.sb["xnprev"][:, :], AF.Square), waits=[mxp])
            mm2 = p.op("tensor", lambda e: e.matmul(
                A["pB"][0:1, 0:16], p.sb["ones1"][:, 0:1], xpt[0][:, 0:16],
                start=True, stop=True), waits=[msq])
            mrd1 = p.op("vector", lambda e: e.reduce_sum(
                p.sb["mrow"][0:1, 0:1], A["pA"][0:1, 0:16], mybir.AxisListType.X),
                waits=[mm1, mm2])
            mrd2 = p.op("vector", lambda e: e.reduce_sum(
                p.sb["mrow"][0:1, 1:2], A["pB"][0:1, 0:16], mybir.AxisListType.X))
            mscl = p.op("vector", lambda e: e.tensor_scalar_mul(
                p.sb["mrow"][0:1, 0:2], p.sb["mrow"][0:1, 0:2], 1.0 / H),
                waits=[mrd1, mrd2])
            msqm = p.op("vector", lambda e: e.tensor_tensor(
                p.sb["mrow"][0:1, 2:3], p.sb["mrow"][0:1, 0:1],
                p.sb["mrow"][0:1, 0:1], ALU.mult), waits=[mscl])
            mv = p.op("vector", lambda e: e.tensor_tensor(
                p.sb["mrow"][0:1, 1:2], p.sb["mrow"][0:1, 1:2],
                p.sb["mrow"][0:1, 2:3], ALU.subtract), waits=[msqm])
            ml = p.op("scalar", lambda e: e.activation(
                p.sb["mrow"][0:1, 3:4], p.sb["mrow"][0:1, 1:2], AF.Ln,
                bias=p.sb["epscol"][0:1, 0:1]), waits=[mv])
            mr = p.op("scalar", lambda e: e.activation(
                p.sb["mrow"][0:1, 4:5], p.sb["mrow"][0:1, 3:4], AF.Exp,
                scale=-0.5), waits=[ml])
            mbc = p.op("tensor", lambda e: e.matmul(
                A["pA"][:, 0:1], p.sb["onesr"][0:1, :], p.sb["mrow"][0:1, 0:1],
                start=True, stop=True), waits=[mr])
            mbc2 = p.op("tensor", lambda e: e.matmul(
                A["pB"][:, 0:1], p.sb["onesr"][0:1, :], p.sb["mrow"][0:1, 4:5],
                start=True, stop=True))
            mxs1 = p.op("vector", lambda e: e.tensor_scalar(
                p.sb["xnprev"][:, :], p.sb["xnprev"][:, :], A["pA"][:, 0:1],
                None, ALU.subtract), waits=[mbc, mbc2])
            mxs2 = p.op("vector", lambda e: e.tensor_scalar(
                p.sb["xnprev"][:, :], p.sb["xnprev"][:, :], A["pB"][:, 0:1],
                None, ALU.mult), waits=[mxs1])
            mxnp = p.op("vector", lambda e: e.tensor_scalar(
                p.sb["xnprev"][:, :], p.sb["xnprev"][:, :],
                p.sb["cmask"][:, 16:17], None, ALU.mult), waits=[mxs2])

            pm = {"lnfree": mbc2}
            for st in range(NST):
                t0 = SW * st
                sm = {}
                bnd = []
                for kt in range(NKT):
                    if st == 0:
                        bnd.append(p.op("vector", lambda e, kt=kt: e.tensor_copy(
                            p.sb["xn2"][:, 513 * kt:513 * kt + 1],
                            p.sb["xnprev"][:, kt:kt + 1]), waits=[mxnp]))
                    else:
                        bnd.append(p.op("vector", lambda e, kt=kt: e.tensor_copy(
                            p.sb["xn2"][:, 513 * kt:513 * kt + 1],
                            p.sb["xn2"][:, 513 * kt + 512:513 * kt + 513]),
                            waits=[pm.get(("xi_done", st - 1))]))

                def out_fn(kt, slot, bnd=bnd):
                    return p.op("vector", lambda e, kt=kt, slot=slot: e.tensor_tensor(
                        p.sb["xn2"][:, 513 * kt + 1:513 * kt + 513],
                        p.sb[f"xsq{slot}"][:, 0:512],
                        A["pB"][:, 0:512], ALU.mult), waits=[bnd[kt]])

                xnm = build_ln_strip(p, p.dr["x1s"], t0, SW, out_fn,
                                     A["pA"], A["pB"],
                                     psum_free=pm.get("lnfree"),
                                     xtile_free=pm.get("xtfree"))
                pm["xtfree"] = xnm["last"]
                for kt in range(NKT):
                    sm[("delta", kt)] = p.op("vector", lambda e, kt=kt: e.tensor_tensor(
                        p.sb["delta"][:, 512 * kt:512 * kt + 512],
                        p.sb["xn2"][:, 513 * kt:513 * kt + 512],
                        p.sb["xn2"][:, 513 * kt + 1:513 * kt + 513], ALU.subtract),
                        waits=[xnm[kt]])
                x0m = []
                for kt in range(NKT):
                    x0m.append(p.op("vector", lambda e, kt=kt: e.scalar_tensor_tensor(
                        p.sb["xi"][:, 512 * kt:512 * kt + 512],
                        p.sb["delta"][:, 512 * kt:512 * kt + 512],
                        p.sb["mux8"][:, kt:kt + 1],
                        p.sb["xn2"][:, 513 * kt + 1:513 * kt + 513],
                        ALU.mult, ALU.add),
                        waits=[sm[("delta", kt)],
                               pm.get(("xiPE", st - 1)) if kt == 0 else None]))

                if cfg.get("attn_stop") == "ln2":
                    continue
                wev, wcons = {}, {}

                def wload(ev, emit):
                    wev[ev] = p.op("sync", emit, dma=True, sem=f"ldw{ev % 2}",
                                   waits=[wcons.get(ev - 2,
                                                    pm.get(f"watfin{ev % 2}")),
                                          p.conv_fence])

                def emit_q(e, name, qh, slot):
                    return e.dma_start(
                        out=p.sb[f"whb{slot}"][:, :].rearrange("p (a b) -> p a b", b=256),
                        in_=p.dr[name].rearrange("p (a h b) -> p a h b", h=4, b=256)
                            [:, :, qh, :])

                wload(0, lambda e: e.dma_start(
                    out=p.sb["whb0"][:, 0:16 * 192].rearrange("p (a b) -> p a b", b=192),
                    in_=p.dr["wx1b"].rearrange("p (a h b) -> p a h b", h=1, b=320)
                        [:, :, 0, 0:192]))
                wload(1, lambda e: e.dma_start(
                    out=p.sb["whb1"][:, 0:16 * 128].rearrange("p (a b) -> p a b", b=128),
                    in_=p.dr["wx1b"].rearrange("p (a h b) -> p a h b", h=1, b=320)
                        [:, :, 0, 192:320]))
                def wload_next(ev):
                    if ev > 15:
                        return
                    if 2 <= ev <= 5:
                        wload(ev, lambda e, qh=ev - 2: emit_q(e, "wrb", qh, ev % 2))
                    elif ev == 6:
                        wload(6, lambda e: e.dma_start(
                            out=p.sb["whb0"][:, 0:16 * 64], in_=p.dr["wwab"][:, :]))
                    elif 7 <= ev <= 10:
                        wload(ev, lambda e, qh=ev - 7: emit_q(e, "wkb", qh, ev % 2))
                    elif 11 <= ev <= 14:
                        wload(ev, lambda e, qh=ev - 11: emit_q(e, "wvb", qh, ev % 2))
                    else:
                        wload(15, lambda e: e.dma_start(
                            out=p.sb["whb1"][:, 0:16 * 64], in_=p.dr["wgab"][:, :]))

                prev_ev = pm.get("pCfree")
                lr_marks = {}
                for i in range(5):
                    ev = 0 if i < 3 else 1
                    w0 = 192 if i < 3 else 128
                    od0 = 64 * i if i < 3 else 64 * (i - 3)
                    slotw = ev % 2
                    mm = None
                    for kt in range(NKT):
                        mm = p.op("tensor", lambda e, od0=od0, kt=kt, slotw=slotw, w0=w0: e.matmul(
                            A["pC"][0:64, 0:512],
                            p.sb[f"whb{slotw}"][:, w0 * kt + od0:w0 * kt + od0 + 64],
                            p.sb["xi"][:, 512 * kt:512 * kt + 512],
                            start=(kt == 0), stop=(kt == NKT - 1)),
                            waits=[wev[ev], x0m[-1], prev_ev] if kt == 0 else ())
                    prev_ev = p.op("scalar", lambda e, i=i: e.activation(
                        p.sb[f"lr{i}"][:, 0:512], A["pC"][0:64, 0:512], AF.Tanh),
                        waits=[mm])
                    lr_marks[i] = prev_ev
                lr_last = prev_ev
                wcons[0] = lr_marks[2]
                wcons[1] = lr_marks[4]
                wload_next(2)
                wload_next(3)

                if cfg.get("attn_stop") == "lr":
                    continue
                for i in range(cfg.get("attn_imax", 5)):
                    wq = [sm.get(("musdone", i - 1), pm.get("musfin")),
                          p.conv_fence]
                    wx2ld = p.op("sync", lambda e, i=i: e.dma_start(
                        out=p.sb["wx2t0"][:, :],
                        in_=p.dr["wx2bs"][:, 2048 * i:2048 * i + 2048]), dma=True,
                        sem="ldq0", waits=wq)
                    xim = []
                    for kt in range(NKT):
                        ps = A["pD"] if kt % 2 == 0 else A["pE"]
                        wmus = [wx2ld, lr_marks[i], sm.get(("musfree", i, kt - 2))]
                        if i == 0 and kt < 2:
                            for x in pm.get(("pDEfree", st), []):
                                wmus.append(x)
                        if i == 4 and kt < 2:
                            wmus.append(sm[("vproj", 3)])
                        if i == 2 and kt < 2:
                            wmus.append(sm[("proj", 1)])
                        mm = p.op("tensor", lambda e, kt=kt, i=i, ps=ps: e.matmul(
                            ps[:, 0:512],
                            p.sb["wx2t0"][:, 128 * kt:128 * kt + 128],
                            p.sb[f"lr{i}"][:, 0:512],
                            start=True, stop=True), waits=wmus)
                        m1 = p.op("vector", lambda e, kt=kt, i=i, ps=ps: e.tensor_scalar(
                            p.sb["mtile0"][:, 0:512], ps[:, 0:512],
                            p.sb["xb8"][:, 16 * i + kt:16 * i + kt + 1], None,
                            ALU.add), waits=[mm])
                        sm[("musfree", i, kt)] = m1
                        p.op("vector", lambda e, kt=kt: e.tensor_tensor(
                            p.sb["mtile0"][:, 0:512], p.sb["mtile0"][:, 0:512],
                            p.sb["delta"][:, 512 * kt:512 * kt + 512], ALU.mult))
                        xim.append(p.op("vector", lambda e, kt=kt: e.tensor_tensor(
                            p.sb["xi"][:, 512 * kt:512 * kt + 512],
                            p.sb["mtile0"][:, 0:512],
                            p.sb["xn2"][:, 513 * kt + 1:513 * kt + 513], ALU.add)))
                    sm[("musdone", i)] = xim[-1]
                    if i == 4:
                        pm[("xi_done", st)] = xim[-1]

                    if i in (0, 2):
                        dst = "rb" if i == 0 else "kb"
                        ev0 = 2 if i == 0 else 7
                        evs = []
                        for j in range(8):
                            qh = j // 2
                            ev = ev0 + qh
                            slotw = ev % 2
                            ps = A["pA"] if j % 2 == 0 else A["pB"]
                            wj = [wev[ev], xim[-1]]
                            if len(evs) >= 2:
                                wj.append(evs[-2])
                            elif i == 0:
                                wj.append(pm.get(("pABfree", st)))
                            mm = None
                            for kt in range(NKT):
                                mm = p.op("tensor", lambda e, j=j, kt=kt, slotw=slotw, ps=ps: e.matmul(
                                    ps[:, 0:512],
                                    p.sb[f"whb{slotw}"][:, 256 * kt + 128 * (j % 2):256 * kt + 128 * (j % 2) + 128],
                                    p.sb["xi"][:, 512 * kt:512 * kt + 512],
                                    start=(kt == 0), stop=(kt == NKT - 1)),
                                    waits=wj if kt == 0 else ())
                            evs.append(p.op("scalar", lambda e, j=j, dst=dst, ps=ps: e.activation(
                                p.sb[dst][:, 512 * j:512 * j + 512], ps[:, 0:512],
                                AF.Identity), waits=[mm]))
                            if j % 2 == 1:
                                wcons[ev] = evs[-1]
                                wload_next(ev + 2)
                        sm[("proj", i)] = evs[-1]
                    elif i in (1, 4):
                        bcol = "bw8" if i == 1 else "bg8"
                        ev = 6 if i == 1 else 15
                        slotw = ev % 2
                        mtb = p.op("sync", lambda e, i=i: e.dma_start(
                            out=p.sb["twb"][:, :],
                            in_=p.dr["wwbs" if i == 1 else "wgbs"][:, :]),
                            dma=True, sem="ldtb",
                            waits=[sm.get(("lowfin", i - 3), pm.get("twbfree")),
                                   p.conv_fence])
                        mm = None
                        for kt in range(NKT):
                            mm = p.op("tensor", lambda e, kt=kt, slotw=slotw: e.matmul(
                                A["pC"][0:64, 0:512],
                                p.sb[f"whb{slotw}"][:, 64 * kt:64 * kt + 64],
                                p.sb["xi"][:, 512 * kt:512 * kt + 512],
                                start=(kt == 0), stop=(kt == NKT - 1)),
                                waits=[wev[ev], xim[-1],
                                       lr_last if i == 1 else sm[("lowev", 1)]]
                                if kt == 0 else ())
                        mtw = p.op("scalar", lambda e: e.activation(
                            p.sb["twa"][0:64, 0:512], A["pC"][0:64, 0:512], AF.Tanh),
                            waits=[mm])
                        sm[("lowev", i)] = mtw
                        wcons[ev] = mtw
                        wload_next(ev + 2)
                        evs = []
                        for j in range(8):
                            ps = A["pD"] if j % 2 == 0 else A["pE"]
                            wj = [mtw, mtb]
                            if len(evs) >= 2:
                                wj.append(evs[-2])
                            else:
                                wj.append(sm[("musfree", i, 14 + j)])
                            mm = p.op("tensor", lambda e, j=j, ps=ps: e.matmul(
                                ps[:, 0:512], p.sb["twb"][:, 128 * j:128 * j + 128],
                                p.sb["twa"][0:64, 0:512], start=True, stop=True),
                                waits=wj)
                            if j == 7:
                                sm[("lowfin", i)] = mm
                            if i == 1:
                                evs.append(p.op("scalar", lambda e, j=j, ps=ps, bcol=bcol: e.activation(
                                    p.sb["Pb"][:, 512 * j:512 * j + 512], ps[:, 0:512],
                                    AF.Exp, bias=p.sb[bcol][:, j:j + 1]), waits=[mm]))
                            else:
                                mge = p.op("scalar", lambda e, j=j, ps=ps, bcol=bcol: e.activation(
                                    p.sb["gq0"][:, 0:512], ps[:, 0:512],
                                    AF.Identity, bias=p.sb[bcol][:, j:j + 1]),
                                    waits=[mm, sm.get(("gstash", j - 1),
                                                      pm.get(("gstash", st - 1)))])
                                evs.append(p.op("scalar", lambda e, j=j, t0=t0: e.dma_start(
                                    out=p.dr["gstash"][128 * j:128 * j + 128, t0:t0 + 512],
                                    in_=p.sb["gq0"][:, 0:512]), dma=True,
                                    sem="stg0", waits=[mge]))
                                sm[("gstash", j)] = evs[-1]
                        sm[("proj", i)] = evs[-1]
                        if i == 4:
                            pm[("gstash", st)] = evs[-1]
                    else:
                        for qh in range(4):
                            ev = 11 + qh
                            slotw = ev % 2
                            evs = []
                            for c in range(8):
                                psv = A["pC"] if c % 2 == 0 else A["pD"]
                                wv0 = [wev[ev], xim[-1]]
                                if len(evs) >= 2:
                                    wv0.append(evs[-2])
                                elif qh == 0:
                                    wv0 += [sm[("lowev", 1)], sm[("proj", 1)]]
                                mm0 = None
                                for kt in range(NKT):
                                    mm0 = p.op("tensor", lambda e, c=c, kt=kt, slotw=slotw, psv=psv: e.matmul(
                                        psv[0:64, 0:256],
                                        p.sb["xi"][:, 512 * kt + 64 * c:512 * kt + 64 * c + 64],
                                        p.sb[f"whb{slotw}"][:, 256 * kt:256 * kt + 256],
                                        start=(kt == 0), stop=(kt == NKT - 1)),
                                        waits=wv0 if kt == 0 else ())
                                evs.append(p.op("scalar", lambda e, c=c, qh=qh, psv=psv: e.activation(
                                    p.sb["v64"][:, 1024 * c + 256 * qh:1024 * c + 256 * qh + 256],
                                    psv[0:64, 0:256], AF.Identity), waits=[mm0]))
                            wcons[ev] = evs[-1]
                            wload_next(ev + 2)
                        sm[("vproj", 3)] = wcons[14]
                        sm[("proj", 3)] = wcons[14]

                    if i == 2 and cfg.get("attn_dk", 9) >= 1:
                        dm = []
                        for j in range(8):
                            dm.append(p.op("vector", lambda e, j=j: e.tensor_tensor_scan(
                                p.sb["CG"][:, 512 * j:512 * j + 512],
                                p.sb["Pb"][:, 512 * j:512 * j + 512],
                                p.sb["zcol"][:, 0:1].broadcast_to([128, 512]), 0.0,
                                ALU.add, ALU.add), waits=[sm[("proj", 1)]]))
                        if cfg.get("attn_dk", 9) < 2:
                            sm["decay"] = dm[-1]
                            continue
                        for j in range(8):
                            p.op("vector", lambda e, j=j: e.tensor_copy(
                                p.sb["mcols"][:, 8 * j:8 * j + 8],
                                p.sb["CG"][:, 512 * j + 31:512 * j + 512:64]),
                                waits=[dm[j]])
                        for j in range(8):
                            p.op("vector", lambda e, j=j: e.tensor_copy(
                                p.sb["ecols"][:, 8 * j:8 * j + 8],
                                p.sb["CG"][:, 512 * j + 63:512 * j + 512:64]))
                        for j in range(8):
                            p.op("vector", lambda e, j=j: e.memset(
                                p.sb["scols"][:, 8 * j:8 * j + 1], 0.0))
                        mcx = None
                        for j in range(8):
                            mcx = p.op("vector", lambda e, j=j: e.tensor_copy(
                                p.sb["scols"][:, 8 * j + 1:8 * j + 8],
                                p.sb["ecols"][:, 8 * j:8 * j + 7]))
                        mav = p.op("vector", lambda e: e.tensor_tensor(
                            p.sb["acols"][:, :], p.sb["scols"][:, :],
                            p.sb["ecols"][:, :], ALU.subtract), waits=[mcx])
                        macol = p.op("scalar", lambda e: e.activation(
                            p.sb["acols"][:, :], p.sb["acols"][:, :], AF.Exp),
                            waits=[mav])
                        mfv = p.op("vector", lambda e: e.tensor_tensor(
                            p.sb["fcols"][:, :], p.sb["scols"][:, :],
                            p.sb["mcols"][:, :], ALU.subtract), waits=[macol])
                        mfcol = p.op("scalar", lambda e: e.activation(
                            p.sb["fcols"][:, :], p.sb["fcols"][:, :], AF.Exp),
                            waits=[mfv])
                        mkv = p.op("vector", lambda e: e.tensor_tensor(
                            p.sb["kcols"][:, :], p.sb["mcols"][:, :],
                            p.sb["ecols"][:, :], ALU.subtract), waits=[mfcol])
                        mkcol = p.op("scalar", lambda e: e.activation(
                            p.sb["kcols"][:, :], p.sb["kcols"][:, :], AF.Exp),
                            waits=[mkv])
                        mcc = None
                        for j in range(8):
                            mcc = p.op("vector", lambda e, j=j: e.tensor_scalar(
                                p.sb["ccols"][:, 8 * j:8 * j + 8],
                                p.sb["mcols"][:, 8 * j:8 * j + 8],
                                p.sb["ltot"][:, j:j + 1], None, ALU.add),
                                waits=[mkcol])
                        mccol = p.op("scalar", lambda e: e.activation(
                            p.sb["ccols"][:, :], p.sb["ccols"][:, :], AF.Exp,
                            scale=-1.0), waits=[mcc])
                        mcst = p.op("vector", lambda e, st=st: e.tensor_copy(
                            p.sb["colrelstash"][:, 64 * st:64 * st + 64],
                            p.sb["ccols"][:, :]), waits=[mccol])
                        if st == 0 and cfg.get("dbg_point") == "cols":
                            for (cn, src, wm) in [
                                    ("acols", "acols", macol), ("fcols", "fcols", mfcol),
                                    ("kcols", "kcols", mkcol), ("mcols", "mcols", mkcol),
                                    ("ecols", "ecols", mkcol), ("scols", "scols", mkcol),
                                    ("ccols", "ccols", mccol)]:
                                off = {"acols": 0, "fcols": 64, "kcols": 128,
                                       "mcols": 192, "ecols": 256, "scols": 320,
                                       "ccols": 384}[cn]
                                p.op("gpsimd", lambda e, src=src, off=off: e.dma_start(
                                    out=p.dr["dbgf"][:, off:off + 64],
                                    in_=p.sb[src][:, :]), dma=True, sem="stdb",
                                    waits=[wm])
                        mlt = None
                        for j in range(8):
                            mlt = p.op("vector", lambda e, j=j: e.tensor_tensor(
                                p.sb["ltot"][:, j:j + 1],
                                p.sb["ltot"][:, j:j + 1],
                                p.sb["CG"][:, 512 * j + 511:512 * j + 512], ALU.add),
                                waits=[mcst])
                        sm["ltotupd"] = mlt
                        if cfg.get("attn_dk", 9) < 3:
                            sm["decay"] = mlt
                            continue
                        mby = p.op("gpsimd", lambda e: e.dma_start(
                            out=p.sb["colsB"][:, 0:64],
                            in_=p.sb["acols"][64:128, :]), dma=True, sem="stag",
                            waits=[macol])
                        mby = p.op("gpsimd", lambda e: e.dma_start(
                            out=p.sb["colsB"][:, 64:128],
                            in_=p.sb["fcols"][64:128, :]), dma=True, sem="stag",
                            waits=[mfcol])
                        mby2 = p.op("gpsimd", lambda e, st=st: e.dma_start(
                            out=p.sb["colrelstashB"][:, 64 * st:64 * st + 64],
                            in_=p.sb["ccols"][64:128, :]), dma=True, sem="stag",
                            waits=[mcst])
                        sm["colsB"] = mby
                        sm["colsBrel"] = mby2
                        if cfg.get("attn_dk", 9) < 4:
                            sm["decay"] = mby2
                            continue
                        etm = {}
                        for j in range(8):
                            e0 = p.op("vector", lambda e, j=j: e.tensor_tensor(
                                p.sb["etmp0"][:, 0:512].rearrange("p (a b) -> p a b", b=64),
                                p.sb["mcols"][:, 8 * j:8 * j + 8]
                                    .rearrange("p (a o) -> p a o", o=1)
                                    .broadcast_to([128, 8, 64]),
                                p.sb["CG"][:, 512 * j:512 * j + 512]
                                    .rearrange("p (a b) -> p a b", b=64),
                                ALU.subtract), waits=[mkcol, etm.get(j - 1)])
                            e1 = p.op("vector", lambda e, j=j: e.tensor_tensor(
                                p.sb["etmp1"][:, 0:512], p.sb["etmp0"][:, 0:512],
                                p.sb["Pb"][:, 512 * j:512 * j + 512], ALU.add))
                            e1x = p.op("scalar", lambda e: e.activation(
                                p.sb["etmp1"][:, 0:512], p.sb["etmp1"][:, 0:512],
                                AF.Exp), waits=[e1, mccol])
                            mrt = p.op("vector", lambda e, j=j: e.tensor_tensor(
                                p.sb["rt"][:, 512 * j:512 * j + 512],
                                p.sb["rb"][:, 512 * j:512 * j + 512],
                                p.sb["etmp1"][:, 0:512], ALU.mult),
                                waits=[e1x, sm[("proj", 0)],
                                       pm.get(("rtstash", st - 1)) if j == 0 else None])
                            e0x = p.op("scalar", lambda e: e.activation(
                                p.sb["etmp0"][:, 0:512], p.sb["etmp0"][:, 0:512],
                                AF.Exp, scale=-1.0), waits=[mrt])
                            mkt = p.op("vector", lambda e, j=j: e.tensor_tensor(
                                p.sb["ktl"][:, 512 * j:512 * j + 512],
                                p.sb["kb"][:, 512 * j:512 * j + 512],
                                p.sb["etmp0"][:, 0:512], ALU.mult),
                                waits=[e0x, sm[("proj", 2)]])
                            etm[j] = mkt
                        sm["decay"] = etm[7]
                        strt = []
                        for j in range(8):
                            strt.append(p.op("scalar", lambda e, j=j, t0=t0: e.dma_start(
                                out=p.dr["rrelstash"][128 * j:128 * j + 128, t0:t0 + 512],
                                in_=p.sb["rt"][:, 512 * j:512 * j + 512]),
                                dma=True, sem="strt", waits=[etm[j]]))
                        pm[("rtstash", st)] = strt[-1]

                if cfg.get("attn_stop") == "proj" or cfg.get("attn_imax", 5) < 5:
                    break
                pm["twbfree"] = sm[("lowfin", 4)]
                pm["watfin0"] = wcons[14]
                pm["watfin1"] = wcons[15]
                pm["musfin"] = sm[("musdone", 4)]
                pm[("xiPE", st)] = sm[("lowev", 4)]

                if cfg.get("attn_stop") == "proj":
                    continue
                oev, upd = {}, {}
                tr_ev = pm.get("trv")
                swm = {}
                for c in range(8):
                    slot = c % 2
                    wsw = [sm["decay"], upd.get((c - 2, 7))]
                    for ti, tn in enumerate(["rt", "ktl", "kb", "rb"]):
                        swm[c] = p.op("gpsimd", lambda e, ti=ti, tn=tn, c=c, slot=slot: e.dma_start(
                            out=p.sb[f"swp{slot}"][:, 512 * ti:512 * ti + 512]
                                .rearrange("p (a b) -> p a b", b=64),
                            in_=p.sb[tn][64:128, 64 * c:4096:512]
                                .rearrange("p (a b) -> p a b", b=64)
                            if False else
                            p.sb[tn].ap()[64:128, :]
                                .rearrange("p (a b) -> p a b", b=512)[:, :, 64 * c:64 * c + 64]),
                            dma=True, sem=f"swp{slot}", waits=wsw if ti == 0 else ())
                    sbA, sbB = {}, {}
                    for q in range(8):
                        sbA[q] = p.op("scalar", lambda e, q=q, c=c: e.activation(
                            p.sb["slocbA"][:, 64 * q:64 * q + 64],
                            p.sb[f"slocA{q}"][:, :], AF.Identity,
                            scale=p.sb["fcols"][0:64, 8 * q + c:8 * q + c + 1]),
                            waits=[upd.get((c - 1, q)), sm["decay"]])
                    for q in range(8):
                        sbB[q] = p.op("scalar", lambda e, q=q, c=c: e.activation(
                            p.sb["slocbB"][:, 64 * q:64 * q + 64],
                            p.sb[f"slocB{q}"][:, :], AF.Identity,
                            scale=p.sb["colsB"][:, 64 + 8 * q + c:64 + 8 * q + c + 1]),
                            waits=[upd.get((c - 1, q)), sm["colsB"]])
                    ktm = {}
                    for q in range(8):
                        ktm[q] = p.op("scalar", lambda e, q=q, c=c: e.activation(
                            p.sb["ktmp"][:, 64 * q:64 * q + 64],
                            p.sb["ktl"][:, 512 * q + 64 * c:512 * q + 64 * c + 64],
                            AF.Identity,
                            scale=p.sb["kcols"][:, 8 * q + c:8 * q + c + 1]),
                            waits=[tr_ev])
                    bkA, bkB = {}, {}
                    for q in range(8):
                        bkA[q] = p.op("scalar", lambda e, q=q, c=c: e.activation(
                            p.sb["bktmpA"][:, 64 * q:64 * q + 64],
                            p.sb["kb"][0:64, 512 * q + 64 * c:512 * q + 64 * c + 64],
                            AF.Identity, scale=p.sb["bonus8"][0:64, q:q + 1]),
                            waits=[sm.get(("scanMb", c - 1, q), sm[("proj", 2)])])
                    for q in range(8):
                        bkB[q] = p.op("scalar", lambda e, q=q, c=c, slot=slot: e.activation(
                            p.sb["bktmpB"][:, 64 * q:64 * q + 64],
                            p.sb[f"swp{slot}"][:, 1024 + 64 * q:1024 + 64 * q + 64],
                            AF.Identity, scale=p.sb["bonus8B"][:, q:q + 1]),
                            waits=[swm[c], sm.get(("scanMb", c - 1, q), sm[("proj", 2)])])
                    mtr = None
                    for q in range(8):
                        mtr = p.op("tensor", lambda e, q=q: e.transpose(
                            A["pT"][0:64, 128 * q:128 * q + 128],
                            p.sb["ktmp"][:, 64 * q:64 * q + 64],
                            p.sb["identb"][:, :]),
                            waits=[ktm[q]] + ([tr_ev] if q == 0 else []))
                    tr_ev = p.op("scalar", lambda e: e.copy(
                        p.sb["khT"][:, :], A["pT"][0:64, :]), waits=[mtr])
                    for q in range(8):
                        c0 = 512 * q + 64 * c
                        sw0 = 64 * q
                        psM = A["pA"] if q % 2 == 0 else A["pB"]
                        psB2 = A["pE"] if q % 2 == 0 else A["pF"]
                        psO = A["pG"] if q % 2 == 0 else A["pC"]
                        wsc = [sm.get(("scanM", c, q - 2)),
                               sm.get(("scanM", c - 1, q + 6)),
                               sm["decay"], swm[c]]
                        if st == 0 and c == 0 and q < 2:
                            wsc.append(sm[("proj", 2)])
                        p.op("tensor", lambda e, c0=c0, psM=psM: e.matmul(
                            psM[0:64, 0:64], p.sb["ktl"][0:64, c0:c0 + 64],
                            p.sb["rt"][0:64, c0:c0 + 64], start=True, stop=True),
                            waits=wsc)
                        mS = p.op("tensor", lambda e, sw0=sw0, slot=slot, psM=psM: e.matmul(
                            psM[0:64, 64:128],
                            p.sb[f"swp{slot}"][:, 512 + sw0:512 + sw0 + 64],
                            p.sb[f"swp{slot}"][:, 0 + sw0:0 + sw0 + 64],
                            start=True, stop=True))
                        wmb = [bkA[q], bkB[q], sm.get(("scanMb", c, q - 2)),
                               sm.get(("scanMb", c - 1, q + 6))]
                        if c == 0 and q < 2:
                            wmb.append(sm[("proj", 4)])
                        p.op("tensor", lambda e, c0=c0, q=q, psB2=psB2: e.matmul(
                            psB2[0:64, 0:64], p.sb["bktmpA"][:, 64 * q:64 * q + 64],
                            p.sb["rb"][0:64, c0:c0 + 64], start=True, stop=True),
                            waits=wmb)
                        mB = p.op("tensor", lambda e, sw0=sw0, q=q, slot=slot, psB2=psB2: e.matmul(
                            psB2[0:64, 64:128], p.sb["bktmpB"][:, 64 * q:64 * q + 64],
                            p.sb[f"swp{slot}"][:, 1536 + sw0:1536 + sw0 + 64],
                            start=True, stop=True))
                        e1 = p.op("vector", lambda e, psM=psM: e.tensor_tensor(
                            p.sb["etmp0"][0:64, 0:128], psM[0:64, 0:128],
                            p.sb["mask2b"][:, :], ALU.mult), waits=[mS])
                        sm[("scanM", c, q)] = e1
                        e2 = p.op("vector", lambda e, psB2=psB2: e.tensor_tensor(
                            p.sb["etmp1"][0:64, 0:128], psB2[0:64, 0:128],
                            p.sb["eye2b"][:, :], ALU.mult), waits=[mB])
                        sm[("scanMb", c, q)] = e2
                        e3 = p.op("vector", lambda e, q=q: e.tensor_tensor(
                            p.sb["msb"][:, 128 * q:128 * q + 128],
                            p.sb["etmp0"][0:64, 0:128], p.sb["etmp1"][0:64, 0:128],
                            ALU.add))
                        vA = 1024 * c + 128 * q
                        wo1 = [e3, sbA[q], sbB[q], sm[("proj", 3)],
                               sm.get(("scanOe", c, q - 2)),
                               sm.get(("scanOe", c - 1, q + 6))]
                        p.op("tensor", lambda e, vA=vA, q=q, psO=psO: e.matmul(
                            psO[0:64, 0:64], p.sb["v64"][:, vA:vA + 64],
                            p.sb["msb"][0:64, 128 * q:128 * q + 64],
                            start=True, stop=False), waits=wo1)
                        p.op("tensor", lambda e, c0=c0, q=q, psO=psO: e.matmul(
                            psO[0:64, 0:64], p.sb["slocbA"][:, 64 * q:64 * q + 64],
                            p.sb["rt"][0:64, c0:c0 + 64], start=False, stop=True))
                        p.op("tensor", lambda e, vA=vA, q=q, psO=psO: e.matmul(
                            psO[0:64, 64:128], p.sb["v64"][:, vA + 64:vA + 128],
                            p.sb["msb"][0:64, 128 * q + 64:128 * q + 128],
                            start=True, stop=False))
                        mo = p.op("tensor", lambda e, sw0=sw0, q=q, slot=slot, psO=psO: e.matmul(
                            psO[0:64, 64:128], p.sb["slocbB"][:, 64 * q:64 * q + 64],
                            p.sb[f"swp{slot}"][:, 0 + sw0:0 + sw0 + 64],
                            start=False, stop=True))
                        oev[(c, q)] = p.op("scalar", lambda e, c=c, q=q, psO=psO: e.copy(
                            p.sb[f"ov{(8 * c + q) % 2}"][:, :],
                            psO[0:64, 0:128]), waits=[mo, sm.get(("ost", 8 * c + q - 2))])
                        sm[("scanOe", c, q)] = oev[(c, q)]
                        p.op("scalar", lambda e, c=c, q=q, t0=t0: e.dma_start(
                            out=p.dr["ostash"][128 * q:128 * q + 64,
                                               t0 + 64 * c:t0 + 64 * c + 64],
                            in_=p.sb[f"ov{(8 * c + q) % 2}"][:, 0:64]),
                            dma=True, sem=f"sto{(8 * c + q) % 2}", waits=[oev[(c, q)]])
                        sm[("ost", 8 * c + q)] = p.op("scalar",
                            lambda e, c=c, q=q, t0=t0: e.dma_start(
                                out=p.dr["ostash"][128 * q + 64:128 * q + 128,
                                                   t0 + 64 * c:t0 + 64 * c + 64],
                                in_=p.sb[f"ov{(8 * c + q) % 2}"][:, 64:128]),
                            dma=True, sem=f"sto{(8 * c + q) % 2}")
                        wu = [tr_ev, upd.get((c - 1, q)), sm.get(("scanU", c, q - 1))]
                        if c == 0 and q == 0:
                            wu += [sm[("proj", 4)], sm[("musfree", 4, 15)],
                                   sm[("musfree", 4, 14)]]
                        p.op("tensor", lambda e, vA=vA, q=q: e.matmul(
                            A["pD"][0:64, 0:64], p.sb["khT"][0:64, 128 * q:128 * q + 64],
                            p.sb["v64"][:, vA:vA + 64], start=True, stop=True),
                            waits=wu)
                        mU = p.op("tensor", lambda e, vA=vA, q=q: e.matmul(
                            A["pD"][0:64, 64:128],
                            p.sb["khT"][0:64, 128 * q + 64:128 * q + 128],
                            p.sb["v64"][:, vA + 64:vA + 128], start=True, stop=True))
                        mua = p.op("vector", lambda e, c=c, q=q: e.scalar_tensor_tensor(
                            p.sb[f"slocA{q}"][:, :], p.sb[f"slocA{q}"][:, :],
                            p.sb["acols"][0:64, 8 * q + c:8 * q + c + 1],
                            A["pD"][0:64, 0:64], ALU.mult, ALU.add), waits=[mU])
                        upd[(c, q)] = p.op("vector", lambda e, c=c, q=q: e.scalar_tensor_tensor(
                            p.sb[f"slocB{q}"][:, :], p.sb[f"slocB{q}"][:, :],
                            p.sb["colsB"][:, 8 * q + c:8 * q + c + 1],
                            A["pD"][0:64, 64:128], ALU.mult, ALU.add), waits=[mua])
                        sm[("scanU", c, q)] = upd[(c, q)]
                pm["trv"] = tr_ev
                pm["lnfree"] = sm[("scanM", 7, 7)]
                pm["pCfree"] = oev[(7, 7)]
                pm[("pABfree", st + 1)] = sm[("scanM", 7, 7)]
                pm[("pDEfree", st + 1)] = [upd[(7, 7)], sm[("scanMb", 7, 7)]]
                pm[("ostash", st)] = sm[("ost", 63)]
                p.flush(f"attns{st}")

            if cfg.get("attn_stop") in ("proj", "scan", "ln2", "lr"):
                return
            stfence = p.fence("scalar", ["sto0", "sto1", "stg0", "strt"])
            ags = []
            for q in range(8):
                ags.append(p.op("gpsimd", lambda e, q=q: e.dma_start(
                    out=p.dr["ag2_in"][128 * q:128 * q + 64, :],
                    in_=p.sb[f"slocA{q}"][:, :]), dma=True, sem="stag",
                    waits=[upd[(7, q)]]))
                ags.append(p.op("gpsimd", lambda e, q=q: e.dma_start(
                    out=p.dr["ag2_in"][128 * q + 64:128 * q + 128, :],
                    in_=p.sb[f"slocB{q}"][:, :]), dma=True, sem="stag"))
            matot = p.op("scalar", lambda e: e.activation(
                p.sb["acols"][:, 0:8], p.sb["ltot"][:, :], AF.Exp, scale=-1.0),
                waits=[upd[(7, 7)], sm["ltotupd"], stfence])
            for q in range(8):
                ags.append(p.op("gpsimd", lambda e, q=q: e.dma_start(
                    out=p.dr["ag2_in"][1024 + 2 * q:1024 + 2 * q + 2, :],
                    in_=p.sb["acols"][:, q:q + 1]), dma=True, sem="stag",
                    waits=[matot]))
            gfence = p.fence("gpsimd", ["stag"])
            mag2 = p.op("gpsimd", lambda e: e.collective_compute(
                "AllGather", ALU.bypass, replica_groups=[list(range(NCORES))],
                ins=[p.dr["ag2_in"][:, :].opt()], outs=[p.dr["ag2_out"][:, :].opt()]),
                collective=True, waits=[gfence, stfence])
            macc = {}
            for q in range(8):
                macc[q] = p.op("vector", lambda e, q=q: e.memset(
                    p.sb[f"sloc{q}"][:, :], 0.0), waits=[ags[-1], mag2])
            mcand = {}
            for gr in range(2):
                for pos in range(1, 4):
                    r = 4 * gr + pos - 1
                    for q in range(8):
                        p.op("scalar", lambda e, r=r, q=q: e.dma_start(
                            out=p.sb["xi"].ap().bitcast(F32)[:, 64 * q:64 * q + 64],
                            in_=p.dr["ag2_out"][1040 * r + 128 * q:1040 * r + 128 * q + 128, :]),
                            dma=True, sem=f"ldx{q % 2}",
                            waits=[mag2, mcand.get((gr, pos - 1, q))])
                        p.op("scalar", lambda e, r=r, q=q: e.dma_start(
                            out=p.sb["mcols"][:, 8 * q:8 * q + 1],
                            in_=p.dr["ag2_out"][1040 * r + 1024 + 2 * q:1040 * r + 1024 + 2 * q + 2, :]),
                            dma=True, sem=f"ldx{q % 2}")
                    relfence = p.fence("scalar", ["ldx0", "ldx1"])
                    for q in range(8):
                        srcap = p.sb["xi"].ap().bitcast(F32)[:, 64 * q:64 * q + 64]
                        if pos == 1:
                            mcand[(gr, pos, q)] = p.op("vector", lambda e, q=q, srcap=srcap: e.tensor_copy(
                                p.sb["CG"][:, 64 * q:64 * q + 64],
                                srcap), waits=[relfence, macc[q]])
                        else:
                            mcand[(gr, pos, q)] = p.op("vector", lambda e, q=q, srcap=srcap: e.scalar_tensor_tensor(
                                p.sb["CG"][:, 64 * q:64 * q + 64],
                                p.sb["CG"][:, 64 * q:64 * q + 64],
                                p.sb["mcols"][:, 8 * q:8 * q + 1],
                                srcap, ALU.mult, ALU.add), waits=[relfence])
                    for q in range(8):
                        mcand[(gr, pos, q)] = p.op("vector", lambda e, q=q, gr=gr, pos=pos: e.scalar_tensor_tensor(
                            p.sb[f"sloc{q}"][:, :], p.sb["CG"][:, 64 * q:64 * q + 64],
                            p.sb["cmask"][:, 4 * gr + pos:4 * gr + pos + 1],
                            p.sb[f"sloc{q}"][:, :], ALU.mult, ALU.add))
            sincp = None
            sinld = {}
            for q in range(8):
                p.op("vector", lambda e, q=q: e.tensor_copy(
                    p.sb[f"sinA{q}"][:, :], p.sb[f"sloc{q}"][0:64, :]))
                mdb = p.op("gpsimd", lambda e, q=q: e.dma_start(
                    out=p.sb["CG"][:, 512 + 64 * q:512 + 64 * q + 64]
                    if False else p.sb["CG"][0:64, 64 * q:64 * q + 64],
                    in_=p.sb[f"sloc{q}"][64:128, :]), dma=True, sem="stag",
                    waits=[mcand[(1, 3, q)]])
                sincp = p.op("vector", lambda e, q=q: e.tensor_copy(
                    p.sb[f"sinB{q}"][:, :], p.sb["CG"][0:64, 64 * q:64 * q + 64]),
                    waits=[mdb])
            p.sin_ready = sincp

        if cfg.get("attn_stop") == "relay":
            return
        p.barrier()
        p.flush("attnmain")
        with contextlib.ExitStack() as s2:
            p.sbuf("wosA", [64, 16 * 1024], BF16, stack=s2)
            p.sbuf("wosB", [64, 16 * 1024], BF16, stack=s2)
            p.sbuf("oaccA", [64, 8 * 512], F32, stack=s2)
            p.sbuf("oaccB", [64, 8 * 512], F32, stack=s2)
            p.sbuf("obA", [64, 8 * 512], BF16, stack=s2)
            p.sbuf("obB", [64, 8 * 512], BF16, stack=s2)
            p.sbuf("gbA", [64, 8 * 512], BF16, stack=s2)
            p.sbuf("gbB", [64, 8 * 512], BF16, stack=s2)
            p.sbuf("rtbA", [64, 8 * 512], BF16, stack=s2)
            p.sbuf("rtbB", [64, 8 * 512], BF16, stack=s2)
            p.sbuf("gatedA", [64, 8 * 512], BF16, stack=s2)
            p.sbuf("gatedB", [64, 8 * 512], BF16, stack=s2)
            for s in range(2):
                p.sbuf(f"xt2{s}", [128, 512], F32, stack=s2)
                p.sbuf(f"tmpa{s}", [64, 512], F32, stack=s2)
                p.sbuf(f"tmpb{s}", [64, 512], F32, stack=s2)
                p.sbuf(f"rrl{s}", [64, 512], BF16, stack=s2)
            p.sbuf("gstat", [1, 512], F32, stack=s2)
            p.sbuf("gstat2", [1, 512], F32, stack=s2)
            A = p.aps
            mwo = p.op("sync", lambda e: e.dma_start(
                out=p.sb["wosA"][:, :], in_=p.dr["wob"][0:64, :]), dma=True,
                sem="ldw0")
            mwo = p.op("sync", lambda e: e.dma_start(
                out=p.sb["wosB"][:, :], in_=p.dr["wob"][64:128, :]), dma=True,
                sem="ldw0")
            if cfg.get("dbg_point") == "wos":
                p.op("gpsimd", lambda e: e.dma_start(
                    out=p.dr["dbgb"][0:64, 0:2048],
                    in_=p.sb["wosA"][:, 0:2048]), dma=True, sem="stdb",
                    waits=[mwo])
                p.op("gpsimd", lambda e: e.dma_start(
                    out=p.dr["dbgb"][64:128, 0:2048],
                    in_=p.sb["wosB"][:, 0:2048]), dma=True, sem="stdb")
            s2m = {}
            for st in range(NST):
                t0 = SW * st
                for q in range(8):
                    p.op("scalar", lambda e, q=q, t0=t0: e.dma_start(
                        out=p.sb["obA"][:, 512 * q:512 * q + 512],
                        in_=p.dr["ostash"][128 * q:128 * q + 64, t0:t0 + 512]),
                        dma=True, sem="ldo", waits=[s2m.get(("orel", st - 1, q))])
                    p.op("scalar", lambda e, q=q, t0=t0: e.dma_start(
                        out=p.sb["obB"][:, 512 * q:512 * q + 512],
                        in_=p.dr["ostash"][128 * q + 64:128 * q + 128, t0:t0 + 512]),
                        dma=True, sem="ldo")
                    p.op("scalar", lambda e, q=q, t0=t0: e.dma_start(
                        out=p.sb["rtbA"][:, 512 * q:512 * q + 512],
                        in_=p.dr["rrelstash"][128 * q:128 * q + 64, t0:t0 + 512]),
                        dma=True, sem="ldo", waits=[s2m.get(("rrl", st - 1, q))])
                    p.op("scalar", lambda e, q=q, t0=t0: e.dma_start(
                        out=p.sb["rtbB"][:, 512 * q:512 * q + 512],
                        in_=p.dr["rrelstash"][128 * q + 64:128 * q + 128, t0:t0 + 512]),
                        dma=True, sem="ldo")
                    p.op("scalar", lambda e, q=q, t0=t0: e.dma_start(
                        out=p.sb["gbA"][:, 512 * q:512 * q + 512],
                        in_=p.dr["gstash"][128 * q:128 * q + 64, t0:t0 + 512]),
                        dma=True, sem="ldo", waits=[s2m.get(("gated", st - 1, q))])
                    p.op("scalar", lambda e, q=q, t0=t0: e.dma_start(
                        out=p.sb["gbB"][:, 512 * q:512 * q + 512],
                        in_=p.dr["gstash"][128 * q + 64:128 * q + 128, t0:t0 + 512]),
                        dma=True, sem="ldo")
                s2fence = p.fence("scalar", ["ldo"])
                for q in range(8):
                    for half in range(2):
                        hx = "A" if half == 0 else "B"
                        rtb = p.sb[f"rtb{hx}"]
                        crel = (p.sb["colrelstash"][0:64, 64 * st + 8 * q:64 * st + 8 * q + 8]
                                if half == 0 else
                                p.sb["colrelstashB"][:, 64 * st + 8 * q:64 * st + 8 * q + 8])
                        mrr = p.op("vector", lambda e, q=q, st=st, rtb=rtb, crel=crel, half=half: e.tensor_tensor(
                            p.sb[f"rrl{half}"][:, 0:512].rearrange("p (a b) -> p a b", b=64),
                            rtb[:, 512 * q:512 * q + 512]
                                .rearrange("p (a b) -> p a b", b=64),
                            crel.rearrange("p (a o) -> p a o", o=1)
                                .broadcast_to([64, 8, 64]),
                            ALU.mult), waits=[s2fence,
                                              s2m.get(("orelmm", st, 2 * q + half - 4)),
                                              p.sin_ready])
                        s2m[("rrl", st, q)] = mrr
                        ps = A["pA"] if half == 0 else A["pB"]
                        mo2 = p.op("tensor", lambda e, q=q, half=half, ps=ps: e.matmul(
                            ps[0:64, 0:512], p.sb[f"sin{hx}{q}" if False else ("sinA" if half == 0 else "sinB") + str(q)][:, :],
                            p.sb[f"rrl{half}"][:, 0:512], start=True, stop=True),
                            waits=[mrr, s2m.get(("orel", st, 2 * q + half - 4)),
                                   s2m.get(("orel", st - 1, 2 * q + half + 12)),
                                   s2m.get(("wofin", st - 1))])
                        s2m[("orelmm", st, 2 * q + half)] = mo2
                        oacc = p.sb[f"oacc{hx}"]
                        ob = p.sb[f"ob{hx}"]
                        mo3 = p.op("vector", lambda e, q=q, ps=ps, oacc=oacc, ob=ob: e.tensor_tensor(
                            oacc[:, 512 * q:512 * q + 512],
                            ob[:, 512 * q:512 * q + 512], ps[0:64, 0:512],
                            ALU.add), waits=[mo2, s2fence])
                        s2m[("orel", st, 2 * q + half)] = mo3
                        msq2 = p.op("scalar", lambda e, q=q, half=half, oacc=oacc: e.activation(
                            p.sb[f"tmpa{half}"][:, 0:512],
                            oacc[:, 512 * q:512 * q + 512], AF.Square),
                            waits=[mo3])
                        mms = p.op("tensor", lambda e, q=q, oacc=oacc: e.matmul(
                            A["pC"][0:1, 0:512], p.sb["ones1"][0:64, 0:1],
                            oacc[:, 512 * q:512 * q + 512], start=True, stop=True),
                            waits=[mo3, s2m.get(("gnst", st, 2 * q + half - 1))])
                        mms2 = p.op("tensor", lambda e, half=half: e.matmul(
                            A["pD"][0:1, 0:512], p.sb["ones1"][0:64, 0:1],
                            p.sb[f"tmpa{half}"][:, 0:512], start=True, stop=True),
                            waits=[msq2])
                        mst1 = p.op("vector", lambda e: e.tensor_scalar_mul(
                            p.sb["gstat"][0:1, 0:512], A["pC"][0:1, 0:512], 1.0 / 64),
                            waits=[mms, mms2])
                        p.op("vector", lambda e: e.tensor_scalar_mul(
                            p.sb["gstat2"][0:1, 0:512], A["pD"][0:1, 0:512], 1.0 / 64))
                        p.op("vector", lambda e: e.tensor_tensor(
                            p.sb["mrow"][0:1, 0:512], p.sb["gstat"][0:1, 0:512],
                            p.sb["gstat"][0:1, 0:512], ALU.mult))
                        mvar2 = p.op("vector", lambda e: e.tensor_tensor(
                            p.sb["gstat2"][0:1, 0:512], p.sb["gstat2"][0:1, 0:512],
                            p.sb["mrow"][0:1, 0:512], ALU.subtract))
                        s2m[("gnst", st, 2 * q + half)] = mvar2
                        mln2 = p.op("scalar", lambda e: e.activation(
                            p.sb["mrow"][0:1, 0:512], p.sb["gstat2"][0:1, 0:512],
                            AF.Ln, bias=p.sb["epscol"][0:1, 0:1]), waits=[mvar2])
                        mrs2 = p.op("scalar", lambda e: e.activation(
                            p.sb["gstat2"][0:1, 0:512], p.sb["mrow"][0:1, 0:512],
                            AF.Exp, scale=-0.5), waits=[mln2])
                        mbm = p.op("tensor", lambda e: e.matmul(
                            A["pE"][0:64, 0:512], p.sb["onesr"][0:1, 0:64],
                            p.sb["gstat"][0:1, 0:512], start=True, stop=True),
                            waits=[mrs2, s2m.get(("gnn", st, 2 * q + half - 1))])
                        mbr = p.op("tensor", lambda e: e.matmul(
                            A["pF"][0:64, 0:512], p.sb["onesr"][0:1, 0:64],
                            p.sb["gstat2"][0:1, 0:512], start=True, stop=True))
                        lncol = (p.sb["lnw8"][0:64, q:q + 1] if half == 0
                                 else p.sb["lnw8B"][:, q:q + 1])
                        lbcol = (p.sb["lnb8"][0:64, q:q + 1] if half == 0
                                 else p.sb["lnb8B"][:, q:q + 1])
                        mn1 = p.op("vector", lambda e, q=q, half=half, oacc=oacc: e.tensor_tensor(
                            p.sb[f"tmpa{half}"][:, 0:512],
                            oacc[:, 512 * q:512 * q + 512], A["pE"][0:64, 0:512],
                            ALU.subtract), waits=[mbm, mbr])
                        mn2 = p.op("vector", lambda e, half=half: e.tensor_tensor(
                            p.sb[f"tmpa{half}"][:, 0:512], p.sb[f"tmpa{half}"][:, 0:512],
                            A["pF"][0:64, 0:512], ALU.mult))
                        s2m[("gnn", st, 2 * q + half)] = mn2
                        mn3 = p.op("vector", lambda e, half=half, lncol=lncol, lbcol=lbcol: e.tensor_scalar(
                            p.sb[f"tmpa{half}"][:, 0:512], p.sb[f"tmpa{half}"][:, 0:512],
                            lncol, lbcol, ALU.mult, ALU.add))
                        gb = p.sb[f"gb{hx}"]
                        msg = p.op("scalar", lambda e, q=q, half=half, gb=gb: e.activation(
                            p.sb[f"tmpb{half}"][:, 0:512],
                            gb[:, 512 * q:512 * q + 512], AF.Sigmoid),
                            waits=[s2fence, mn2])
                        mgg = p.op("vector", lambda e, q=q, half=half, gb=gb: e.tensor_tensor(
                            p.sb[f"tmpb{half}"][:, 0:512], p.sb[f"tmpb{half}"][:, 0:512],
                            gb[:, 512 * q:512 * q + 512], ALU.mult), waits=[msg])
                        gated = p.sb[f"gated{hx}"]
                        mgt = p.op("vector", lambda e, q=q, half=half, gated=gated: e.tensor_tensor(
                            gated[:, 512 * q:512 * q + 512],
                            p.sb[f"tmpa{half}"][:, 0:512], p.sb[f"tmpb{half}"][:, 0:512],
                            ALU.mult), waits=[mn3])
                        s2m[("gated", st, 2 * q + half)] = mgt
                if st == 0 and cfg.get("dbg_point") == "gated2":
                    p.op("gpsimd", lambda e: e.dma_start(
                        out=p.dr["dbgb"][0:64, 0:2048],
                        in_=p.sb["gatedA"][:, 2048:4096]), dma=True, sem="stdb",
                        waits=[s2m[("gated", st, 15)]])
                    p.op("gpsimd", lambda e: e.dma_start(
                        out=p.dr["dbgb"][64:128, 0:2048],
                        in_=p.sb["gatedB"][:, 2048:4096]), dma=True, sem="stdb")
                if st == 0 and cfg.get("dbg_point") == "gated":
                    p.op("gpsimd", lambda e: e.dma_start(
                        out=p.dr["dbgb"][0:64, 0:2048],
                        in_=p.sb["gatedA"][:, 0:2048]), dma=True, sem="stdb",
                        waits=[s2m[("gated", st, 15)]])
                    p.op("gpsimd", lambda e: e.dma_start(
                        out=p.dr["dbgb"][64:128, 0:2048],
                        in_=p.sb["gatedB"][:, 0:2048]), dma=True, sem="stdb")
                if st == 0 and cfg.get("dbg_point") == "oacc":
                    p.op("gpsimd", lambda e: e.dma_start(
                        out=p.dr["dbgf"][0:64, 0:1024],
                        in_=p.sb["oaccA"][:, 0:1024]), dma=True, sem="stdb",
                        waits=[s2m[("gated", st, 15)]])
                    p.op("gpsimd", lambda e: e.dma_start(
                        out=p.dr["dbgf"][64:128, 0:1024],
                        in_=p.sb["oaccB"][:, 0:1024]), dma=True, sem="stdb")
                fin, stm = {}, {}
                for ht in range(NHT):
                    ps = A["pG"] if ht % 2 == 0 else A["pB"]
                    mld = p.op("scalar", lambda e, ht=ht, t0=t0: e.dma_start(
                        out=p.sb[f"xt2{ht % 2}"][:, 0:512],
                        in_=p.dr["x1s"][128 * ht:128 * ht + 128, t0:t0 + 512]),
                        dma=True, sem=f"ldx{ht % 2}",
                        waits=[stm.get(ht - 2), s2m.get(("xst", st - 1, ht + 14))])
                    mm = None
                    for q in range(8):
                        mm = p.op("tensor", lambda e, ht=ht, q=q, ps=ps: e.matmul(
                            ps[:, 0:512],
                            p.sb["wosA"][:, 2048 * q + 128 * ht:2048 * q + 128 * ht + 128],
                            p.sb["gatedA"][:, 512 * q:512 * q + 512],
                            start=(q == 0), stop=False),
                            waits=[mwo, s2m[("gated", st, 15)],
                                   fin.get(ht - 2)] if q == 0 else ())
                        mm = p.op("tensor", lambda e, ht=ht, q=q, ps=ps: e.matmul(
                            ps[:, 0:512],
                            p.sb["wosB"][:, 2048 * q + 128 * ht:2048 * q + 128 * ht + 128],
                            p.sb["gatedB"][:, 512 * q:512 * q + 512],
                            start=False, stop=(q == 7)))
                    mf = p.op("vector", lambda e, ht=ht, ps=ps: e.tensor_tensor(
                        p.sb[f"xt2{ht % 2}"][:, 0:512], p.sb[f"xt2{ht % 2}"][:, 0:512],
                        ps[:, 0:512], ALU.add), waits=[mm, mld, stm.get(ht - 2)])
                    fin[ht] = mf
                    stm[ht] = p.op("gpsimd", lambda e, ht=ht, t0=t0: e.dma_start(
                        out=p.dr["x2s"][128 * ht:128 * ht + 128, t0:t0 + 512],
                        in_=p.sb[f"xt2{ht % 2}"][:, 0:512]), dma=True,
                        sem=f"stx{ht % 2}", waits=[mf])
                    s2m[("xst", st, ht)] = stm[ht]
                s2m[("wofin", st)] = fin[NHT - 1]
    p.x2_done = p.fence("gpsimd", ["stx0", "stx1"])


PERM_H = np.concatenate([np.arange(0, H, 2), np.arange(1, H, 2)])
PERM_N = np.concatenate([np.arange(0, MLP, 2), np.arange(1, MLP, 2)])


def prep_inputs(inp):
    A = {k[8:]: np.asarray(v) for k, v in inp.items() if k.startswith("L0_attn_")}
    Fn = {k[7:]: np.asarray(v) for k, v in inp.items() if k.startswith("L0_ffn_")}
    x = A["x"].reshape(2 * 8192, H).astype(np.float32)[:, PERM_H]

    gpkT = np.ascontiguousarray(Fn["gate_W_packed"].T[:, PERM_N]).astype(np.int32)
    upkT = np.ascontiguousarray(Fn["up_W_packed"].T[:, PERM_N]).astype(np.int32)
    dpkT = np.ascontiguousarray(Fn["down_W_packed"].T[:, PERM_H]).astype(np.int32)
    gscT = np.ascontiguousarray(Fn["gate_scales"].T[:, PERM_N]).astype(np.float32)
    uscT = np.ascontiguousarray(Fn["up_scales"].T[:, PERM_N]).astype(np.float32)
    dscT = np.ascontiguousarray(Fn["down_scales"].T[:, PERM_H]).astype(np.float32)

    com = {}
    com["wx1"] = np.ascontiguousarray(A["Wx1"][PERM_H, :]).astype(np.float32)
    wx2p = np.transpose(A["Wx2"], (2, 1, 0))[:, :, PERM_H]
    com["wx2"] = np.ascontiguousarray(wx2p.reshape(64, 5 * H)).astype(np.float32)
    xb = A["x_bias"][:, PERM_H].reshape(5, 16, 128)
    com["xb8in"] = np.ascontiguousarray(np.transpose(xb, (2, 0, 1)).reshape(128, 80)).astype(np.float32)
    com["mux8in"] = np.ascontiguousarray(A["mu_x"][PERM_H].reshape(16, 128).T).astype(np.float32)
    com["wr"] = np.ascontiguousarray(A["Wr"][PERM_H, :]).astype(np.float32)
    com["wk"] = np.ascontiguousarray(A["Wk"][PERM_H, :]).astype(np.float32)
    com["wv"] = np.ascontiguousarray(A["Wv"][PERM_H, :]).astype(np.float32)
    com["wwa"] = np.ascontiguousarray(A["Ww_a"][PERM_H, :]).astype(np.float32)
    com["wga"] = np.ascontiguousarray(A["Wg_a"][PERM_H, :]).astype(np.float32)
    com["wwb"] = np.ascontiguousarray(A["Ww_b"]).astype(np.float32)
    com["wgb"] = np.ascontiguousarray(A["Wg_b"]).astype(np.float32)
    com["bw8in"] = np.ascontiguousarray(A["bw"].reshape(8, 128).T).astype(np.float32)
    com["bg8in"] = np.ascontiguousarray(A["bg"].reshape(8, 128).T).astype(np.float32)
    com["bonus8in"] = np.ascontiguousarray(A["bonus"].reshape(-1).reshape(8, 128).T).astype(np.float32)
    com["lnw8in"] = np.ascontiguousarray(np.tile(A["ln_w"], 16).reshape(8, 128).T).astype(np.float32)
    com["lnb8in"] = np.ascontiguousarray(np.tile(A["ln_b"], 16).reshape(8, 128).T).astype(np.float32)
    com["wo"] = np.ascontiguousarray(A["Wo"][:, PERM_H]).astype(np.float32)
    com["ident"] = np.eye(128).astype(ml_dtypes.bfloat16)
    smask = (np.arange(64)[:, None] < np.arange(64)[None, :]).astype(np.float32)
    com["mask2"] = np.concatenate([smask, smask], 1).astype(ml_dtypes.bfloat16)
    eye = np.eye(64, dtype=np.float32)
    com["eye2"] = np.concatenate([eye, eye], 1).astype(ml_dtypes.bfloat16)
    com["bonus8Bin"] = np.ascontiguousarray(com["bonus8in"][64:128])
    com["lnw8Bin"] = np.ascontiguousarray(com["lnw8in"][64:128])
    com["lnb8Bin"] = np.ascontiguousarray(com["lnb8in"][64:128])
    bones = np.zeros((128, 2), np.float32)
    bones[0:64, 0] = 1.0; bones[64:128, 1] = 1.0
    com["bonesin"] = bones
    com["bonesTin"] = np.ascontiguousarray(bones.T)

    in_maps = []
    for c in range(NCORES):
        d = dict(com)
        r0 = c * 2048
        d["xin"] = np.ascontiguousarray(x[r0:r0 + 2048].T)
        d["gpk_my"] = np.ascontiguousarray(gpkT[:, 2048 * c:2048 * c + 2048])
        d["upk_my"] = np.ascontiguousarray(upkT[:, 2048 * c:2048 * c + 2048])
        d["dpk_my"] = np.ascontiguousarray(dpkT[1024 * c:1024 * c + 1024, :])
        d["gsc_my"] = np.ascontiguousarray(gscT[:, 2048 * c:2048 * c + 2048])
        d["usc_my"] = np.ascontiguousarray(uscT[:, 2048 * c:2048 * c + 2048])
        d["dsc_my"] = np.ascontiguousarray(dscT[64 * c:64 * c + 64, :])
        cm = np.zeros((128, 32), np.float32)
        gpos, gr = c % 4, c // 4
        cm[:, gr * 4 + gpos] = 1.0
        if gpos > 0:
            cm[:, 8 + (c - 1)] = 1.0
            cm[:, 16] = 1.0
        d["cmaskin"] = cm
        in_maps.append(d)
    return in_maps


def assemble_out(results, key="out"):
    full = np.empty((16384, H), np.float32)
    for c in range(NCORES):
        full[2048 * c:2048 * c + 2048] = results[c][key].T
    orig = np.empty_like(full)
    orig[:, PERM_H] = full
    return orig.reshape(2, 8192, H)



def kernel(**inputs):
    from concourse.bass_utils import run_bass_kernel_spmd
    in_maps = prep_inputs(inputs)
    nc, p = build_program({"upto": "full"})
    res = run_bass_kernel_spmd(nc, in_maps, core_ids=list(range(NCORES)))
    return assemble_out(res.results)
